# version 03
# baseline (speedup 1.0000x reference)
import contextlib
import numpy as np
import ml_dtypes
import concourse.bass as bass
import concourse.mybir as mybir
from concourse.bass_utils import run_bass_kernel_spmd

F32 = mybir.dt.float32
BF16 = mybir.dt.bfloat16
I32 = mybir.dt.int32
AF = mybir.ActivationFunctionType
OP = mybir.AluOpType
AX = mybir.AxisListType

P = 128
T = 2048
DM = 1024
NF = DM // P
N_LAYERS = 4
DEBUG_DUMP = False
DEBUG_L1 = False
DEBUG_S = False
DEBUG_L3 = False

DI = 2048
NDI = DI // P
DST = 16
DTR = 64
TC0 = 256
NTC0 = T // TC0
FI = 2816
NFI = FI // P

MASK_HI = int(np.int32(np.uint32(0xFFC00000).view(np.int32)))
ADD_LO = (1 << 21) - 1

ENGS = ("sync", "gpsimd", "vector", "scalar", "tensor")


class Prog:
    def __init__(self):
        self.items = {e: [] for e in ENGS}
        self.tick = {e: 0 for e in ENGS}
        self.dmacnt = {e: 0 for e in ENGS}
        self.hw = {e: {} for e in ENGS}

    def _wait(self, eng, tok):
        kind, src, val = tok
        key = (kind, src)
        if self.hw[eng].get(key, 0) >= val:
            return
        self.hw[eng][key] = val
        self.items[eng].append(("w", key, val))

    def run(self, eng, fn, deps=()):
        for d in deps:
            if d is not None:
                self._wait(eng, d)
        self.items[eng].append(("x", fn))
        self.tick[eng] += 1
        return ("e", eng, self.tick[eng])

    def dma(self, q, out, in_, deps=()):
        for d in deps:
            if d is not None:
                self._wait(q, d)
        self.items[q].append(("dma", out, in_))
        self.dmacnt[q] += 16
        return ("d", q, self.dmacnt[q])

    def final_wait(self, eng, tok):
        kind, src, val = tok
        self.items[eng].append(("w", (kind, src), val))

    def replay(self, nc, esems, dsems):
        with nc.Block() as block:
            for ename in ENGS:
                items = self.items[ename]
                esem = esems[ename]
                dsem = dsems.get(ename)

                def body(eng, items=items, esem=esem, dsem=dsem):
                    for it in items:
                        if it[0] == "w":
                            _, key, val = it
                            sem = esems[key[1]] if key[0] == "e" else dsems[key[1]]
                            eng.wait_ge(sem, val)
                        elif it[0] == "x":
                            inst = it[1](eng)
                            inst.then_inc(esem, 1)
                        else:
                            _, out, in_ = it
                            eng.dma_start(out=out, in_=in_).then_inc(dsem, 16)

                getattr(block, ename)(body)


class BufT:

    def __init__(self):
        self.w = None
        self.u = {}

    def rd(self):
        return [self.w] if self.w is not None else []

    def wr(self):
        return list(self.u.values())

    def wrote(self, tok):
        self.w = tok
        self.u = {}
        self._note(tok)

    def piece(self, tok):
        self.w = tok
        self._note(tok)

    def read(self, tok):
        self._note(tok)

    def _note(self, tok):
        key = (tok[0], tok[1])
        old = self.u.get(key)
        if old is None or tok[2] > old[2]:
            self.u[key] = tok


def joinb(*bufs):
    out = []
    for b in bufs:
        out += b.wr()
    return out


def build_program(n_layers=N_LAYERS):
    nc = bass.Bass("TRN2", target_bir_lowering=False)
    Pr = Prog()
    st = contextlib.ExitStack()

    def din(name, shape, dtype=F32):
        return nc.dram_tensor(name, list(shape), dtype, kind="ExternalInput").ap()

    def dout(name, shape, dtype=F32):
        return nc.dram_tensor(name, list(shape), dtype, kind="ExternalOutput").ap()

    def dtmp(name, shape, dtype=F32):
        return nc.dram_tensor(name, list(shape), dtype).ap()

    def sbuf(name, shape, dtype=F32):
        return st.enter_context(nc.sbuf_tensor(name, list(shape), dtype))

    x_in = din("x", [DM, T])
    y_out = dout("y", [DM, T])
    dbg_s = dout("dbg_s", [DM, T], BF16) if DEBUG_S else None
    dbg_xs = dout("dbg_xs", [DI, T], BF16) if DEBUG_L1 else None
    dbg_yz = dout("dbg_yz", [DI, T], BF16) if DEBUG_L1 else None
    dbg_dt = dout("dbg_dt", [DI, T]) if DEBUG_L1 else None
    dbg_bc = dout("dbg_bc", [32, T], BF16) if DEBUG_L1 else None
    dbg_g3 = dout("dbg_g3", [8, DM]) if DEBUG_L3 else None
    dbg_cnt = dout("dbg_cnt", [1, 8]) if DEBUG_L3 else None
    dbg_fpk = dout("dbg_fpk", [32, T], BF16) if DEBUG_L3 else None
    dbg_e3 = dout("dbg_e3", [16, T], BF16) if DEBUG_L3 else None
    dbg_u13 = dout("dbg_u13", [P, 512], BF16) if DEBUG_L3 else None
    dbg_pkg = dout("dbg_pkg", [8, DM]) if DEBUG_L3 else None
    dbg_xgc = dout("dbg_xgc", [P, 64], BF16) if DEBUG_L3 else None
    dbg_xg8 = dout("dbg_xg8", [P, 64]) if DEBUG_L3 else None
    dbg_ab8 = dout("dbg_ab8", [P, 16]) if DEBUG_L3 else None
    dbg_xq = dout("dbg_xq", [DM, T], BF16) if DEBUG_DUMP else None
    dbg_hq = dout("dbg_hq", [NFI * P, T], BF16) if DEBUG_DUMP else None
    dbg_wqg = dout("dbg_wqg", [DM, FI], BF16) if DEBUG_DUMP else None
    dbg_hs = dout("dbg_hs", [T, FI], BF16) if DEBUG_DUMP else None
    idf32 = din("c_idf32", [P, P])
    idbf = din("c_idbf", [P, P], BF16)
    ones = din("c_ones", [P, 1])

    wi_T = din("l0_wi_T", [DM, 2 * DI])
    wo_T = din("l0_wo_T", [DI, DM])
    wx_T = din("l0_wx_T", [P, NDI * 96])
    wdt_T = din("l0_wdt_T", [DTR, DI])
    convw = din("l0_convw", [P, NDI * 4])
    convb = din("l0_convb", [P, NDI])
    bdt = din("l0_bdt", [P, NDI])
    alog = din("l0_alog", [P, NDI * DST])
    dvec = din("l0_dvec", [P, NDI])
    wg_f = din("l0_wg", [FI, DM])
    wu_f = din("l0_wu", [FI, DM])
    wd_f = din("l0_wd", [DM, FI])

    l1_qkv_T = din("l1_qkv_T", [DM, 3 * DM])
    l1_oproj_T = din("l1_oproj_T", [DM, DM])
    l1_ogate_T = din("l1_ogate_T", [DM, DM])
    l1_ogate_b = din("l1_ogate_b", [P, NF])
    l1_decay = din("l1_decay", [1, 8])
    l1_convw = din("l1_convw", [P, NF * 4])
    l1_convb = din("l1_convb", [P, NF])
    l1_pw_T = din("l1_pw_T", [DM, DM])
    l1_gate_T = din("l1_gate_T", [2 * DM, DM])
    l1_w1_T = din("l1_w1_T", [DM, FI])
    l1_w2_T = din("l1_w2_T", [DM, FI])
    l1_w3_T = din("l1_w3_T", [FI, DM])
    l1_n1 = din("l1_n1", [P, NF])
    l1_n2 = din("l1_n2", [P, NF])
    l1_wr_T = din("l1_wr_T", [DM, DM])
    l1_wk_T = din("l1_wk_T", [DM, DM])
    l1_wv_T = din("l1_wv_T", [DM, DM])
    l1_mur = din("l1_mur", [P, NF])
    l1_muk = din("l1_muk", [P, NF])
    l2_qks = din("l2_qks", [DM, 1024])
    l2_vs = din("l2_vs", [DM, 512])
    l2_qkl = din("l2_qkl", [DM, 2048])
    l2_vl = din("l2_vl", [DM, 512])
    l2_proj = din("l2_proj", [DM, DM])
    l2_rout_T = din("l2_rout_T", [DM, 8])
    l2_rout_b = din("l2_rout_b", [8, 1])
    l2_wexp_T = din("l2_wexp_T", [8 * DM, DM])
    l2_bexp_T = din("l2_bexp_T", [8, DM])
    l2_mask_s = din("l2_mask_s", [P, 384])
    l2_mask_l = din("l2_mask_l", [P, 1152])

    l3_qkv_T = din("l3_qkv_T", [DM, 5120])
    l3_proj_T = din("l3_proj_T", [DM, DM])
    l3_bproj = din("l3_bproj", [P, 8])
    l3_cc = din("l3_cc", [P, T])
    l3_ssn = din("l3_ssn", [P, T])
    l3_mask = din("l3_mask", [P, 640])
    l3_gate_T = din("l3_gate_T", [DM, 8])
    l3_ebias = din("l3_ebias", [1, 8])
    l3_sw1_T = din("l3_sw1_T", [DM, 1024])
    l3_sw3_T = din("l3_sw3_T", [DM, 1024])
    l3_sw2_T = din("l3_sw2_T", [DM, DM])
    l3_w1 = din("l3_w1", [8 * DM, 512])
    l3_w3 = din("l3_w3", [8 * DM, 512])
    l3_w2 = din("l3_w2", [8 * 512, DM])
    l3_iota = din("l3_iota", [1, T])
    l3_c0 = din("l3_c0", [P, 1])

    wi_bf = dtmp("wi_bf", [DM, 2 * DI], BF16)
    wo_bf = dtmp("wo_bf", [DI, DM], BF16)
    ab_dr = dtmp("ab_dr", [2, T])
    bc_dr = dtmp("bc_dr", [32, T], BF16)
    row_dr = dtmp("row_dr", [4, T])
    dec_dr = dtmp("dec_dr", [1, 8])
    l1qkv_bf = dtmp("l1qkv_bf", [DM, 3 * DM], BF16)
    l1pw_bf = dtmp("l1pw_bf", [DM, DM], BF16)
    l1og_bf = dtmp("l1og_bf", [DM, DM], BF16)
    l1op_bf = dtmp("l1op_bf", [DM, DM], BF16)
    l1gt_bf = dtmp("l1gt_bf", [2 * DM, DM], BF16)
    l1w1_bf = dtmp("l1w1_bf", [DM, FI], BF16)
    l1w2_bf = dtmp("l1w2_bf", [DM, FI], BF16)
    l1w3_bf = dtmp("l1w3_bf", [FI, DM], BF16)
    l1wr_bf = dtmp("l1wr_bf", [DM, DM], BF16)
    l1wk_bf = dtmp("l1wk_bf", [DM, DM], BF16)
    l1wv_bf = dtmp("l1wv_bf", [DM, DM], BF16)
    l2qks_bf = dtmp("l2qks_bf", [DM, 1024], BF16)
    l2vs_bf = dtmp("l2vs_bf", [DM, 512], BF16)
    l2qkl_bf = dtmp("l2qkl_bf", [DM, 2048], BF16)
    l2vl_bf = dtmp("l2vl_bf", [DM, 512], BF16)
    l2pj_bf = dtmp("l2pj_bf", [DM, DM], BF16)
    l2we_bf = dtmp("l2we_bf", [8 * DM, DM], BF16)
    selm_dr = dtmp("selm_dr", [8, T], BF16)
    wqg_dr = dtmp("wqg_dr", [DM, FI], BF16)
    wqu_dr = dtmp("wqu_dr", [DM, FI], BF16)
    wqd_dr = dtmp("wqd_dr", [FI, DM], BF16)
    l3qkv_bf = dtmp("l3qkv_bf", [DM, 5120], BF16)
    l3pj_bf = dtmp("l3pj_bf", [DM, DM], BF16)
    l3sw1_bf = dtmp("l3sw1_bf", [DM, 1024], BF16)
    l3sw3_bf = dtmp("l3sw3_bf", [DM, 1024], BF16)
    l3sw2_bf = dtmp("l3sw2_bf", [DM, DM], BF16)
    l3w1_bf = dtmp("l3w1_bf", [8 * DM, 512], BF16)
    l3w3_bf = dtmp("l3w3_bf", [8 * DM, 512], BF16)
    l3w2_bf = dtmp("l3w2_bf", [8 * 512, DM], BF16)
    pk3_dr = dtmp("pk3_dr", [9, DM])
    pko3_dr = dtmp("pko3_dr", [9, DM])
    g3_dr = dtmp("g3_dr", [8, DM])
    go3_dr = dtmp("go3_dr", [8, DM])
    fe3_dr = dtmp("fe3_dr", [16, T], BF16)
    ff3_dr = dtmp("ff3_dr", [32, T], BF16)
    u13_dr = dtmp("u13_dr", [P, 512], BF16)

    X = sbuf("X", [P, NF * T])
    S = sbuf("S", [P, NF * T], BF16)
    IDF = sbuf("IDF", [P, P])
    IDB = sbuf("IDB", [P, P], BF16)
    ONE = sbuf("ONE", [P, 1])
    AB = sbuf("AB", [P, 2 * 512])
    CW = sbuf("CW", [P, NDI * 4])
    CBs = sbuf("CBs", [P, NDI])
    BDTs = sbuf("BDTs", [P, NDI])
    AMAT = sbuf("AMAT", [P, NDI * DST])
    DVs = sbuf("DVs", [P, NDI])
    WXs = sbuf("WXs", [P, NDI * 96], BF16)
    WDTs = sbuf("WDTs", [DTR, DI], BF16)
    STATE = sbuf("STATE", [P, NDI * DST])
    TAILS = sbuf("TAILS", [P, NDI * 3])
    L1CW = sbuf("L1CW", [P, NF * 4])
    L1CB = sbuf("L1CB", [P, NF])
    L1N1 = sbuf("L1N1", [P, NF])
    L1N2 = sbuf("L1N2", [P, NF])
    L1OGB = sbuf("L1OGB", [P, NF])
    DECB = sbuf("DECB", [P, 8])
    KVST = sbuf("KVST", [P, 8])
    KSST = sbuf("KSST", [P, 8])
    L1TL = sbuf("L1TL", [P, NF * 3])
    MUR = sbuf("MUR", [P, NF])
    MUK = sbuf("MUK", [P, NF])
    MUR1 = sbuf("MUR1", [P, NF])
    MUK1 = sbuf("MUK1", [P, NF])
    XPRV = sbuf("XPRV", [P, NF])
    MSK_S = sbuf("MSK_S", [P, 384])
    MSK_L = sbuf("MSK_L", [P, 1152])
    RBIAS = sbuf("RBIAS", [8, 1])
    ONES8 = sbuf("ONES8", [P, 8])
    BP3 = sbuf("BP3", [P, 8])
    C0M = sbuf("C0M", [P, 1])

    POOL = sbuf("POOL", [P, 23100])

    def pv(off_w, n, dtype=F32, p0=0, pn=P):
        if dtype == F32:
            return POOL[p0:p0 + pn, off_w:off_w + n]
        if dtype == BF16:
            return POOL.bitcast(BF16)[p0:p0 + pn, 2 * off_w:2 * off_w + n]
        raise ValueError(dtype)

    class Alloc:
        def __init__(self):
            self.o = 0

        def take(self, nw):
            v = self.o
            self.o += nw
            return v

    aa = Alloc()
    A_XNB = aa.take(NF * TC0 // 2)
    A_WIB = aa.take(8 * 512 // 2)
    A_XS1 = aa.take(TC0 + 4)
    A_XSB = aa.take(NDI * TC0 // 2)
    A_ZB = aa.take(NDI * TC0 // 2)
    A_DT = aa.take(NDI * TC0)
    A_XDB = aa.take(TC0 // 2)
    A_XDF = aa.take(TC0)
    A_BCB = aa.take(DST * TC0 // 2)
    A_BCC = aa.take(DST * TC0 // 2)
    A_DA = aa.take(4 * TC0)
    A_DB = aa.take(4 * TC0)
    A_H4 = aa.take(4 * TC0)
    A_W1 = aa.take(TC0)
    A_Y1 = aa.take(TC0)
    A_YT = aa.take(TC0)
    A_SZ = aa.take(TC0)
    A_YZ = aa.take(NDI * TC0 // 2)
    A_WOB = aa.take(DM // 2)
    assert aa.o <= 23100, aa.o

    fa = Alloc()
    F_QSC = fa.take(DM)
    F_QA = fa.take(DM)
    F_QB = fa.take(DM)
    F_QT = fa.take(DM)
    F_QS = fa.take(DM)
    F_QAM = fa.take(DM // 16)
    F_QRS = fa.take(DM // 16)
    F_WROW = fa.take(DM)
    F_WQR = fa.take(DM // 2)
    F_TSTG = fa.take(22 * 128 // 2)
    F_XNT = fa.take(DM)
    F_X32 = fa.take(256)
    F_XQF = fa.take(NF * 512 // 2)
    F_WQB = fa.take(2 * 8 * 512 // 2)
    F_HT1 = fa.take(512)
    F_HQF = fa.take(NFI * 512 // 2)
    F_WDB = fa.take(DM // 2)
    F_G1 = fa.take(256)
    assert fa.o <= 23100, fa.o

    SR0 = pv(F_WQB, T, F32, 0, 1)
    SR1 = pv(F_WQB + T, T, F32, 0, 1)
    SRT = pv(F_HQF, T, F32, 0, 1)

    PM = [st.enter_context(nc.psum_tensor(f"pm{i}", [P, 512], F32)) for i in range(4)]
    PTB = st.enter_context(nc.psum_tensor("ptb", [P, 1024], BF16))
    PTB2 = st.enter_context(nc.psum_tensor("ptb2", [P, 1024], BF16))
    PF0 = st.enter_context(nc.psum_tensor("pf0", [P, 512], F32))
    PF1 = st.enter_context(nc.psum_tensor("pf1", [P, 512], F32))
    pm_busy = [BufT() for _ in range(4)]
    pf_busy = [BufT() for _ in range(2)]
    ptb_busy = [BufT() for _ in range(2)]

    esems = {e: st.enter_context(nc.semaphore("se_" + e)) for e in ENGS}
    dsems = {q: st.enter_context(nc.semaphore("sd_" + q))
             for q in ("sync", "gpsimd", "scalar")}

    with st:
        bX = BufT()
        xt = [Pr.dma("sync", X[:, f * T:(f + 1) * T], x_in[f * P:(f + 1) * P, :])
              for f in range(NF)]
        bX.wrote(xt[-1])
        for tok in xt:
            bX._note(tok)
        t_idf = Pr.dma("sync", IDF[:], idf32)
        t_idb = Pr.dma("sync", IDB[:], idbf)
        t_one = Pr.dma("sync", ONE[:], ones)
        t_cw = Pr.dma("sync", CW[:], convw)
        t_cb = Pr.dma("sync", CBs[:], convb)
        t_bdt = Pr.dma("sync", BDTs[:], bdt)
        t_alog = Pr.dma("sync", AMAT[:], alog)
        t_dv = Pr.dma("sync", DVs[:], dvec)
        t_wx = Pr.dma("gpsimd", WXs[:], wx_T)
        t_cw1 = Pr.dma("sync", L1CW[:], l1_convw)
        t_cb1 = Pr.dma("sync", L1CB[:], l1_convb)
        t_n1 = Pr.dma("sync", L1N1[:], l1_n1)
        t_n2 = Pr.dma("sync", L1N2[:], l1_n2)
        t_ogb = Pr.dma("sync", L1OGB[:], l1_ogate_b)
        t_mur = Pr.dma("sync", MUR[:], l1_mur)
        t_msks = Pr.dma("sync", MSK_S[:], l2_mask_s)
        t_mskl = Pr.dma("sync", MSK_L[:], l2_mask_l)
        t_rb = Pr.dma("sync", RBIAS[:], l2_rout_b)
        t_o8 = Pr.run("vector", lambda e: e.memset(ONES8[:], 1.0))
        t_bp3 = Pr.dma("sync", BP3[:], l3_bproj)
        t_c0m = Pr.dma("sync", C0M[:], l3_c0)
        t_muk = Pr.dma("sync", MUK[:], l1_muk)
        t_mu1 = Pr.run("vector", lambda e: (
            e.tensor_scalar(out=MUR1[:], in0=MUR[:], scalar1=-1.0, scalar2=1.0,
                            op0=OP.mult, op1=OP.add),
            e.tensor_scalar(out=MUK1[:], in0=MUK[:], scalar1=-1.0, scalar2=1.0,
                            op0=OP.mult, op1=OP.add))[-1], deps=[t_mur, t_muk])
        t_wdt = Pr.dma("gpsimd", WDTs[:], wdt_T)

        t_A = Pr.run("scalar", lambda e: e.activation(out=AMAT[:], in_=AMAT[:],
                                                      func=AF.Exp), deps=[t_alog])
        bA = BufT()
        bA.wrote(t_A)
        t_nb = Pr.run("vector", lambda e: e.tensor_single_scalar(
            out=BDTs[:], in_=BDTs[:], scalar=-1.0, op=OP.mult), deps=[t_bdt])

        WCB = pv(A_WIB, 4096, BF16)
        bWIB = BufT()
        prev_w = None
        wi_ready = None
        wo_ready = None
        cast_list = [(wi_T, wi_bf, DM, 2 * DI), (wo_T, wo_bf, DI, DM)]
        if n_layers >= 2:
            cast_list += [(l1_qkv_T, l1qkv_bf, DM, 3 * DM),
                          (l1_pw_T, l1pw_bf, DM, DM),
                          (l1_ogate_T, l1og_bf, DM, DM),
                          (l1_oproj_T, l1op_bf, DM, DM),
                          (l1_gate_T, l1gt_bf, 2 * DM, DM),
                          (l1_w1_T, l1w1_bf, DM, FI),
                          (l1_w2_T, l1w2_bf, DM, FI),
                          (l1_w3_T, l1w3_bf, FI, DM),
                          (l1_wr_T, l1wr_bf, DM, DM),
                          (l1_wk_T, l1wk_bf, DM, DM),
                          (l1_wv_T, l1wv_bf, DM, DM)]
        if n_layers >= 3:
            cast_list += [(l2_qks, l2qks_bf, DM, 1024),
                          (l2_vs, l2vs_bf, DM, 512),
                          (l2_qkl, l2qkl_bf, DM, 2048),
                          (l2_vl, l2vl_bf, DM, 512),
                          (l2_proj, l2pj_bf, DM, DM),
                          (l2_wexp_T, l2we_bf, 8 * DM, DM)]
        if n_layers >= 4:
            qkv_s1, qkv_d1 = l3_qkv_T[:, 0:2560], l3qkv_bf[:, 0:2560]
            qkv_s2, qkv_d2 = l3_qkv_T[:, 2560:5120], l3qkv_bf[:, 2560:5120]
            cast_list += [(qkv_s1, qkv_d1, DM, 2560),
                          (qkv_s2, qkv_d2, DM, 2560),
                          (l3_proj_T, l3pj_bf, DM, DM),
                          (l3_sw1_T, l3sw1_bf, DM, 1024),
                          (l3_sw3_T, l3sw3_bf, DM, 1024),
                          (l3_sw2_T, l3sw2_bf, DM, DM),
                          (l3_w1, l3w1_bf, 8 * DM, 512),
                          (l3_w3, l3w3_bf, 8 * DM, 512),
                          (l3_w2, l3w2_bf, 8 * 512, DM)]
        cast_ready = {}
        pend = [prev_w, prev_w]
        pi = 0
        for src, dst, rows, cols in cast_list:
            last_tw = None
            for r in range(rows // P):
                c0 = 0
                while c0 < cols:
                    cw = min(2048, cols - c0)
                    h = pi % 2
                    pi += 1
                    off = h * 2048
                    tl = Pr.dma("gpsimd", WCB[:, off:off + cw],
                                src[r * P:(r + 1) * P, c0:c0 + cw],
                                deps=[pend[h]])
                    tw = Pr.dma("sync", dst[r * P:(r + 1) * P, c0:c0 + cw],
                                WCB[:, off:off + cw], deps=[tl])
                    pend[h] = tw
                    last_tw = tw
                    c0 += cw
            cast_ready[id(dst)] = last_tw
        prev_w = pend[pi % 2] if False else last_tw
        if n_layers >= 4:
            cast_ready[id(l3qkv_bf)] = cast_ready[id(qkv_d2)]
        wi_ready = cast_ready[id(wi_bf)]
        wo_ready = cast_ready[id(wo_bf)]
        bWIB.wrote(pend[0])
        bWIB._note(pend[1])

        bS = BufT()
        bSTAB = BufT()
        bAB = BufT()
        bSTATE = BufT()
        bSTATE.wrote(Pr.run("vector", lambda e: e.memset(STATE[:], 0.0)))
        bTAILS = BufT()
        bTAILS.wrote(Pr.run("vector", lambda e: e.memset(TAILS[:], 0.0)))
        bQ = BufT()

        def layernorm(sq_off, extra_deps=()):
            SQ = pv(sq_off, T)
            mm_last = None
            for f in range(NF):
                def mmfn(e, f=f):
                    last = None
                    for ct in range(4):
                        last = e.matmul(PM[ct][0:1, 0:512], ONE[:],
                                        X[:, f * T + ct * 512: f * T + (ct + 1) * 512],
                                        start=(f == 0), stop=(f == NF - 1), skip_group_check=True)
                    return last
                mm_last = Pr.run("tensor", mmfn,
                                 deps=[t_one] + bX.rd() +
                                 (sum((pm_busy[i].wr() for i in range(4)), [])
                                  if f == 0 else []) + (list(extra_deps) if f == 0 else []))
                bX.read(mm_last)
            def evac0(e):
                last = None
                for ct in range(4):
                    last = e.tensor_copy(SR0[:, ct * 512:(ct + 1) * 512],
                                         PM[ct][0:1, 0:512])
                return last
            tev0 = Pr.run("vector", evac0, deps=[mm_last] + bSTAB.wr())
            mm_last2 = None
            tsq = None
            for f in range(NF):
                tsq = Pr.run("scalar", lambda e, f=f: e.activation(
                    out=SQ, in_=X[:, f * T:(f + 1) * T], func=AF.Square),
                    deps=bX.rd() + ([mm_last2] if mm_last2 else list(extra_deps)))
                def mmfn2(e, f=f):
                    last = None
                    for ct in range(4):
                        last = e.matmul(PM[ct][0:1, 0:512], ONE[:],
                                        SQ[:, ct * 512:(ct + 1) * 512],
                                        start=(f == 0), stop=(f == NF - 1), skip_group_check=True)
                    return last
                mm_last2 = Pr.run("tensor", mmfn2,
                                  deps=[tsq, t_one] + ([tev0] if f == 0 else []))
            for i in range(4):
                pm_busy[i].wrote(mm_last2)
            def evac(e):
                last = None
                for ct in range(4):
                    last = e.tensor_copy(SR1[:, ct * 512:(ct + 1) * 512],
                                         PM[ct][0:1, 0:512])
                return last
            tev = Pr.run("vector", evac, deps=[mm_last2, tev0])
            for i in range(4):
                pm_busy[i].read(tev)
            def small(e):
                e.tensor_single_scalar(out=SR0, in_=SR0, scalar=1.0 / DM, op=OP.mult)
                e.tensor_single_scalar(out=SR1, in_=SR1, scalar=1.0 / DM, op=OP.mult)
                e.tensor_tensor(out=SRT, in0=SR0, in1=SR0, op=OP.mult)
                e.tensor_tensor(out=SR1, in0=SR1, in1=SRT, op=OP.subtract)
                return e.tensor_single_scalar(out=SR1, in_=SR1, scalar=1e-5, op=OP.add)
            tsm = Pr.run("vector", small, deps=[tev])
            tsq2 = Pr.run("scalar", lambda e: e.activation(
                out=SR1, in_=SR1, func=AF.Sqrt), deps=[tsm])
            def small2(e):
                e.reciprocal(out=SR1, in_=SR1)
                e.tensor_tensor(out=SRT, in0=SR0, in1=SR1, op=OP.mult)
                return e.tensor_single_scalar(out=SRT, in_=SRT,
                                              scalar=-1.0, op=OP.mult)
            tsm2 = Pr.run("vector", small2, deps=[tsq2])
            bSTAB.wrote(tsm2)
            t_a = Pr.dma("sync", ab_dr[0:1, :], SR1, deps=[tsm2])
            t_ab = Pr.dma("sync", ab_dr[1:2, :], SRT, deps=[tsm2])
            bSTAB.read(t_a)
            bSTAB.read(t_ab)
            return t_ab

        def load_ab(t0, tw, t_ab):
            d = bAB.wr()
            ta = Pr.dma("gpsimd", AB[:, 0:tw],
                        ab_dr[0:1, t0:t0 + tw].to_broadcast([P, tw]),
                        deps=[t_ab] + d)
            tb = Pr.dma("gpsimd", AB[:, 512:512 + tw],
                        ab_dr[1:2, t0:t0 + tw].to_broadcast([P, tw]),
                        deps=[t_ab] + d)
            bAB.wrote(tb)
            bAB._note(ta)
            return [ta, tb]

        def quant(src_ap, dst_ap, w, deps):
            nseg = w // 16
            QSC = pv(F_QSC, w)
            QA = pv(F_QA, w)
            QB = pv(F_QB, w)
            QT = pv(F_QT, w)
            QSg = pv(F_QS, w)
            QAM = pv(F_QAM, nseg)
            QRS = pv(F_QRS, nseg)
            scr = bQ.wr()
            t0_ = Pr.run("vector", lambda e: e.tensor_reduce(
                out=QAM, in_=src_ap.rearrange("p (s k) -> p s k", k=16),
                axis=AX.X, op=OP.max, apply_absolute_value=True),
                deps=list(deps) + scr)
            t1a = Pr.run("vector", lambda e: e.tensor_scalar(
                out=QRS, in0=QAM, scalar1=1.0 / 6.0, scalar2=1e-12,
                op0=OP.mult, op1=OP.max), deps=[t0_])
            t1_ = Pr.run("vector", lambda e: e.reciprocal(out=QAM, in_=QRS),
                         deps=[t1a])
            tabs = Pr.run("scalar", lambda e: e.activation(out=QSC, in_=src_ap,
                                                           func=AF.Abs),
                          deps=list(deps) + scr)
            tsgn = Pr.run("scalar", lambda e: e.sign(out=QSg, in_=src_ap),
                          deps=list(deps) + scr)
            def main(e):
                e.tensor_tensor(out=QSC.rearrange("p (s k) -> p s k", k=16),
                                in0=QSC.rearrange("p (s k) -> p s k", k=16),
                                in1=QAM.unsqueeze(-1).to_broadcast([P, nseg, 16]),
                                op=OP.mult)
                e.tensor_single_scalar(out=QT.bitcast(I32), in_=QSC.bitcast(I32),
                                       scalar=ADD_LO, op=OP.add)
                e.tensor_single_scalar(out=QT.bitcast(I32), in_=QT.bitcast(I32),
                                       scalar=MASK_HI, op=OP.bitwise_and)
                e.scalar_tensor_tensor(out=QT, in0=QT, scalar=1.0, in1=QT,
                                       op0=OP.is_ge, op1=OP.mult)
                e.tensor_scalar(out=QA, in0=QSC, scalar1=0.25, scalar2=0.5,
                                op0=OP.is_gt, op1=OP.mult)
                e.tensor_scalar(out=QB, in0=QSC, scalar1=0.75, scalar2=0.5,
                                op0=OP.is_gt, op1=OP.mult)
                e.tensor_tensor(out=QA, in0=QA, in1=QB, op=OP.add)
                e.tensor_tensor(out=QT, in0=QT, in1=QA, op=OP.max)
                e.tensor_tensor(out=QSg.rearrange("p (s k) -> p s k", k=16),
                                in0=QSg.rearrange("p (s k) -> p s k", k=16),
                                in1=QRS.unsqueeze(-1).to_broadcast([P, nseg, 16]),
                                op=OP.mult)
                return e.tensor_tensor(out=dst_ap, in0=QT, in1=QSg, op=OP.mult)
            tmain = Pr.run("vector", main, deps=[t1_, tabs, tsgn])
            bQ.wrote(tmain)
            return tmain

        def layer0():
            XNB = pv(A_XNB, NF * TC0, BF16)
            WIB = pv(A_WIB, 8 * 512, BF16)
            XS1 = pv(A_XS1, TC0 + 4)
            XSB = pv(A_XSB, NDI * TC0, BF16)
            ZB = pv(A_ZB, NDI * TC0, BF16)
            DT = pv(A_DT, NDI * TC0)
            XDB = pv(A_XDB, TC0, BF16, 0, 96)
            XDF = pv(A_XDF, TC0, F32, 64, 32)
            BCB = pv(A_BCB, DST * TC0, BF16)
            BCC = pv(A_BCC, DST * TC0, BF16)
            DA = pv(A_DA, 4 * TC0)
            DB = pv(A_DB, 4 * TC0)
            H4 = pv(A_H4, 4 * TC0)
            W1 = pv(A_W1, TC0)
            Y1 = pv(A_Y1, TC0)
            YT = pv(A_YT, TC0)
            SZ = pv(A_SZ, TC0)
            YZ = pv(A_YZ, NDI * TC0, BF16)
            WOB = pv(A_WOB, DM, BF16)

            bXNB = BufT()
            bXSB = BufT()
            bZB = BufT()
            bXS1 = BufT()
            bXDB = BufT()
            bXDF = BufT()
            bBC = BufT()
            bDA = BufT()
            bDB = BufT()
            bH4 = BufT()
            bW1 = BufT()
            bY1 = BufT()
            bSZ = BufT()
            bYZ = BufT()
            bWOB = BufT()
            bDT = BufT()
            bDT.read(bWIB.w)

            t_ab = layernorm(A_DT, extra_deps=bWIB.wr())
            bDT.wrote(t_ab)

            for tc in range(NTC0):
                t0 = tc * TC0
                ab_bc = load_ab(t0, TC0, t_ab)

                def xnb_fn(e, t0=t0):
                    last = None
                    for f in range(NF):
                        e.tensor_tensor(out=W1,
                                        in0=X[:, f * T + t0:f * T + t0 + TC0],
                                        in1=AB[:, 0:TC0], op=OP.mult)
                        last = e.tensor_tensor(
                            out=XNB[:, f * TC0:(f + 1) * TC0],
                            in0=W1, in1=AB[:, 512:512 + TC0], op=OP.add)
                    return last
                t_xnb = Pr.run("vector", xnb_fn,
                               deps=ab_bc + bX.rd() + bXNB.wr() + bW1.wr())
                bXNB.wrote(t_xnb)
                bW1.wrote(t_xnb)
                bAB.read(t_xnb)
                bX.read(t_xnb)

                for og in range(8):
                    t_wld = Pr.dma(
                        "sync", WIB,
                        wi_bf.rearrange("(k p) o -> p k o", p=P)[:, :, og * 512:(og + 1) * 512],
                        deps=[wi_ready] + bWIB.wr())
                    bWIB.wrote(t_wld)
                    for m4 in range(4):
                        d = og * 4 + m4
                        bank = d % 4
                        def mmfn(e, m4=m4, bank=bank):
                            last = None
                            for k in range(8):
                                last = e.matmul(
                                    PM[bank][:, 0:TC0],
                                    WIB[:, k * 512 + m4 * 128: k * 512 + (m4 + 1) * 128],
                                    XNB[:, k * TC0:(k + 1) * TC0],
                                    start=(k == 0), stop=(k == 7), skip_group_check=True)
                            return last
                        t_mm = Pr.run("tensor", mmfn,
                                      deps=[t_wld, t_xnb] + pm_busy[bank].wr())
                        pm_busy[bank].wrote(t_mm)
                        bWIB.read(t_mm)
                        bXNB.read(t_mm)
                        if d < NDI:
                            def evconv(e, d=d, bank=bank):
                                e.tensor_copy(XS1[:, 0:3], TAILS[:, d * 3:(d + 1) * 3])
                                e.tensor_copy(XS1[:, 3:3 + TC0], PM[bank][:, 0:TC0])
                                e.tensor_copy(TAILS[:, d * 3:(d + 1) * 3],
                                              PM[bank][:, TC0 - 3:TC0])
                                e.tensor_scalar(out=W1, in0=XS1[:, 0:TC0],
                                                scalar1=CW[:, d * 4:d * 4 + 1],
                                                scalar2=None, op0=OP.mult)
                                last = None
                                for j in range(1, 4):
                                    last = e.scalar_tensor_tensor(
                                        out=W1, in0=XS1[:, j:j + TC0],
                                        scalar=CW[:, d * 4 + j:d * 4 + j + 1],
                                        in1=W1, op0=OP.mult, op1=OP.add)
                                return last
                            t_cv = Pr.run("vector", evconv,
                                          deps=[t_mm, t_cw, bTAILS.w] +
                                          bXS1.wr() + bW1.wr())
                            bXS1.wrote(t_cv)
                            bTAILS.wrote(t_cv)
                            bW1.wrote(t_cv)
                            pm_busy[bank].read(t_cv)
                            t_si = Pr.run("scalar", lambda e, d=d: e.activation(
                                out=XSB[:, d * TC0:(d + 1) * TC0], in_=W1,
                                func=AF.Silu, bias=CBs[:, d:d + 1]),
                                deps=[t_cv, t_cb] + (bXSB.wr() if d == 0 else []))
                            bW1.read(t_si)
                            if d == 0:
                                bXSB.wrote(t_si)
                            else:
                                bXSB.piece(t_si)
                        else:
                            d2 = d - NDI
                            t_z = Pr.run("scalar", lambda e, d2=d2, bank=bank:
                                         e.activation(out=ZB[:, d2 * TC0:(d2 + 1) * TC0],
                                                      in_=PM[bank][:, 0:TC0],
                                                      func=AF.Silu),
                                         deps=[t_mm] + (bZB.wr() if d2 == 0 else []))
                            pm_busy[bank].read(t_z)
                            if d2 == 0:
                                bZB.wrote(t_z)
                            else:
                                bZB.piece(t_z)

                def xdbl(e):
                    last = None
                    for k in range(NDI):
                        last = e.matmul(PF0[0:96, 0:TC0],
                                        WXs[:, k * 96:(k + 1) * 96],
                                        XSB[:, k * TC0:(k + 1) * TC0],
                                        start=(k == 0), stop=(k == NDI - 1), skip_group_check=True)
                    return last
                t_xd = Pr.run("tensor", xdbl,
                              deps=[t_wx, bXSB.w] + pf_busy[0].wr())
                pf_busy[0].wrote(t_xd)
                bXSB.read(t_xd)
                def xdev(e):
                    e.tensor_copy(XDB, PF0[0:96, 0:TC0])
                    return e.tensor_copy(XDF, PF0[64:96, 0:TC0])
                t_xde = Pr.run("vector", xdev, deps=[t_xd] + bXDB.wr() + bXDF.wr())
                bXDB.wrote(t_xde)
                bXDF.wrote(t_xde)
                pf_busy[0].read(t_xde)
                t_bcw = Pr.dma("gpsimd", bc_dr[0:32, t0:t0 + TC0], XDF,
                               deps=[t_xde])
                bXDF.read(t_bcw)
                d_bc = bBC.wr()
                t_bb = Pr.dma("sync", BCB.rearrange("p (n t) -> p n t", n=DST),
                              bc_dr[0:16, t0:t0 + TC0]
                              .unsqueeze(0).to_broadcast([P, DST, TC0]),
                              deps=[t_bcw] + d_bc)
                t_cc = Pr.dma("sync", BCC.rearrange("p (n t) -> p n t", n=DST),
                              bc_dr[16:32, t0:t0 + TC0]
                              .unsqueeze(0).to_broadcast([P, DST, TC0]),
                              deps=[t_bcw] + d_bc)
                bBC.wrote(t_cc)
                bBC._note(t_bb)

                dt_toks = []
                for dd in range(NDI):
                    bank = dd % 4
                    t_m = Pr.run("tensor", lambda e, dd=dd, bank=bank: e.matmul(
                        PM[bank][:, 0:TC0], WDTs[:, dd * P:(dd + 1) * P],
                        XDB[0:DTR, :], start=True, stop=True, skip_group_check=True),
                        deps=[t_wdt, t_xde] + pm_busy[bank].wr())
                    pm_busy[bank].wrote(t_m)
                    bXDB.read(t_m)
                    t_sp = Pr.run("scalar", lambda e, dd=dd, bank=bank: e.activation(
                        out=DT[:, dd * TC0:(dd + 1) * TC0], in_=PM[bank][:, 0:TC0],
                        func=AF.Sigmoid, scale=-1.0, bias=BDTs[:, dd:dd + 1]),
                        deps=[t_m, t_nb] + (bDT.wr() if dd == 0 else []))
                    pm_busy[bank].read(t_sp)
                    if dd == 0:
                        bDT.wrote(t_sp)
                    else:
                        bDT.piece(t_sp)
                t_ln = Pr.run("scalar", lambda e: e.activation(
                    out=DT[:, 0:NDI * TC0], in_=DT[:, 0:NDI * TC0], func=AF.Ln),
                    deps=[bDT.w])
                bDT.wrote(t_ln)
                for dd in range(NDI):
                    dt_toks.append(t_ln)

                for dd in range(NDI):
                    t_w1 = Pr.run("vector", lambda e, dd=dd: e.scalar_tensor_tensor(
                        out=W1, in0=DT[:, dd * TC0:(dd + 1) * TC0], scalar=-1.0,
                        in1=XSB[:, dd * TC0:(dd + 1) * TC0], op0=OP.mult, op1=OP.mult),
                        deps=[dt_toks[dd], bXSB.w] + bW1.wr())
                    bW1.wrote(t_w1)
                    bDT.read(t_w1)
                    bXSB.read(t_w1)
                    last_rd = None
                    for q in range(4):
                        nb = q * 4
                        def exps(e, dd=dd, nb=nb):
                            last = None
                            for j in range(4):
                                last = e.activation(
                                    out=DA[:, j * TC0:(j + 1) * TC0],
                                    in_=DT[:, dd * TC0:(dd + 1) * TC0],
                                    func=AF.Exp,
                                    scale=AMAT[:, dd * DST + nb + j:dd * DST + nb + j + 1])
                            return last
                        t_da = Pr.run("scalar", exps,
                                      deps=[dt_toks[dd]] + bA.rd() + bDA.wr())
                        bDA.wrote(t_da)
                        bA.read(t_da)
                        bDT.read(t_da)
                        t_db = Pr.run("vector", lambda e, nb=nb: e.tensor_tensor(
                            out=DB.rearrange("p (n t) -> p n t", n=4),
                            in0=W1.unsqueeze(1).to_broadcast([P, 4, TC0]),
                            in1=BCB[:, nb * TC0:(nb + 4) * TC0]
                            .rearrange("p (n t) -> p n t", n=4),
                            op=OP.mult),
                            deps=[t_w1, bBC.w] + bDB.wr())
                        bDB.wrote(t_db)
                        bW1.read(t_db)
                        bBC.read(t_db)
                        def scans(e, dd=dd, nb=nb):
                            for j in range(4):
                                e.tensor_tensor_scan(
                                    out=H4[:, j * TC0:(j + 1) * TC0],
                                    data0=DA[:, j * TC0:(j + 1) * TC0],
                                    data1=DB[:, j * TC0:(j + 1) * TC0],
                                    initial=STATE[:, dd * DST + nb + j:dd * DST + nb + j + 1],
                                    op0=OP.mult, op1=OP.add)
                            for j in range(4):
                                e.tensor_copy(
                                    STATE[:, dd * DST + nb + j:dd * DST + nb + j + 1],
                                    H4[:, j * TC0 + TC0 - 1:j * TC0 + TC0])
                            return e.tensor_tensor(
                                out=H4.rearrange("p (n t) -> p n t", n=4),
                                in0=H4.rearrange("p (n t) -> p n t", n=4),
                                in1=BCC[:, nb * TC0:(nb + 4) * TC0]
                                .rearrange("p (n t) -> p n t", n=4),
                                op=OP.mult)
                        t_sc = Pr.run("vector", scans,
                                      deps=[t_da, t_db, bSTATE.w] + bH4.wr())
                        bH4.wrote(t_sc)
                        bDA.read(t_sc)
                        bDB.read(t_sc)
                        bSTATE.wrote(t_sc)
                        bBC.read(t_sc)
                        def red(e, q=q):
                            if q == 0:
                                return e.tensor_reduce(
                                    out=Y1, in_=H4.rearrange("p (n t) -> p t n", n=4),
                                    axis=AX.X, op=OP.add)
                            e.tensor_reduce(out=YT,
                                            in_=H4.rearrange("p (n t) -> p t n", n=4),
                                            axis=AX.X, op=OP.add)
                            return e.tensor_tensor(out=Y1, in0=Y1, in1=YT, op=OP.add)
                        t_rd = Pr.run("vector", red,
                                      deps=[t_sc] + (bY1.wr() if q == 0 else []))
                        bY1.wrote(t_rd)
                        bH4.read(t_rd)
                        last_rd = t_rd
                    t_yd = Pr.run("vector", lambda e, dd=dd: e.scalar_tensor_tensor(
                        out=Y1, in0=XSB[:, dd * TC0:(dd + 1) * TC0],
                        scalar=DVs[:, dd:dd + 1], in1=Y1, op0=OP.mult, op1=OP.add),
                        deps=[last_rd, t_dv, bXSB.w])
                    bY1.wrote(t_yd)
                    bXSB.read(t_yd)
                    t_yz = Pr.run("vector", lambda e, dd=dd: e.tensor_tensor(
                        out=YZ[:, dd * TC0:(dd + 1) * TC0], in0=Y1,
                        in1=ZB[:, dd * TC0:(dd + 1) * TC0], op=OP.mult),
                        deps=[t_yd, bZB.w] + (bYZ.wr() if dd == 0 else []))
                    bZB.read(t_yz)
                    bY1.read(t_yz)
                    if dd == 0:
                        bYZ.wrote(t_yz)
                    else:
                        bYZ.piece(t_yz)

                if False:
                    for c_ in range(NDI):
                        Pr.dma("gpsimd", dbg_xs[c_ * P:(c_ + 1) * P, t0:t0 + TC0],
                               XSB[:, c_ * TC0:(c_ + 1) * TC0], deps=[bXSB.w, bYZ.w])
                        Pr.dma("gpsimd", dbg_yz[c_ * P:(c_ + 1) * P, t0:t0 + TC0],
                               YZ[:, c_ * TC0:(c_ + 1) * TC0], deps=[bYZ.w])
                        tdm = Pr.dma("gpsimd", dbg_dt[c_ * P:(c_ + 1) * P, t0:t0 + TC0],
                                     DT[:, c_ * TC0:(c_ + 1) * TC0], deps=[bDT.w])
                        bXSB.read(tdm)
                        bYZ.read(tdm)
                        bDT.read(tdm)

                for half in range(2):
                    mm_last = None
                    for k in range(NDI):
                        t_wo = Pr.dma("sync", WOB[:, 0:512],
                                      wo_bf[k * P:(k + 1) * P, half * 512:(half + 1) * 512],
                                      deps=[wo_ready] + bWOB.wr())
                        bWOB.wrote(t_wo)
                        def mmo(e, k=k):
                            last = None
                            for oi in range(4):
                                last = e.matmul(
                                    PM[oi][:, 0:TC0],
                                    WOB[:, oi * P:(oi + 1) * P],
                                    YZ[:, k * TC0:(k + 1) * TC0],
                                    start=(k == 0), stop=(k == NDI - 1), skip_group_check=True)
                            return last
                        mm_last = Pr.run("tensor", mmo,
                                         deps=[t_wo, bYZ.w] +
                                         (sum((pm_busy[i].wr() for i in range(4)), [])
                                          if k == 0 else []))
                        bWOB.read(mm_last)
                        bYZ.read(mm_last)
                    for i in range(4):
                        pm_busy[i].wrote(mm_last)
                    def sevac(e, t0=t0, half=half):
                        last = None
                        for oi in range(4):
                            oo = half * 4 + oi
                            last = e.activation(
                                out=S[:, oo * T + t0: oo * T + t0 + TC0],
                                in_=PM[oi][:, 0:TC0], func=AF.Copy)
                        return last
                    t_se = Pr.run("scalar", sevac,
                                  deps=[mm_last] + (bS.wr() if tc == 0 and half == 0 else []))
                    for i in range(4):
                        pm_busy[i].read(t_se)
                    if tc == 0 and half == 0:
                        bS.wrote(t_se)
                    else:
                        bS.piece(t_se)

            attn_join = joinb(bSTAB, bXNB, bXSB, bZB, bDT, bBC, bDA, bDB, bH4, bW1,
                              bY1, bSZ, bYZ, bWOB, bXS1, bXDB, bXDF, bWIB)

            WROW = pv(F_WROW, DM)
            WQR = pv(F_WQR, DM, BF16)
            TSTG = pv(F_TSTG, 22 * 128, BF16)
            XNT = pv(F_XNT, DM)
            X32 = pv(F_X32, 256)
            XQF = pv(F_XQF, NF * 512, BF16)
            WQB = pv(F_WQB, 2 * 8 * 512, BF16)
            HT1 = pv(F_HT1, 512)
            HQF = pv(F_HQF, NFI * 512, BF16)
            WDB = pv(F_WDB, DM, BF16)
            G1 = pv(F_G1, 256)
            bWROW = BufT()
            bWQR = BufT()
            bTSTG = BufT()
            bXNT = BufT()
            bX32 = BufT()
            bXQF = BufT()
            bWQB = BufT()
            bHT1 = BufT()
            bHQF = BufT()
            bWDB = BufT()
            bG1 = BufT()
            bQ.read(attn_join[0] if attn_join else None)
            for tok in attn_join:
                bQ._note(tok)
                bWROW._note(tok)
                bTSTG._note(tok)
                bXNT._note(tok)
                bX32._note(tok)
                bXQF._note(tok)
                bWQB._note(tok)
                bHT1._note(tok)
                bHQF._note(tok)
                bWDB._note(tok)
                bG1._note(tok)
                bWQR._note(tok)

            wq_ready = []
            for wsrc, wdst, nrc, win in ((wg_f, wqg_dr, NFI, DM),
                                         (wu_f, wqu_dr, NFI, DM),
                                         (wd_f, wqd_dr, NF, FI)):
                npc = (win + DM - 1) // DM
                for rc in range(nrc):
                    for pc in range(npc):
                        c0 = pc * DM
                        cw = min(DM, win - c0)
                        ntl = cw // P
                        t_ld = Pr.dma("sync", WROW[:, 0:cw],
                                      wsrc[rc * P:(rc + 1) * P, c0:c0 + cw],
                                      deps=bWROW.wr())
                        bWROW.wrote(t_ld)
                        t_q = quant(WROW[:, 0:cw], WQR[:, 0:cw], cw,
                                    deps=[t_ld] + bWQR.wr())
                        bWROW.read(t_q)
                        bWQR.wrote(t_q)
                        def trp(e, ntl=ntl):
                            last = None
                            for j in range(ntl):
                                last = e.transpose(PTB[:, j * P:(j + 1) * P],
                                                   WQR[:, j * P:(j + 1) * P], IDB[:])
                            return last
                        t_tr = Pr.run("tensor", trp,
                                      deps=[t_q, t_idb] + ptb_busy[0].wr())
                        ptb_busy[0].wrote(t_tr)
                        bWQR.read(t_tr)
                        t_ev = Pr.run("vector", lambda e, ntl=ntl: e.tensor_copy(
                            TSTG[:, 0:ntl * P], PTB[:, 0:ntl * P]),
                            deps=[t_tr] + bTSTG.wr())
                        ptb_busy[0].read(t_ev)
                        bTSTG.wrote(t_ev)
                        t_w = Pr.dma(
                            "gpsimd",
                            wdst.rearrange("(j p) o -> p j o", p=P)
                            [:, pc * 8:pc * 8 + ntl, rc * P:(rc + 1) * P],
                            TSTG[:, 0:ntl * P].rearrange("p (j o) -> p j o", o=P),
                            deps=[t_ev])
                        bTSTG.read(t_w)
                        wq_ready.append(t_w)
            t_wq = wq_ready[-1]
            t_wqg = wq_ready[NFI - 1]
            t_wqu = wq_ready[2 * NFI - 1]

            for sub in range(4):
                tbase = sub * 512
                xq_done = None
                for hh in range(2):
                    h0 = tbase + hh * 256
                    ab_bc = load_ab(h0, TC0, t_ab)
                    tr_toks = []
                    for f in range(NF):
                        def x32fn(e, f=f, h0=h0):
                            e.tensor_tensor(out=X32,
                                            in0=X[:, f * T + h0:f * T + h0 + 256],
                                            in1=AB[:, 0:256], op=OP.mult)
                            return e.tensor_tensor(out=X32, in0=X32,
                                                   in1=AB[:, 512:512 + 256], op=OP.add)
                        t_x32 = Pr.run("vector", x32fn,
                                       deps=ab_bc + bX.rd() + bX32.wr())
                        bX32.wrote(t_x32)
                        bAB.read(t_x32)
                        bX.read(t_x32)
                        def trp2(e, f=f):
                            lo = (f % 4) * P
                            if f < 4:
                                e.transpose(PF0[:, lo:lo + P], X32[:, 0:P], IDF[:])
                                return e.transpose(PM[0][:, lo:lo + P],
                                                   X32[:, P:2 * P], IDF[:])
                            e.transpose(PF1[:, lo:lo + P], X32[:, 0:P], IDF[:])
                            return e.transpose(PM[1][:, lo:lo + P],
                                               X32[:, P:2 * P], IDF[:])
                        t_tr = Pr.run("tensor", trp2,
                                      deps=[t_x32, t_idf] +
                                      (pf_busy[0].wr() + pf_busy[1].wr() +
                                       pm_busy[0].wr() + pm_busy[1].wr()
                                       if f == 0 else []))
                        bX32.read(t_tr)
                        tr_toks.append(t_tr)
                    pf_busy[0].wrote(tr_toks[-1])
                    pf_busy[1].wrote(tr_toks[-1])
                    pm_busy[0].wrote(tr_toks[-1])
                    pm_busy[1].wrote(tr_toks[-1])
                    for tj2 in range(2):
                        tj = hh * 2 + tj2
                        if tj2 == 0:
                            def evx(e):
                                e.tensor_copy(XNT[:, 0:512], PF0[:, 0:512])
                                return e.tensor_copy(XNT[:, 512:1024], PF1[:, 0:512])
                            t_ev = Pr.run("vector", evx,
                                          deps=[tr_toks[-1]] + bXNT.wr())
                            pf_busy[0].read(t_ev)
                            pf_busy[1].read(t_ev)
                        else:
                            def evx(e):
                                e.tensor_copy(XNT[:, 0:512], PM[0][:, 0:512])
                                return e.tensor_copy(XNT[:, 512:1024], PM[1][:, 0:512])
                            t_ev = Pr.run("vector", evx,
                                          deps=[tr_toks[-1]] + bXNT.wr())
                            pm_busy[0].read(t_ev)
                            pm_busy[1].read(t_ev)
                        bXNT.wrote(t_ev)
                        XT, bXT = XNT, bXNT
                        t_q = quant(XT, WQR[:, 0:DM], DM, deps=bXT.rd() + bWQR.wr())
                        bXT.read(t_q)
                        bWQR.wrote(t_q)
                        def trq(e):
                            last = None
                            for j in range(NF):
                                last = e.transpose(PTB[:, j * P:(j + 1) * P],
                                                   WQR[:, j * P:(j + 1) * P], IDB[:])
                            return last
                        t_tr = Pr.run("tensor", trq,
                                      deps=[t_q, t_idb] + ptb_busy[0].wr())
                        ptb_busy[0].wrote(t_tr)
                        bWQR.read(t_tr)
                        t_ev2 = Pr.run("vector", lambda e, tj=tj: e.tensor_copy(
                            XQF.rearrange("p (k t) -> p k t", k=NF)
                            [:, :, tj * P:(tj + 1) * P],
                            PTB[:, 0:NF * P].rearrange("p (k t) -> p k t", k=NF)),
                            deps=[t_tr] + (bXQF.wr() if tj == 0 else []))
                        ptb_busy[0].read(t_ev2)
                        if tj == 0:
                            bXQF.wrote(t_ev2)
                        else:
                            bXQF.piece(t_ev2)
                        xq_done = t_ev2

                if DEBUG_DUMP:
                    for k_ in range(NF):
                        tq_ = Pr.dma("gpsimd", dbg_xq[k_ * P:(k_ + 1) * P, tbase:tbase + 512],
                                     XQF[:, k_ * 512:(k_ + 1) * 512], deps=[xq_done])
                        bXQF.read(tq_)

                ofw_list = [512, 512, 512, 512, 512, 256]
                for of in range(6):
                    ofw = ofw_list[of]
                    t_wg = Pr.dma("sync", WQB[:, 0:8 * ofw].rearrange(
                        "p (k o) -> p k o", k=8),
                        wqg_dr.rearrange("(k p) o -> p k o", p=P)
                        [:, :, of * 512:of * 512 + ofw],
                        deps=[t_wqg] + bWQB.wr())
                    t_wu = Pr.dma("sync", WQB[:, 8 * 512:8 * 512 + 8 * ofw].rearrange(
                        "p (k o) -> p k o", k=8),
                        wqu_dr.rearrange("(k p) o -> p k o", p=P)
                        [:, :, of * 512:of * 512 + ofw],
                        deps=[t_wqu] + bWQB.wr())
                    bWQB.wrote(t_wu)
                    bWQB._note(t_wg)
                    for tj in range(4):
                        def mmg(e, tj=tj, ofw=ofw):
                            last = None
                            for k in range(8):
                                last = e.matmul(
                                    PM[0][:, 0:ofw],
                                    XQF[:, k * 512 + tj * P: k * 512 + (tj + 1) * P],
                                    WQB[:, k * ofw:(k + 1) * ofw],
                                    start=(k == 0), stop=(k == 7), skip_group_check=True)
                            return last
                        t_g = Pr.run("tensor", mmg,
                                     deps=[t_wg, xq_done] + pm_busy[0].wr())
                        pm_busy[0].wrote(t_g)
                        bXQF.read(t_g)
                        bWQB.read(t_g)
                        def mmu(e, tj=tj, ofw=ofw):
                            last = None
                            for k in range(8):
                                last = e.matmul(
                                    PM[1][:, 0:ofw],
                                    XQF[:, k * 512 + tj * P: k * 512 + (tj + 1) * P],
                                    WQB[:, 8 * 512 + k * ofw:8 * 512 + (k + 1) * ofw],
                                    start=(k == 0), stop=(k == 7), skip_group_check=True)
                            return last
                        t_u = Pr.run("tensor", mmu,
                                     deps=[t_wu, xq_done] + pm_busy[1].wr())
                        pm_busy[1].wrote(t_u)
                        bXQF.read(t_u)
                        bWQB.read(t_u)
                        t_si = Pr.run("scalar", lambda e, ofw=ofw: e.activation(
                            out=HT1[:, 0:ofw], in_=PM[0][:, 0:ofw], func=AF.Silu),
                            deps=[t_g] + bHT1.wr())
                        pm_busy[0].read(t_si)
                        bHT1.wrote(t_si)
                        t_h = Pr.run("vector", lambda e, ofw=ofw: e.tensor_tensor(
                            out=HT1[:, 0:ofw], in0=HT1[:, 0:ofw],
                            in1=PM[1][:, 0:ofw], op=OP.mult),
                            deps=[t_si, t_u])
                        if DEBUG_DUMP:
                            t_hd = Pr.dma("gpsimd",
                                          dbg_hs[tbase + tj * P:tbase + (tj + 1) * P,
                                                 of * 512:of * 512 + ofw],
                                          HT1[:, 0:ofw], deps=[t_h])
                            bHT1.read(t_hd)
                            bHT1.w = t_h
                        pm_busy[1].read(t_h)
                        bHT1.wrote(t_h)
                        t_q = quant(HT1[:, 0:ofw], WQR[:, 0:ofw], ofw,
                                    deps=[t_h] + bWQR.wr())
                        bHT1.read(t_q)
                        bWQR.wrote(t_q)
                        ntl = ofw // P
                        def trh(e, ntl=ntl):
                            last = None
                            for j in range(ntl):
                                last = e.transpose(PTB2[:, j * P:(j + 1) * P],
                                                   WQR[:, j * P:(j + 1) * P], IDB[:])
                            return last
                        t_tr = Pr.run("tensor", trh,
                                      deps=[t_q, t_idb] + ptb_busy[1].wr())
                        ptb_busy[1].wrote(t_tr)
                        bWQR.read(t_tr)
                        c0 = of * 4
                        first_piece = (of == 0 and tj == 0)
                        t_ev3 = Pr.run("vector", lambda e, ntl=ntl, c0=c0, tj=tj:
                                       e.tensor_copy(
                                           HQF.rearrange("p (c t) -> p c t", c=NFI)
                                           [:, c0:c0 + ntl, tj * P:(tj + 1) * P],
                                           PTB2[:, 0:ntl * P]
                                           .rearrange("p (c t) -> p c t", c=ntl)),
                                       deps=[t_tr] + (bHQF.wr() if first_piece else []))
                        ptb_busy[1].read(t_ev3)
                        if first_piece:
                            bHQF.wrote(t_ev3)
                        else:
                            bHQF.piece(t_ev3)

                if DEBUG_DUMP:
                    for c_ in range(NFI):
                        th_ = Pr.dma("gpsimd", dbg_hq[c_ * P:(c_ + 1) * P, tbase:tbase + 512],
                                     HQF[:, c_ * 512:(c_ + 1) * 512], deps=[bHQF.w])
                        bHQF.read(th_)

                for hh in range(2):
                    col = hh * 256
                    for half in range(2):
                        mm_last = None
                        for c in range(NFI):
                            t_wd = Pr.dma("sync", WDB[:, 0:512],
                                          wqd_dr[c * P:(c + 1) * P, half * 512:(half + 1) * 512],
                                          deps=[t_wq] + bWDB.wr())
                            bWDB.wrote(t_wd)
                            def mmd(e, c=c, col=col):
                                last = None
                                for oi in range(4):
                                    last = e.matmul(
                                        PM[oi][:, 0:256],
                                        WDB[:, oi * P:(oi + 1) * P],
                                        HQF[:, c * 512 + col: c * 512 + col + 256],
                                        start=(c == 0), stop=(c == NFI - 1), skip_group_check=True)
                                return last
                            mm_last = Pr.run("tensor", mmd,
                                             deps=[t_wd, bHQF.w] +
                                             (sum((pm_busy[i].wr() for i in range(4)), [])
                                              if c == 0 else []))
                            bWDB.read(mm_last)
                            bHQF.read(mm_last)
                        for i in range(4):
                            pm_busy[i].wrote(mm_last)
                        for oi in range(4):
                            oo = half * 4 + oi
                            t_gs = Pr.run("scalar", lambda e, oi=oi: e.activation(
                                out=G1, in_=PM[oi][:, 0:256], func=AF.Silu),
                                deps=[mm_last] + bG1.wr())
                            pm_busy[oi].read(t_gs)
                            bG1.wrote(t_gs)
                            tcol = tbase + col
                            def upd(e, oo=oo, tcol=tcol):
                                e.tensor_tensor(out=G1, in0=G1,
                                                in1=S[:, oo * T + tcol:oo * T + tcol + 256],
                                                op=OP.mult)
                                return e.tensor_tensor(
                                    out=X[:, oo * T + tcol:oo * T + tcol + 256],
                                    in0=X[:, oo * T + tcol:oo * T + tcol + 256],
                                    in1=G1, op=OP.add)
                            t_up = Pr.run("vector", upd, deps=[t_gs, bS.w] + bX.rd())
                            bG1.wrote(t_up)
                            bS.read(t_up)
                            bX.piece(t_up)

            return joinb(bSTAB, bXNB, bXSB, bZB, bDT, bBC, bDA, bDB, bH4, bW1,
                         bY1, bYZ, bWOB, bXS1, bXDB, bXDF, bWIB, bQ, bWROW,
                         bWQR, bTSTG, bXNT, bX32, bXQF, bWQB, bHT1, bHQF, bWDB,
                         bG1)

        def layer1(join_in):
            la = Alloc()
            L_RSTD = la.take(512)
            L_ROW = la.take(512)
            L_ROW2 = la.take(512)
            L_ROW3 = la.take(512)
            L_HB = la.take(NF * 516 // 2)
            L_CVSV = la.take(2048)
            L_LOC = la.take(2048)
            L_O = la.take(2048)
            L_GB = la.take(2048)
            L_XI = la.take(4096)
            L_SH = la.take(6144)
            L_WS = la.take(512)
            assert la.o <= 23100, la.o

            RSTD = pv(L_RSTD, 512)
            ROW0 = pv(L_ROW, 512, F32, 0, 1)
            ROW1 = pv(L_ROW2, 512, F32, 0, 1)
            ROWB = pv(L_ROW3, 512, F32, 0, 1)
            HB = pv(L_HB, NF * 516, BF16)
            AT = pv(L_HB, NF * 512, BF16)
            CVSV = pv(L_CVSV, 4096, BF16)
            LOC = pv(L_LOC, 4096, BF16)
            H2B = pv(L_LOC, 4096, BF16)
            OB = pv(L_O, 4096, BF16)
            GB = pv(L_GB, 4096, BF16)
            XI = pv(L_XI, 4096)
            QH = pv(L_SH, 512)
            KH = pv(L_SH + 512, 512)
            T1 = pv(L_SH + 1024, 512)
            T2 = pv(L_SH + 1536, 512)
            KVB = pv(L_SH + 2048, 512)
            MB = pv(L_SH, 22 * 512, BF16)
            M1 = pv(L_SH + 5632, 512)
            WS = pv(L_WS, 1024, BF16)
            WSg = pv(L_SH + 3072, 4096, BF16)
            W1S = pv(L_GB, 4096, BF16)
            W2S = pv(L_O, 4096, BF16)

            bufs = {}
            for nm in ("RSTD", "ROW", "HB", "CVSV", "LOC", "OB", "GB", "XI",
                       "SH", "WS", "ROWD"):
                bufs[nm] = BufT()
                for tok in join_in:
                    bufs[nm]._note(tok)
            bR = bufs["RSTD"]; bRW = bufs["ROW"]; bHB = bufs["HB"]
            bCV = bufs["CVSV"]; bLC = bufs["LOC"]; bO = bufs["OB"]
            bG = bufs["GB"]; bXI = bufs["XI"]; bSH = bufs["SH"]
            bWS = bufs["WS"]; bRD = bufs["ROWD"]

            t_ab = layernorm(L_SH, extra_deps=join_in)
            t_dc0 = Pr.dma("sync", pv(L_ROW, 8, F32, 0, 1), l1_decay,
                           deps=bRW.wr())
            t_dc1 = Pr.run("scalar", lambda e: e.activation(
                out=pv(L_ROW, 8, F32, 0, 1), in_=pv(L_ROW, 8, F32, 0, 1),
                func=AF.Sigmoid), deps=[t_dc0])
            t_dc2 = Pr.run("vector", lambda e: e.tensor_single_scalar(
                out=pv(L_ROW, 8, F32, 0, 1), in_=pv(L_ROW, 8, F32, 0, 1),
                scalar=1e-8, op=OP.max), deps=[t_dc1])
            t_dc3 = Pr.dma("sync", dec_dr, pv(L_ROW, 8, F32, 0, 1), deps=[t_dc2])
            t_dc = Pr.dma("gpsimd", DECB[:], dec_dr.to_broadcast([P, 8]),
                          deps=[t_dc3])
            bRW.wrote(t_dc3)
            t_st0 = Pr.run("vector", lambda e: (
                e.memset(KVST[:], 0.0), e.memset(KSST[:], 0.0),
                e.memset(L1TL[:], 0.0), e.memset(XPRV[:], 0.0))[-1], deps=join_in)
            bKV = BufT(); bKV.wrote(t_st0)
            t_pad = Pr.run("vector", lambda e: e.memset(HB, 0.0),
                           deps=join_in + bHB.wr())
            bHB.wrote(t_pad)

            def r1(e):
                e.tensor_tensor(out=ROW1, in0=ROW0, in1=ROW0, op=OP.mult)
                return e.tensor_scalar(out=ROW1, in0=ROW1, scalar1=-1e-5,
                                       scalar2=1.0 + 1e-6, op0=OP.mult, op1=OP.add)

            for tc in range(4):
                t0 = tc * 512
                ab_bc = load_ab(t0, 512, t_ab)
                t_ra = Pr.dma("sync", ROW0, ab_dr[0:1, t0:t0 + 512], deps=bRW.wr())
                bRW.wrote(t_ra)
                t_r1 = Pr.run("vector", r1, deps=[t_ra])
                t_r2 = Pr.run("scalar", lambda e: e.activation(
                    out=ROW1, in_=ROW1, func=AF.Sqrt), deps=[t_r1])
                t_r3 = Pr.run("vector", lambda e: e.reciprocal(out=ROW1, in_=ROW1),
                              deps=[t_r2])
                bRW._note(t_r3); bRW.w = t_r3
                t_rb = Pr.dma("sync", ROWB, ab_dr[1:2, t0:t0 + 512], deps=[t_r3])
                def r3(e):
                    e.tensor_tensor(out=ROW0, in0=ROW0, in1=ROW1, op=OP.mult)
                    return e.tensor_tensor(out=ROWB, in0=ROWB, in1=ROW1, op=OP.mult)
                t_r4 = Pr.run("vector", r3, deps=[t_rb, t_r3])
                t_r5 = Pr.dma("sync", row_dr[0:1, 0:512], ROW0, deps=[t_r4])
                t_r6 = Pr.dma("sync", row_dr[1:2, 0:512], ROWB, deps=[t_r4])
                bRW.wrote(t_r6); bRW._note(t_r5)
                d_r = bR.wr()
                t_a2 = Pr.dma("gpsimd", RSTD, row_dr[0:1, 0:512].to_broadcast([P, 512]),
                              deps=[t_r5] + d_r)
                t_b2 = Pr.dma("gpsimd", pv(L_XI, 512),
                              row_dr[1:2, 0:512].to_broadcast([P, 512]),
                              deps=[t_r6] + bXI.wr())
                bR.wrote(t_a2)
                bXI.wrote(t_b2)

                hb_toks = []
                for f in range(NF):
                    def hfn(e, f=f, t0=t0):
                        e.tensor_tensor(out=M1, in0=X[:, f * T + t0:f * T + t0 + 512],
                                        in1=RSTD, op=OP.mult)
                        e.tensor_tensor(out=M1, in0=M1, in1=pv(L_XI, 512), op=OP.add)
                        return e.tensor_scalar(out=HB[:, f * 516 + 4:f * 516 + 516],
                                               in0=M1, scalar1=L1N1[:, f:f + 1],
                                               scalar2=None, op0=OP.mult)
                    tk = Pr.run("vector", hfn,
                                deps=[t_a2, t_b2, t_n1] + bX.rd() + bSH.wr() +
                                ([t_pad] if tc == 0 else bHB.wr() if f == 0 else []))
                    bSH.wrote(tk)
                    bX.read(tk)
                    if f == 0 and tc == 0:
                        bHB.wrote(tk)
                    else:
                        bHB.piece(tk)
                    hb_toks.append(tk)
                def tfn(e):
                    last = None
                    for f in range(NF):
                        last = e.tensor_copy(HB[:, f * 516 + 1:f * 516 + 4],
                                             L1TL[:, f * 3:(f + 1) * 3].bitcast(BF16)[:, 0:3])
                    return last
                t_tl = Pr.run("vector", tfn, deps=hb_toks + [bKV.w] + bHB.wr())
                bHB.piece(t_tl)
                def tsv(e):
                    last = None
                    for f in range(NF):
                        last = e.tensor_copy(
                            L1TL[:, f * 3:(f + 1) * 3].bitcast(BF16)[:, 0:3],
                            HB[:, f * 516 + 513:f * 516 + 516])
                    return last
                t_ts = Pr.run("vector", tsv, deps=[t_tl])
                bKV.wrote(t_ts)
                bHB.read(t_ts)

                cv_toks = []
                for f in range(NF):
                    def cfn(e, f=f):
                        e.tensor_scalar(out=M1, in0=HB[:, f * 516 + 1:f * 516 + 513],
                                        scalar1=L1CW[:, f * 4:f * 4 + 1],
                                        scalar2=None, op0=OP.mult)
                        last = None
                        for j in range(1, 4):
                            last = e.scalar_tensor_tensor(
                                out=M1, in0=HB[:, f * 516 + 1 + j:f * 516 + 513 + j],
                                scalar=L1CW[:, f * 4 + j:f * 4 + j + 1],
                                in1=M1, op0=OP.mult, op1=OP.add)
                        return last
                    tk = Pr.run("vector", cfn,
                                deps=[t_ts, t_cw1] + bSH.wr())
                    bSH.wrote(tk)
                    bHB.read(tk)
                    tk2 = Pr.run("scalar", lambda e, f=f: e.activation(
                        out=CVSV[:, f * 512:(f + 1) * 512], in_=M1, func=AF.Silu,
                        bias=L1CB[:, f:f + 1]),
                        deps=[tk, t_cb1] + (bCV.wr() if f == 0 else []))
                    bSH.read(tk2)
                    if f == 0:
                        bCV.wrote(tk2)
                    else:
                        bCV.piece(tk2)
                    cv_toks.append(tk2)

                if DEBUG_L1:
                    for f_ in range(NF):
                        th_ = Pr.dma("gpsimd",
                                     dbg_yz[DI // 2 + f_ * P:DI // 2 + (f_ + 1) * P,
                                            t0:t0 + 512],
                                     HB[:, f_ * 516 + 4:f_ * 516 + 516],
                                     deps=[bHB.w, bCV.w])
                        bHB.read(th_)
                        bCV.read(th_)

                if DEBUG_L1:
                    for f_ in range(NF):
                        tc_ = Pr.dma("gpsimd",
                                     dbg_dt[DM + f_ * P:DM + (f_ + 1) * P, t0:t0 + 512],
                                     CVSV[:, f_ * 512:(f_ + 1) * 512],
                                     deps=[bCV.w])
                        bCV.read(tc_)

                def mm_set(wbf, wrows, o_list, rhs_fn, evac_fn, strip_buf, strip_bt,
                           extra_w_dep=None):
                    toks = []
                    nk = wrows // P
                    for oi, oc in enumerate(o_list):
                        t_w = Pr.dma("sync",
                                     strip_buf[:, 0:nk * P].rearrange(
                                         "p (k o) -> p k o", o=P),
                                     wbf.rearrange("(k p) o -> p k o", p=P)
                                     [:, :, oc * P:(oc + 1) * P],
                                     deps=([extra_w_dep] if extra_w_dep else []) +
                                     strip_bt.wr())
                        strip_bt.wrote(t_w)
                        bank = oi % 2
                        def mmfn(e, nk=nk, bank=bank):
                            last = None
                            for k in range(nk):
                                last = e.matmul(PM[bank][:, 0:512],
                                                strip_buf[:, k * P:(k + 1) * P],
                                                rhs_fn(k),
                                                start=(k == 0), stop=(k == nk - 1),
                                                skip_group_check=True)
                            return last
                        rhs_deps = [rhs_fn("dep")]
                        t_mm = Pr.run("tensor", mmfn,
                                      deps=[t_w] + rhs_deps + pm_busy[bank].wr())
                        pm_busy[bank].wrote(t_mm)
                        strip_bt.read(t_mm)
                        t_ev = evac_fn(oc, bank, t_mm)
                        pm_busy[bank].read(t_ev)
                        toks.append(t_ev)
                    return toks

                def loc_ev(oc, bank, t_mm):
                    tk = Pr.run("vector", lambda e, oc=oc, bank=bank: e.tensor_copy(
                        LOC[:, oc * 512:(oc + 1) * 512], PM[bank][:, 0:512]),
                        deps=[t_mm] + (bLC.wr() if oc == 0 else []))
                    if oc == 0:
                        bLC.wrote(tk)
                    else:
                        bLC.piece(tk)
                    return tk
                def cv_rhs(k):
                    if k == "dep":
                        return bCV.w
                    return CVSV[:, k * 512:(k + 1) * 512]
                loc_toks = mm_set(l1pw_bf, DM, list(range(8)), cv_rhs, loc_ev,
                                  WS, bWS, cast_ready[id(l1pw_bf)])
                for tk in loc_toks:
                    bCV.read(tk)

                def hb_rhs(k):
                    if k == "dep":
                        return bHB.w
                    return HB[:, k * 516 + 4:k * 516 + 516]
                o_piece = [None]
                for h in range(8):
                    def make_ev(dst, kind, h=h):
                        def ev(oc, bank, t_mm):
                            if kind == "v":
                                tk = Pr.run("vector", lambda e: e.tensor_copy(
                                    CVSV[:, h * 512:(h + 1) * 512],
                                    PM[bank][:, 0:512]),
                                    deps=[t_mm] + (bCV.wr() if h == 0 else []))
                                if h == 0:
                                    bCV.wrote(tk)
                                else:
                                    bCV.piece(tk)
                                return tk
                            t1 = Pr.run("vector", lambda e: e.tensor_single_scalar(
                                out=T1, in_=PM[bank][:, 0:512], scalar=0.0, op=OP.min),
                                deps=[t_mm] + bSH.wr())
                            bSH.wrote(t1)
                            t2 = Pr.run("scalar", lambda e: e.activation(
                                out=T1, in_=T1, func=AF.Exp), deps=[t1])
                            bSH.wrote(t2)
                            t3 = Pr.run("vector", lambda e, dst=dst: e.scalar_tensor_tensor(
                                out=dst, in0=PM[bank][:, 0:512], scalar=0.0,
                                in1=T1, op0=OP.max, op1=OP.add), deps=[t2, t_mm])
                            bSH.wrote(t3)
                            return t3
                        return ev
                    tq = mm_set(l1qkv_bf, DM, [h], hb_rhs, make_ev(QH, "q"),
                                WS, bWS, cast_ready[id(l1qkv_bf)])[0]
                    tk_ = mm_set(l1qkv_bf, DM, [8 + h], hb_rhs, make_ev(KH, "k"),
                                 WS, bWS, cast_ready[id(l1qkv_bf)])[0]
                    tv = mm_set(l1qkv_bf, DM, [16 + h], hb_rhs, make_ev(None, "v"),
                                WS, bWS, cast_ready[id(l1qkv_bf)])[0]
                    def kvf(e, h=h):
                        return e.tensor_tensor(out=KVB, in0=KH,
                                               in1=CVSV[:, h * 512:(h + 1) * 512],
                                               op=OP.mult)
                    t_kv = Pr.run("vector", kvf, deps=[tk_, tv] + bSH.wr())
                    bSH.wrote(t_kv)
                    def scf(e, h=h):
                        e.tensor_tensor_scan(out=T1, data0=DECB[:, h:h + 1]
                                             .broadcast_to([P, 512]), data1=KVB,
                                             initial=KVST[:, h:h + 1],
                                             op0=OP.mult, op1=OP.add)
                        e.tensor_copy(KVST[:, h:h + 1], T1[:, 511:512])
                        e.tensor_tensor_scan(out=T2, data0=DECB[:, h:h + 1]
                                             .broadcast_to([P, 512]), data1=KH,
                                             initial=KSST[:, h:h + 1],
                                             op0=OP.mult, op1=OP.add)
                        return e.tensor_copy(KSST[:, h:h + 1], T2[:, 511:512])
                    t_sc = Pr.run("vector", scf, deps=[t_kv, t_dc, bKV.w])
                    bKV.wrote(t_sc)
                    bSH.wrote(t_sc)
                    t_d1 = Pr.run("vector", lambda e: e.tensor_tensor(
                        out=KVB, in0=QH, in1=T2, op=OP.mult), deps=[t_sc])
                    bSH.wrote(t_d1)
                    t_d2 = Pr.run("tensor", lambda e: e.matmul(
                        PF0[0:1, 0:512], ONE[:], KVB, start=True, stop=True,
                        skip_group_check=True),
                        deps=[t_d1, t_one] + pf_busy[0].wr())
                    pf_busy[0].wrote(t_d2)
                    t_d3 = Pr.run("vector", lambda e: e.tensor_copy(
                        ROW0, PF0[0:1, 0:512]), deps=[t_d2] + bRW.wr())
                    pf_busy[0].read(t_d3)
                    t_d4 = Pr.run("vector", lambda e: e.tensor_single_scalar(
                        out=ROW0, in_=ROW0, scalar=1e-6, op=OP.max), deps=[t_d3])
                    t_d5 = Pr.run("vector", lambda e: e.reciprocal(
                        out=ROW0, in_=ROW0), deps=[t_d4])
                    t_d6 = Pr.dma("sync", row_dr[2:3, 0:512], ROW0, deps=[t_d5])
                    bRW.wrote(t_d6)
                    t_d7 = Pr.dma("gpsimd", RSTD,
                                  row_dr[2:3, 0:512].to_broadcast([P, 512]),
                                  deps=[t_d6] + bR.wr())
                    bR.wrote(t_d7)
                    def ofn(e, h=h):
                        e.tensor_tensor(out=T1, in0=QH, in1=T1, op=OP.mult)
                        return e.tensor_tensor(out=OB[:, h * 512:(h + 1) * 512],
                                               in0=T1, in1=RSTD, op=OP.mult)
                    t_o = Pr.run("vector", ofn,
                                 deps=[t_sc, t_d7, tq] + (bO.wr() if h == 0 else []))
                    bSH.wrote(t_o)
                    bR.read(t_o)
                    if h == 0:
                        bO.wrote(t_o)
                    else:
                        bO.piece(t_o)

                def ob_rhs(k):
                    if k == "dep":
                        return bO.w
                    return OB[:, k * 512:(k + 1) * 512]
                def g_ev(oc, bank, t_mm):
                    tk = Pr.run("scalar", lambda e, oc=oc, bank=bank: e.activation(
                        out=GB[:, oc * 512:(oc + 1) * 512], in_=PM[bank][:, 0:512],
                        func=AF.Sigmoid, bias=L1OGB[:, oc:oc + 1]),
                        deps=[t_mm, t_ogb] + (bG.wr() if oc == 0 else []))
                    if oc == 0:
                        bG.wrote(tk)
                    else:
                        bG.piece(tk)
                    return tk
                g_toks = mm_set(l1og_bf, DM, list(range(8)), ob_rhs, g_ev,
                                WS, bWS, cast_ready[id(l1og_bf)])
                for tk in g_toks:
                    bO.read(tk)
                o2_toks = []
                for f in range(NF):
                    def o2f(e, f=f):
                        e.tensor_tensor(out=M1, in0=OB[:, f * 512:(f + 1) * 512],
                                        in1=CVSV[:, f * 512:(f + 1) * 512],
                                        op=OP.subtract)
                        e.tensor_tensor(out=M1, in0=M1,
                                        in1=GB[:, f * 512:(f + 1) * 512], op=OP.mult)
                        return e.tensor_tensor(out=OB[:, f * 512:(f + 1) * 512],
                                               in0=M1,
                                               in1=CVSV[:, f * 512:(f + 1) * 512],
                                               op=OP.add)
                    tk = Pr.run("vector", o2f,
                                deps=[g_toks[f], bO.w, bCV.w] + bSH.wr())
                    bSH.wrote(tk)
                    bG.read(tk)
                    bCV.read(tk)
                    bO.piece(tk)
                    o2_toks.append(tk)
                def at_ev(oc, bank, t_mm):
                    tk = Pr.run("vector", lambda e, oc=oc, bank=bank: e.tensor_copy(
                        AT[:, oc * 512:(oc + 1) * 512], PM[bank][:, 0:512]),
                        deps=[t_mm] + (bHB.wr() if oc == 0 else []))
                    if oc == 0:
                        bHB.wrote(tk)
                    else:
                        bHB.piece(tk)
                    return tk
                at_toks = mm_set(l1op_bf, DM, list(range(8)), ob_rhs, at_ev,
                                 WS, bWS, cast_ready[id(l1op_bf)])
                for tk in at_toks:
                    bO.read(tk)

                def comb_rhs(k):
                    if k == "dep":
                        return bHB.w
                    if k < 8:
                        return LOC[:, k * 512:(k + 1) * 512]
                    return AT[:, (k - 8) * 512:(k - 8) * 512 + 512]
                def gm_ev(oc, bank, t_mm):
                    t1 = Pr.run("scalar", lambda e, bank=bank: e.activation(
                        out=M1, in_=PM[bank][:, 0:512], func=AF.Sigmoid),
                        deps=[t_mm] + bSH.wr())
                    bSH.wrote(t1)
                    def mixf(e, oc=oc, t0=t0):
                        e.tensor_tensor(out=QH, in0=LOC[:, oc * 512:(oc + 1) * 512],
                                        in1=AT[:, oc * 512:(oc + 1) * 512],
                                        op=OP.subtract)
                        e.tensor_tensor(out=QH, in0=QH, in1=M1, op=OP.mult)
                        e.tensor_tensor(out=QH, in0=QH,
                                        in1=AT[:, oc * 512:(oc + 1) * 512], op=OP.add)
                        e.tensor_tensor(out=T2, in0=X[:, oc * T + t0:oc * T + t0 + 512],
                                        in1=AB[:, 0:512], op=OP.mult)
                        e.tensor_tensor(out=T2, in0=T2, in1=AB[:, 512:1024], op=OP.add)
                        return e.tensor_tensor(out=XI[:, oc * 512:(oc + 1) * 512],
                                               in0=QH, in1=T2, op=OP.add)
                    tk = Pr.run("vector", mixf,
                                deps=[t1, bHB.w, bLC.w] + ab_bc + bX.rd() +
                                (bXI.wr() if oc == 0 else []))
                    bSH.wrote(tk)
                    bHB.read(tk)
                    bLC.read(tk)
                    bX.read(tk)
                    bAB.read(tk)
                    if oc == 0:
                        bXI.wrote(tk)
                    else:
                        bXI.piece(tk)
                    return tk
                mix_toks = mm_set(l1gt_bf, 2 * DM, list(range(8)), comb_rhs, gm_ev,
                                  WSg, bufs["WS"] if False else bSH,
                                  cast_ready[id(l1gt_bf)])

                sq_toks = []
                mmrow = [None]
                for f in range(NF):
                    tsq = Pr.run("scalar", lambda e, f=f: e.activation(
                        out=QH, in_=XI[:, f * 512:(f + 1) * 512], func=AF.Square),
                        deps=[bXI.w] + bSH.wr())
                    bSH.wrote(tsq)
                    tmm = Pr.run("tensor", lambda e, f=f: e.matmul(
                        PF0[0:1, 0:512], ONE[:], QH, start=(f == 0),
                        stop=(f == NF - 1), skip_group_check=True),
                        deps=[tsq, t_one] + (pf_busy[0].wr() if f == 0 else []))
                    bSH.read(tmm)
                    mmrow[0] = tmm
                pf_busy[0].wrote(mmrow[0])
                t_q1 = Pr.run("vector", lambda e: e.tensor_copy(
                    ROW0, PF0[0:1, 0:512]), deps=[mmrow[0]] + bRW.wr())
                pf_busy[0].read(t_q1)
                t_q2 = Pr.run("vector", lambda e: e.tensor_scalar(
                    out=ROW0, in0=ROW0, scalar1=1.0 / DM, scalar2=1e-6,
                    op0=OP.mult, op1=OP.add), deps=[t_q1])
                t_q3 = Pr.run("scalar", lambda e: e.activation(
                    out=ROW0, in_=ROW0, func=AF.Sqrt), deps=[t_q2])
                t_q4 = Pr.run("vector", lambda e: e.reciprocal(out=ROW0, in_=ROW0),
                              deps=[t_q3])
                t_q5 = Pr.dma("sync", row_dr[3:4, 0:512], ROW0, deps=[t_q4])
                bRW.wrote(t_q5)
                t_q6 = Pr.dma("gpsimd", RSTD, row_dr[3:4, 0:512].to_broadcast([P, 512]),
                              deps=[t_q5] + bR.wr())
                bR.wrote(t_q6)
                h2_toks = []
                for f in range(NF):
                    def h2f(e, f=f):
                        e.tensor_tensor(out=QH, in0=XI[:, f * 512:(f + 1) * 512],
                                        in1=RSTD, op=OP.mult)
                        return e.tensor_scalar(out=H2B[:, f * 512:(f + 1) * 512],
                                               in0=QH, scalar1=L1N2[:, f:f + 1],
                                               scalar2=None, op0=OP.mult)
                    tk = Pr.run("vector", h2f,
                                deps=[t_q6, t_n2, bXI.w] + bSH.wr() +
                                (bLC.wr() if f == 0 else []))
                    bSH.wrote(tk)
                    bR.read(tk)
                    bXI.read(tk)
                    if f == 0:
                        bLC.wrote(tk)
                    else:
                        bLC.piece(tk)
                    h2_toks.append(tk)

                for c in range(NFI):
                    t_w1 = Pr.dma("sync", W1S[:, 0:1024].rearrange(
                        "p (k o) -> p k o", o=P),
                        l1w1_bf.rearrange("(k p) o -> p k o", p=P)
                        [:, :, c * P:(c + 1) * P],
                        deps=[cast_ready[id(l1w1_bf)]] + bG.wr())
                    bG.wrote(t_w1)
                    t_w2 = Pr.dma("sync", W2S[:, 0:1024].rearrange(
                        "p (k o) -> p k o", o=P),
                        l1w2_bf.rearrange("(k p) o -> p k o", p=P)
                        [:, :, c * P:(c + 1) * P],
                        deps=[cast_ready[id(l1w2_bf)]] + bO.wr())
                    bO.wrote(t_w2)
                    def mm1(e):
                        last = None
                        for k in range(8):
                            last = e.matmul(PM[2][:, 0:512],
                                            W1S[:, k * P:(k + 1) * P],
                                            H2B[:, k * 512:(k + 1) * 512],
                                            start=(k == 0), stop=(k == 7),
                                            skip_group_check=True)
                        return last
                    t_m1 = Pr.run("tensor", mm1,
                                  deps=[t_w1, bLC.w] + pm_busy[2].wr())
                    pm_busy[2].wrote(t_m1)
                    bG.read(t_m1)
                    bLC.read(t_m1)
                    def mm2(e):
                        last = None
                        for k in range(8):
                            last = e.matmul(PM[3][:, 0:512],
                                            W2S[:, k * P:(k + 1) * P],
                                            H2B[:, k * 512:(k + 1) * 512],
                                            start=(k == 0), stop=(k == 7),
                                            skip_group_check=True)
                        return last
                    t_m2 = Pr.run("tensor", mm2,
                                  deps=[t_w2, bLC.w] + pm_busy[3].wr())
                    pm_busy[3].wrote(t_m2)
                    bO.read(t_m2)
                    bLC.read(t_m2)
                    t_si = Pr.run("scalar", lambda e: e.activation(
                        out=M1, in_=PM[2][:, 0:512], func=AF.Silu),
                        deps=[t_m1] + bSH.wr())
                    bSH.wrote(t_si)
                    pm_busy[2].read(t_si)
                    tk = Pr.run("vector", lambda e, c=c: e.tensor_tensor(
                        out=MB[:, c * 512:(c + 1) * 512],
                        in0=M1, in1=PM[3][:, 0:512], op=OP.mult),
                        deps=[t_si, t_m2] + (bSH.wr() if c == 0 else []))
                    pm_busy[3].read(tk)
                    if c == 0:
                        bSH.wrote(tk)
                    else:
                        bSH.piece(tk)
                for half in range(2):
                    mm_last = None
                    for c in range(NFI):
                        t_w3 = Pr.dma("sync", WS[:, 0:512],
                                      l1w3_bf[c * P:(c + 1) * P,
                                              half * 512:(half + 1) * 512],
                                      deps=[cast_ready[id(l1w3_bf)]] + bWS.wr())
                        bWS.wrote(t_w3)
                        def mmd(e, c=c):
                            last = None
                            for oi in range(4):
                                last = e.matmul(PM[oi][:, 0:512],
                                                WS[:, oi * P:(oi + 1) * P],
                                                MB[:, c * 512:(c + 1) * 512],
                                                start=(c == 0), stop=(c == NFI - 1),
                                                skip_group_check=True)
                            return last
                        mm_last = Pr.run("tensor", mmd,
                                         deps=[t_w3, bSH.w] +
                                         (sum((pm_busy[i].wr() for i in range(4)), [])
                                          if c == 0 else []))
                        bWS.read(mm_last)
                        bSH.read(mm_last)
                    for i in range(4):
                        pm_busy[i].wrote(mm_last)
                    for oi in range(4):
                        oc = half * 4 + oi
                        tk = Pr.run("vector", lambda e, oc=oc, oi=oi, t0=t0:
                                    e.tensor_tensor(
                                        out=S[:, oc * T + t0:oc * T + t0 + 512],
                                        in0=PM[oi][:, 0:512],
                                        in1=XI[:, oc * 512:(oc + 1) * 512], op=OP.add),
                                    deps=[mm_last, bXI.w] + bS.wr())
                        pm_busy[oi].read(tk)
                        bXI.read(tk)
                        bS.piece(tk)

                if DEBUG_L1:
                    for f_ in range(NF):
                        Pr.dma("gpsimd", dbg_xs[f_ * P:(f_ + 1) * P, t0:t0 + 512],
                               LOC[:, f_ * 512:(f_ + 1) * 512], deps=[bLC.w, bHB.w])
                        Pr.dma("gpsimd", dbg_xs[DM + f_ * P:DM + (f_ + 1) * P, t0:t0 + 512],
                               AT[:, f_ * 512:(f_ + 1) * 512], deps=[bHB.w])
                        Pr.dma("gpsimd", dbg_yz[f_ * P:(f_ + 1) * P, t0:t0 + 512],
                               OB[:, f_ * 512:(f_ + 1) * 512], deps=[bO.w])
                        tdm_ = Pr.dma("gpsimd", dbg_dt[f_ * P:(f_ + 1) * P, t0:t0 + 512],
                                      XI[:, f_ * 512:(f_ + 1) * 512], deps=[bXI.w])
                        bLC.read(tdm_)
                        bHB.read(tdm_)
                        bO.read(tdm_)
                        bXI.read(tdm_)

                XRB = pv(L_HB, 4096, BF16)
                XKB = pv(L_CVSV, 4096, BF16)
                RBB = pv(L_LOC, 4096, BF16)
                KAB = pv(L_GB, 4096, BF16)
                SHF = pv(L_XI, 516)
                for f in range(NF):
                    def shfn(e, f=f, t0=t0):
                        e.tensor_copy(SHF[:, 0:1], XPRV[:, f:f + 1])
                        e.tensor_tensor(out=SHF[:, 1:513],
                                        in0=X[:, f * T + t0:f * T + t0 + 512],
                                        in1=AB[:, 0:512], op=OP.mult)
                        e.tensor_tensor(out=SHF[:, 1:513], in0=SHF[:, 1:513],
                                        in1=AB[:, 512:1024], op=OP.add)
                        e.tensor_copy(XPRV[:, f:f + 1], SHF[:, 512:513])
                        e.tensor_scalar(out=M1, in0=SHF[:, 0:512],
                                        scalar1=MUR1[:, f:f + 1], scalar2=None,
                                        op0=OP.mult)
                        e.scalar_tensor_tensor(out=XRB[:, f * 512:(f + 1) * 512],
                                               in0=SHF[:, 1:513],
                                               scalar=MUR[:, f:f + 1], in1=M1,
                                               op0=OP.mult, op1=OP.add)
                        e.tensor_scalar(out=M1, in0=SHF[:, 0:512],
                                        scalar1=MUK1[:, f:f + 1], scalar2=None,
                                        op0=OP.mult)
                        return e.scalar_tensor_tensor(
                            out=XKB[:, f * 512:(f + 1) * 512], in0=SHF[:, 1:513],
                            scalar=MUK[:, f:f + 1], in1=M1, op0=OP.mult, op1=OP.add)
                    tk = Pr.run("vector", shfn,
                                deps=ab_bc + bX.rd() + [t_mu1, bKV.w] + bSH.wr() +
                                (bHB.wr() + bCV.wr() + bXI.wr() if f == 0 else []))
                    bSH.wrote(tk)
                    bKV.wrote(tk)
                    bX.read(tk)
                    bAB.read(tk)
                    bXI.wrote(tk)
                    if f == 0:
                        bHB.wrote(tk)
                        bCV.wrote(tk)
                    else:
                        bHB.piece(tk)
                        bCV.piece(tk)
                def xr_rhs(k):
                    if k == "dep":
                        return bHB.w
                    return XRB[:, k * 512:(k + 1) * 512]
                def r_ev(oc, bank, t_mm):
                    tk = Pr.run("scalar", lambda e, oc=oc, bank=bank: e.activation(
                        out=RBB[:, oc * 512:(oc + 1) * 512], in_=PM[bank][:, 0:512],
                        func=AF.Sigmoid),
                        deps=[t_mm] + (bLC.wr() if oc == 0 else []))
                    if oc == 0:
                        bLC.wrote(tk)
                    else:
                        bLC.piece(tk)
                    return tk
                r_toks = mm_set(l1wr_bf, DM, list(range(8)), xr_rhs, r_ev,
                                WS, bWS, cast_ready[id(l1wr_bf)])
                for tk in r_toks:
                    bHB.read(tk)
                def xk_rhs(k):
                    if k == "dep":
                        return bCV.w
                    return XKB[:, k * 512:(k + 1) * 512]
                def k_ev(oc, bank, t_mm):
                    t1 = Pr.run("scalar", lambda e, bank=bank: e.activation(
                        out=M1, in_=PM[bank][:, 0:512], func=AF.Relu),
                        deps=[t_mm] + bSH.wr())
                    bSH.wrote(t1)
                    tk = Pr.run("scalar", lambda e, oc=oc: e.activation(
                        out=KAB[:, oc * 512:(oc + 1) * 512], in_=M1, func=AF.Square),
                        deps=[t1] + (bG.wr() if oc == 0 else []))
                    bSH.read(tk)
                    if oc == 0:
                        bG.wrote(tk)
                    else:
                        bG.piece(tk)
                    return tk
                k_toks = mm_set(l1wk_bf, DM, list(range(8)), xk_rhs, k_ev,
                                WS, bWS, cast_ready[id(l1wk_bf)])
                for tk in k_toks:
                    bCV.read(tk)
                def ka_rhs(k):
                    if k == "dep":
                        return bG.w
                    return KAB[:, k * 512:(k + 1) * 512]
                def v_ev(oc, bank, t_mm):
                    t1 = Pr.run("vector", lambda e, oc=oc, bank=bank: e.tensor_tensor(
                        out=M1, in0=PM[bank][:, 0:512],
                        in1=RBB[:, oc * 512:(oc + 1) * 512], op=OP.mult),
                        deps=[t_mm, bLC.w] + bSH.wr())
                    bSH.wrote(t1)
                    bLC.read(t1)
                    t2 = Pr.run("scalar", lambda e: e.activation(
                        out=M1, in_=M1, func=AF.Silu), deps=[t1])
                    bSH.wrote(t2)
                    def upd(e, oc=oc, t0=t0):
                        e.tensor_tensor(out=M1, in0=M1,
                                        in1=S[:, oc * T + t0:oc * T + t0 + 512],
                                        op=OP.mult)
                        return e.tensor_tensor(
                            out=X[:, oc * T + t0:oc * T + t0 + 512],
                            in0=X[:, oc * T + t0:oc * T + t0 + 512],
                            in1=M1, op=OP.add)
                    tk = Pr.run("vector", upd, deps=[t2, bS.w] + bX.rd())
                    bSH.wrote(tk)
                    bS.read(tk)
                    bX.piece(tk)
                    return tk
                v_toks = mm_set(l1wv_bf, DM, list(range(8)), ka_rhs, v_ev,
                                WS, bWS, cast_ready[id(l1wv_bf)])
                for tk in v_toks:
                    bG.read(tk)

            return joinb(*bufs.values())


        def layer2(join_in):
            la = Alloc()
            C_XNQ = la.take(NF * 1536 // 2)
            C_KL = la.take(NF * 1536 // 2)
            C_VL = la.take(12 * 512 // 2)
            C_QT = la.take(512)
            C_SS = la.take(1152)
            C_PP = la.take(576)
            C_PT = la.take(576)
            C_OS = la.take(2048)
            C_WS = la.take(1024)
            C_RS = la.take(512)
            C_MS = la.take(512)
            assert la.o <= 23100, la.o

            XNQ = pv(C_XNQ, NF * 1536, BF16)
            KF = pv(C_KL, NF * 1536, BF16)
            VT = pv(C_VL, 12 * 512, BF16)
            QT = pv(C_QT, NF * 128, BF16)
            SS = pv(C_SS, 1152)
            PPB = pv(C_PP, 1152, BF16)
            PT = pv(C_PT, 1152, BF16)
            OSS = pv(C_OS, 2048, BF16)
            OSL = pv(C_OS + 1024, 2048, BF16)
            WS = pv(C_WS, 2048, BF16)
            VW = pv(C_SS, 4096, BF16)
            NM = pv(C_RS, 1)
            SM = pv(C_RS + 4, 1)
            MS = pv(C_MS, 512)
            XN5 = pv(C_XNQ, NF * 512, BF16)
            ME = pv(C_KL, NF * 512, BF16)
            MASKE = pv(C_KL + 4096, 512, BF16)
            LG = pv(C_RS, 512, F32, 0, 8)
            GT = pv(C_MS + 64, 8)
            ROUT = pv(C_PP, 64, BF16)
            BEXB = pv(C_QT, 1024, BF16, 0, 8)
            SELFB = pv(C_VL + 512, 512, BF16, 0, 8)

            bufs = {nm: BufT() for nm in
                    ("XNQ", "KF", "VT", "QT", "SS", "PP", "PT", "OS", "WS",
                     "RS", "MS")}
            for b in bufs.values():
                for tok in join_in:
                    b._note(tok)
            bXQ = bufs["XNQ"]; bKF = bufs["KF"]; bVT = bufs["VT"]
            bQT = bufs["QT"]; bSS = bufs["SS"]; bPP = bufs["PP"]
            bPT = bufs["PT"]; bOS = bufs["OS"]; bWS = bufs["WS"]
            bRS = bufs["RS"]; bMS = bufs["MS"]

            t_ab = layernorm(C_XNQ, extra_deps=join_in)

            def build_xnq(v0, span, extra):
                toks = []
                nsub = (span + 511) // 512
                for si in range(nsub):
                    c0 = v0 + si * 512
                    cw = min(512, v0 + span - c0)
                    abb = load_ab(c0, cw, t_ab)
                    for f in range(NF):
                        def fn(e, f=f, c0=c0, cw=cw, si=si, extra=extra):
                            e.tensor_tensor(out=MS[:, 0:cw],
                                            in0=X[:, f * T + c0:f * T + c0 + cw],
                                            in1=AB[:, 0:cw], op=OP.mult)
                            return e.tensor_tensor(
                                out=XNQ[:, f * 1536 + extra + si * 512:
                                        f * 1536 + extra + si * 512 + cw],
                                in0=MS[:, 0:cw], in1=AB[:, 512:512 + cw], op=OP.add)
                        tk = Pr.run("vector", fn,
                                    deps=abb + bX.rd() + bMS.wr() +
                                    (bXQ.wr() if si == 0 and f == 0 else []))
                        bMS.wrote(tk)
                        bX.read(tk)
                        bAB.read(tk)
                        if si == 0 and f == 0:
                            bXQ.wrote(tk)
                        else:
                            bXQ.piece(tk)
                        toks.append(tk)
                return toks

            def attn_pass(qr, mode):
                t0 = qr * 512
                halo = 256 if mode == "s" else 1024
                span = halo + 512
                v0 = t0 - halo
                pad = 0
                if v0 < 0:
                    pad = -v0
                    v0 = 0
                if pad > 0:
                    def zf(e, pad=pad):
                        last = None
                        for f in range(NF):
                            last = e.memset(XNQ[:, f * 1536:f * 1536 + pad], 0.0)
                        return last
                    tz = Pr.run("vector", zf, deps=bXQ.wr() + [t_ab])
                    bXQ.wrote(tz)
                build_xnq(v0, span - pad, pad)
                nspan = span
                wq_bf = l2qks_bf if mode == "s" else l2qkl_bf
                wv_bf = l2vs_bf if mode == "s" else l2vl_bf
                qcols = 512 if mode == "s" else 1024
                dqk = 64 if mode == "s" else 128
                win = 384 if mode == "s" else 1152
                MSK = MSK_S if mode == "s" else MSK_L
                scale = float(1.0 / np.sqrt(32.0 if mode == "s" else 128.0))
                OSX = OSS if mode == "s" else OSL
                nkc = qcols // P
                for oc in range(nkc):
                    t_w = Pr.dma("sync", WS[:, 0:1024].rearrange(
                        "p (k o) -> p k o", o=P),
                        wq_bf.rearrange("(k p) o -> p k o", p=P)
                        [:, :, qcols + oc * P:qcols + (oc + 1) * P],
                        deps=[cast_ready[id(wq_bf)]] + bWS.wr())
                    bWS.wrote(t_w)
                    for nt in range((nspan + 511) // 512):
                        nw = min(512, nspan - nt * 512)
                        bank = nt % 2
                        def mmfn(e, oc=oc, nt=nt, nw=nw, bank=bank):
                            last = None
                            for k in range(8):
                                last = e.matmul(
                                    PM[bank][:, 0:nw],
                                    WS[:, k * P:(k + 1) * P],
                                    XNQ.rearrange("p (f t) -> p f t", f=NF)
                                    [:, k, nt * 512:nt * 512 + nw],
                                    start=(k == 0), stop=(k == 7),
                                    skip_group_check=True)
                            return last
                        t_mm = Pr.run("tensor", mmfn,
                                      deps=[t_w, bXQ.w] + pm_busy[bank].wr())
                        pm_busy[bank].wrote(t_mm)
                        bWS.read(t_mm)
                        bXQ.read(t_mm)
                        t_ev = Pr.run("scalar", lambda e, oc=oc, nt=nt, nw=nw,
                                      bank=bank: e.activation(
                                          out=KF[:, oc * 1536 + nt * 512:
                                                 oc * 1536 + nt * 512 + nw],
                                          in_=PM[bank][:, 0:nw], func=AF.Copy),
                                      deps=[t_mm] +
                                      (bKF.wr() if oc == 0 and nt == 0 else []))
                        pm_busy[bank].read(t_ev)
                        if oc == 0 and nt == 0:
                            bKF.wrote(t_ev)
                        else:
                            bKF.piece(t_ev)
                nvc = (nspan + P - 1) // P
                t_wv = Pr.dma("sync", VW.rearrange("p (k o) -> p k o", o=512),
                              wv_bf.rearrange("(k p) o -> p k o", p=P),
                              deps=[cast_ready[id(wv_bf)]] + bSS.wr() +
                              bPP.wr() + bPT.wr())
                bSS.wrote(t_wv)
                bPP.wrote(t_wv)
                bPT.wrote(t_wv)
                for tc_ in range(nvc):
                    bank = 2 + (tc_ % 2)
                    def mmv(e, tc_=tc_, bank=bank):
                        last = None
                        for k in range(8):
                            last = e.matmul(
                                PM[bank][:, 0:512],
                                XNQ.rearrange("p (f t) -> p f t", f=NF)
                                [:, k, tc_ * P:(tc_ + 1) * P],
                                VW[:, k * 512:(k + 1) * 512],
                                start=(k == 0), stop=(k == 7),
                                skip_group_check=True)
                        return last
                    t_mm = Pr.run("tensor", mmv,
                                  deps=[t_wv, bXQ.w] + pm_busy[bank].wr())
                    pm_busy[bank].wrote(t_mm)
                    bSS.read(t_mm)
                    bPP.read(t_mm)
                    bPT.read(t_mm)
                    bXQ.read(t_mm)
                    t_ev = Pr.run("scalar", lambda e, tc_=tc_, bank=bank:
                                  e.activation(
                                      out=VT[:, tc_ * 512:(tc_ + 1) * 512],
                                      in_=PM[bank][:, 0:512], func=AF.Copy),
                                  deps=[t_mm] + (bVT.wr() if tc_ == 0 else []))
                    pm_busy[bank].read(t_ev)
                    if tc_ == 0:
                        bVT.wrote(t_ev)
                    else:
                        bVT.piece(t_ev)
                for ti in range(4):
                    qoff = halo + ti * P
                    for oc in range(nkc):
                        t_w = Pr.dma("sync", WS[:, 0:1024].rearrange(
                            "p (k o) -> p k o", o=P),
                            wq_bf.rearrange("(k p) o -> p k o", p=P)
                            [:, :, oc * P:(oc + 1) * P],
                            deps=[cast_ready[id(wq_bf)]] + bWS.wr())
                        bWS.wrote(t_w)
                        bank = oc % 2
                        def mmq(e, oc=oc, qoff=qoff, bank=bank):
                            last = None
                            for k in range(8):
                                last = e.matmul(
                                    PM[bank][:, 0:P],
                                    WS[:, k * P:(k + 1) * P],
                                    XNQ.rearrange("p (f t) -> p f t", f=NF)
                                    [:, k, qoff:qoff + P],
                                    start=(k == 0), stop=(k == 7),
                                    skip_group_check=True)
                            return last
                        t_mm = Pr.run("tensor", mmq,
                                      deps=[t_w, bXQ.w] + pm_busy[bank].wr())
                        pm_busy[bank].wrote(t_mm)
                        bWS.read(t_mm)
                        bXQ.read(t_mm)
                        t_ev = Pr.run("scalar", lambda e, oc=oc, bank=bank:
                                      e.activation(
                                          out=QT[:, oc * P:(oc + 1) * P],
                                          in_=PM[bank][:, 0:P], func=AF.Copy),
                                      deps=[t_mm] + (bQT.wr() if oc == 0 else []))
                        pm_busy[bank].read(t_ev)
                        if oc == 0:
                            bQT.wrote(t_ev)
                        else:
                            bQT.piece(t_ev)
                    nheads = 8
                    for h in range(nheads):
                        if mode == "s":
                            ch = h // 2
                            base = (h % 2) * 64
                        else:
                            ch = h
                            base = 0
                        for sc3 in range(win // 384):
                            bank = sc3 % 2
                            def mms(e, ch=ch, base=base, sc3=sc3, qoff=qoff,
                                    bank=bank, dqk=dqk):
                                return e.matmul(
                                    PM[bank][:, 0:384],
                                    QT[base:base + dqk, ch * P:(ch + 1) * P],
                                    KF[base:base + dqk,
                                       ch * 1536 + qoff - win + P + sc3 * 384:
                                       ch * 1536 + qoff - win + P + sc3 * 384 + 384],
                                    start=True, stop=True,
                                    skip_group_check=True)
                            t_mm = Pr.run("tensor", mms,
                                          deps=[bQT.w, bKF.w] + pm_busy[bank].wr())
                            pm_busy[bank].wrote(t_mm)
                            bQT.read(t_mm)
                            bKF.read(t_mm)
                            t_ms = Pr.run("vector", lambda e, sc3=sc3, bank=bank,
                                          scale=scale: e.scalar_tensor_tensor(
                                              out=SS[:, sc3 * 384:(sc3 + 1) * 384],
                                              in0=PM[bank][:, 0:384], scalar=scale,
                                              in1=MSK[:, sc3 * 384:(sc3 + 1) * 384],
                                              op0=OP.mult, op1=OP.add),
                                          deps=[t_mm, t_msks, t_mskl] +
                                          (bSS.wr() if sc3 == 0 else []))
                            pm_busy[bank].read(t_ms)
                            if sc3 == 0:
                                bSS.wrote(t_ms)
                            else:
                                bSS.piece(t_ms)
                        clean0 = max(0, pad - (qoff - win + P))
                        if clean0 > 0:
                            t_pz = Pr.run("vector",
                                          lambda e, clean0=clean0: e.memset(
                                              SS[:, 0:clean0], -1e30),
                                          deps=[bSS.w])
                            bSS.wrote(t_pz)
                        t_nm = Pr.run("vector", lambda e: e.tensor_reduce(
                            out=NM, in_=SS[:, 0:win], axis=AX.X, op=OP.max,
                            negate=True), deps=[bSS.w] + bRS.wr())
                        bRS.wrote(t_nm)
                        bSS.read(t_nm)
                        t_ex = Pr.run("scalar", lambda e: e.activation(
                            out=SS[:, 0:win], in_=SS[:, 0:win], func=AF.Exp,
                            bias=NM, accum_out=SM), deps=[t_nm])
                        bSS.wrote(t_ex)
                        bRS.wrote(t_ex)
                        t_rs = Pr.run("vector", lambda e: e.reciprocal(
                            out=SM, in_=SM), deps=[t_ex])
                        bRS.wrote(t_rs)
                        t_pn = Pr.run("vector", lambda e: e.tensor_scalar(
                            out=PPB[:, 0:win], in0=SS[:, 0:win], scalar1=SM,
                            scalar2=None, op0=OP.mult),
                            deps=[t_rs] + bPP.wr())
                        bPP.wrote(t_pn)
                        bSS.read(t_pn)
                        bRS.read(t_pn)
                        def trp(e):
                            last = None
                            for j in range(win // P):
                                dst = (PTB[:, j * P:(j + 1) * P] if j < 8
                                       else PTB2[:, 0:P])
                                last = e.transpose(dst,
                                                   PPB[:, j * P:(j + 1) * P],
                                                   IDB[:])
                            return last
                        t_tr = Pr.run("tensor", trp,
                                      deps=[t_pn, t_idb] + ptb_busy[0].wr() +
                                      (ptb_busy[1].wr() if win > 1024 else []))
                        ptb_busy[0].wrote(t_tr)
                        if win > 1024:
                            ptb_busy[1].wrote(t_tr)
                        bPP.read(t_tr)
                        def pev(e):
                            r = e.activation(out=PT[:, 0:min(win, 1024)],
                                             in_=PTB[:, 0:min(win, 1024)],
                                             func=AF.Copy)
                            if win > 1024:
                                r = e.activation(out=PT[:, 1024:win],
                                                 in_=PTB2[:, 0:P],
                                                 func=AF.Copy)
                            return r
                        t_pe = Pr.run("scalar", pev, deps=[t_tr] + bPT.wr())
                        ptb_busy[0].read(t_pe)
                        if win > 1024:
                            ptb_busy[1].read(t_pe)
                        bPT.wrote(t_pe)
                        obank = 2 + ((h // 2) % 2)
                        obase = (h % 2) * 64
                        def mmo(e, h=h, qoff=qoff, obank=obank, obase=obase):
                            last = None
                            nkt = win // P
                            for j in range(nkt):
                                kt = (qoff - win + P) // P + j
                                last = e.matmul(
                                    PM[obank][obase:obase + 64, 0:P],
                                    VT[:, kt * 512 + h * 64:kt * 512 + h * 64 + 64],
                                    PT[:, j * P:(j + 1) * P],
                                    start=(j == 0), stop=(j == nkt - 1),
                                    skip_group_check=True)
                            return last
                        t_om = Pr.run("tensor", mmo,
                                      deps=[t_pe, bVT.w] + pm_busy[obank].wr())
                        pm_busy[obank].wrote(t_om)
                        bPT.read(t_om)
                        bVT.read(t_om)
                        t_oe = Pr.run("scalar", lambda e, h=h, ti=ti, obank=obank,
                                      obase=obase: e.activation(
                                          out=OSX[obase:obase + 64,
                                                  (h // 2) * 512 + ti * P:
                                                  (h // 2) * 512 + ti * P + P],
                                          in_=PM[obank][obase:obase + 64, 0:P],
                                          func=AF.Copy),
                                      deps=[t_om] +
                                      (bOS.wr() if h == 0 and ti == 0 and
                                       mode == "s" else []))
                        pm_busy[obank].read(t_oe)
                        if h == 0 and ti == 0 and mode == "s":
                            bOS.wrote(t_oe)
                        else:
                            bOS.piece(t_oe)

            for qr in range(4):
                t0 = qr * 512
                attn_pass(qr, "s")
                attn_pass(qr, "l")
                for oc in range(8):
                    t_w = Pr.dma("sync", WS[:, 0:1024].rearrange(
                        "p (k o) -> p k o", o=P),
                        l2pj_bf.rearrange("(k p) o -> p k o", p=P)
                        [:, :, oc * P:(oc + 1) * P],
                        deps=[cast_ready[id(l2pj_bf)]] + bWS.wr())
                    bWS.wrote(t_w)
                    bank = oc % 2
                    def mmp(e, bank=bank):
                        last = None
                        for k in range(8):
                            rhs = (OSS[:, (k % 4) * 512:(k % 4) * 512 + 512]
                                   if k < 4 else
                                   OSL[:, (k - 4) * 512:(k - 4) * 512 + 512])
                            last = e.matmul(PM[bank][:, 0:512],
                                            WS[:, k * P:(k + 1) * P], rhs,
                                            start=(k == 0), stop=(k == 7),
                                            skip_group_check=True)
                        return last
                    t_mm = Pr.run("tensor", mmp,
                                  deps=[t_w, bOS.w] + pm_busy[bank].wr())
                    pm_busy[bank].wrote(t_mm)
                    bWS.read(t_mm)
                    bOS.read(t_mm)
                    t_se = Pr.run("scalar", lambda e, oc=oc, t0=t0, bank=bank:
                                  e.activation(
                                      out=S[:, oc * T + t0:oc * T + t0 + 512],
                                      in_=PM[bank][:, 0:512], func=AF.Copy),
                                  deps=[t_mm] + bS.wr())
                    pm_busy[bank].read(t_se)
                    bS.piece(t_se)

            t_ro = Pr.dma("gpsimd", ROUT.rearrange("p (k o) -> p k o", o=8),
                          l2_rout_T.rearrange("(k p) o -> p k o", p=P),
                          deps=bPP.wr())
            bPP.wrote(t_ro)
            t_be = Pr.dma("gpsimd", BEXB, l2_bexp_T, deps=bQT.wr())
            bQT.wrote(t_be)
            t_z1 = Pr.run("vector", lambda e: e.memset(
                pv(C_RS, 512), 0.0), deps=bRS.wr())
            bRS.wrote(t_z1)
            t_z2 = Pr.run("vector", lambda e: e.memset(
                pv(C_MS + 64, 128), 0.0), deps=bMS.wr())
            bMS.wrote(t_z2)
            for tc in range(4):
                t0 = tc * 512
                abb = load_ab(t0, 512, t_ab)
                for f in range(NF):
                    def fn(e, f=f, t0=t0):
                        e.tensor_tensor(out=MS, in0=X[:, f * T + t0:f * T + t0 + 512],
                                        in1=AB[:, 0:512], op=OP.mult)
                        return e.tensor_tensor(
                            out=XN5[:, f * 512:(f + 1) * 512],
                            in0=MS, in1=AB[:, 512:1024], op=OP.add)
                    tk = Pr.run("vector", fn,
                                deps=abb + bX.rd() + bMS.wr() +
                                (bXQ.wr() if f == 0 else []))
                    bMS.wrote(tk)
                    bX.read(tk)
                    bAB.read(tk)
                    if f == 0:
                        bXQ.wrote(tk)
                    else:
                        bXQ.piece(tk)
                def mmr(e):
                    last = None
                    for k in range(8):
                        last = e.matmul(PF0[0:8, 0:512], ROUT[:, k * 8:(k + 1) * 8],
                                        XN5[:, k * 512:(k + 1) * 512],
                                        start=(k == 0), stop=(k == 7),
                                        skip_group_check=True)
                    return last
                t_lg = Pr.run("tensor", mmr,
                              deps=[t_ro, bXQ.w] + pf_busy[0].wr())
                pf_busy[0].wrote(t_lg)
                bXQ.read(t_lg)
                bPP.read(t_lg)
                t_le = Pr.run("vector", lambda e: e.tensor_tensor(
                    out=LG, in0=PF0[0:8, 0:512],
                    in1=RBIAS[:].to_broadcast([8, 512]), op=OP.add),
                    deps=[t_lg, t_rb, t_z1] + bRS.wr())
                bRS.wrote(t_le)
                pf_busy[0].read(t_le)
                for tch in range(4):
                    t_tr = Pr.run("tensor", lambda e, tch=tch: e.transpose(
                        PF1[0:P, 0:P],
                        pv(C_RS, 512)[:, tch * P:(tch + 1) * P],
                        IDF[:]), deps=[t_le, t_idf] + pf_busy[1].wr())
                    pf_busy[1].wrote(t_tr)
                    bRS.read(t_tr)
                    t_ge = Pr.run("vector", lambda e: e.tensor_copy(
                        GT, PF1[0:P, 0:8]), deps=[t_tr, t_z2] + bMS.wr())
                    pf_busy[1].read(t_ge)
                    bMS.wrote(t_ge)
                    t_t1 = Pr.run("vector", lambda e: e.tensor_reduce(
                        out=pv(C_MS, 1), in_=GT, axis=AX.X, op=OP.max),
                        deps=[t_ge])
                    bMS.wrote(t_t1)
                    t_t2 = Pr.run("vector", lambda e: e.tensor_scalar(
                        out=pv(C_MS + 8, 8), in0=GT, scalar1=pv(C_MS, 1),
                        scalar2=None, op0=OP.is_ge), deps=[t_t1])
                    bMS.wrote(t_t2)
                    t_t3 = Pr.run("vector", lambda e: e.tensor_tensor_scan(
                        out=pv(C_MS + 16, 8), data0=ONES8[:, 0:8],
                        data1=pv(C_MS + 8, 8), initial=0.0,
                        op0=OP.mult, op1=OP.add), deps=[t_t2, t_o8])
                    bMS.wrote(t_t3)
                    t_t4a = Pr.run("vector", lambda e: e.tensor_scalar(
                        out=pv(C_MS + 24, 8), in0=pv(C_MS + 16, 8),
                        scalar1=-1.0, scalar2=2.0, op0=OP.mult, op1=OP.add),
                        deps=[t_t3])
                    bMS.wrote(t_t4a)
                    t_t4b = Pr.run("vector", lambda e: e.tensor_scalar(
                        out=pv(C_MS + 24, 8), in0=pv(C_MS + 24, 8),
                        scalar1=0.0, scalar2=None, op0=OP.max), deps=[t_t4a])
                    bMS.wrote(t_t4b)
                    t_t4 = Pr.run("vector", lambda e: e.tensor_tensor(
                        out=pv(C_MS + 48, 8), in0=pv(C_MS + 24, 8),
                        in1=pv(C_MS + 8, 8), op=OP.mult), deps=[t_t4b])
                    bMS.wrote(t_t4)
                    t_t5 = Pr.run("vector", lambda e: e.scalar_tensor_tensor(
                        out=pv(C_MS + 24, 8), in0=pv(C_MS + 48, 8),
                        scalar=-1e30, in1=GT, op0=OP.mult, op1=OP.add),
                        deps=[t_t4])
                    bMS.wrote(t_t5)
                    t_t6 = Pr.run("vector", lambda e: e.tensor_reduce(
                        out=pv(C_MS, 1), in_=pv(C_MS + 24, 8), axis=AX.X,
                        op=OP.max), deps=[t_t5])
                    bMS.wrote(t_t6)
                    t_t7 = Pr.run("vector", lambda e: e.tensor_scalar(
                        out=pv(C_MS + 8, 8), in0=pv(C_MS + 24, 8),
                        scalar1=pv(C_MS, 1), scalar2=None, op0=OP.is_ge),
                        deps=[t_t6])
                    bMS.wrote(t_t7)
                    t_t8 = Pr.run("vector", lambda e: e.tensor_tensor_scan(
                        out=pv(C_MS + 16, 8), data0=ONES8[:, 0:8],
                        data1=pv(C_MS + 8, 8), initial=0.0,
                        op0=OP.mult, op1=OP.add), deps=[t_t7, t_o8])
                    bMS.wrote(t_t8)
                    t_t9a = Pr.run("vector", lambda e: e.tensor_scalar(
                        out=pv(C_MS + 32, 8), in0=pv(C_MS + 16, 8),
                        scalar1=-1.0, scalar2=2.0, op0=OP.mult, op1=OP.add),
                        deps=[t_t8])
                    bMS.wrote(t_t9a)
                    t_t9b = Pr.run("vector", lambda e: e.tensor_scalar(
                        out=pv(C_MS + 32, 8), in0=pv(C_MS + 32, 8),
                        scalar1=0.0, scalar2=None, op0=OP.max), deps=[t_t9a])
                    bMS.wrote(t_t9b)
                    t_t9 = Pr.run("vector", lambda e: e.tensor_tensor(
                        out=pv(C_MS + 40, 8), in0=pv(C_MS + 32, 8),
                        in1=pv(C_MS + 8, 8), op=OP.mult), deps=[t_t9b])
                    bMS.wrote(t_t9)
                    t_sl = Pr.run("vector", lambda e: e.tensor_tensor(
                        out=GT, in0=pv(C_MS + 48, 8), in1=pv(C_MS + 40, 8),
                        op=OP.add), deps=[t_t9])
                    bMS.wrote(t_sl)
                    t_tb = Pr.run("tensor", lambda e, tch=tch: e.transpose(
                        PF1[0:P, 0:P], pv(C_MS + 64, 128), IDF[:]),
                        deps=[t_sl, t_idf] + pf_busy[1].wr())
                    pf_busy[1].wrote(t_tb)
                    bMS.read(t_tb)
                    t_sf = Pr.run("vector", lambda e, tch=tch: e.tensor_copy(
                        SELFB[:, tch * P:(tch + 1) * P], PF1[0:8, 0:P]),
                        deps=[t_tb] + (bVT.wr() if tch == 0 else []))
                    pf_busy[1].read(t_sf)
                    if tch == 0:
                        bVT.wrote(t_sf)
                    else:
                        bVT.piece(t_sf)
                t_sw = Pr.dma("gpsimd", selm_dr[0:8, t0:t0 + 512], SELFB,
                              deps=[bVT.w])
                bVT.read(t_sw)
                for half in range(2):
                    mm_last = None
                    for e_ in range(8):
                        t_mk = Pr.dma("sync", MASKE,
                                      selm_dr[e_:e_ + 1, t0:t0 + 512]
                                      .to_broadcast([P, 512]),
                                      deps=[t_sw] + bKF.wr())
                        bKF.wrote(t_mk)
                        me_toks = []
                        for k in range(8):
                            tk = Pr.run("vector", lambda e, k=k: e.tensor_tensor(
                                out=ME[:, k * 512:(k + 1) * 512],
                                in0=XN5[:, k * 512:(k + 1) * 512],
                                in1=MASKE, op=OP.mult),
                                deps=[t_mk, bXQ.w] +
                                (bKF.wr() if k == 0 else []))
                            bKF.piece(tk)
                            bXQ.read(tk)
                            me_toks.append(tk)
                        for oi in range(4):
                            oc = half * 4 + oi
                            t_w = Pr.dma("sync", WS[:, 0:1024].rearrange(
                                "p (k o) -> p k o", o=P),
                                l2we_bf.rearrange("(k p) o -> p k o", p=P)
                                [:, e_ * 8:(e_ + 1) * 8, oc * P:(oc + 1) * P],
                                deps=[cast_ready[id(l2we_bf)]] + bWS.wr())
                            bWS.wrote(t_w)
                            def mme(e, oi=oi, e_=e_):
                                last = None
                                for k in range(8):
                                    last = e.matmul(
                                        PM[oi][:, 0:512],
                                        WS[:, k * P:(k + 1) * P],
                                        ME[:, k * 512:(k + 1) * 512],
                                        start=(e_ == 0 and k == 0), stop=False,
                                        skip_group_check=True)
                                return last
                            mm_last = Pr.run("tensor", mme,
                                             deps=[t_w, me_toks[-1]] +
                                             (pm_busy[oi].wr()
                                              if e_ == 0 else []))
                            pm_busy[oi].wrote(mm_last)
                            bWS.read(mm_last)
                            bKF.read(mm_last)
                    for oi in range(4):
                        oc = half * 4 + oi
                        t_bm = Pr.run("tensor", lambda e, oi=oi, oc=oc: e.matmul(
                            PM[oi][:, 0:512], BEXB[:, oc * P:(oc + 1) * P],
                            SELFB, start=False, stop=True,
                            skip_group_check=True),
                            deps=[t_be, bVT.w, mm_last] + pm_busy[oi].wr())
                        pm_busy[oi].wrote(t_bm)
                        bVT.read(t_bm)
                        t_gs = Pr.run("scalar", lambda e, oi=oi: e.activation(
                            out=MS, in_=PM[oi][:, 0:512], func=AF.Silu),
                            deps=[t_bm] + bMS.wr())
                        bMS.wrote(t_gs)
                        pm_busy[oi].read(t_gs)
                        def upd(e, oc=oc, t0=t0):
                            e.tensor_tensor(out=MS, in0=MS,
                                            in1=S[:, oc * T + t0:oc * T + t0 + 512],
                                            op=OP.mult)
                            return e.tensor_tensor(
                                out=X[:, oc * T + t0:oc * T + t0 + 512],
                                in0=X[:, oc * T + t0:oc * T + t0 + 512],
                                in1=MS, op=OP.add)
                        t_up = Pr.run("vector", upd, deps=[t_gs, bS.w] + bX.rd())
                        bMS.wrote(t_up)
                        bS.read(t_up)
                        bX.piece(t_up)

            return joinb(*bufs.values())


        def layer3(join_in):
            A_XNQ, A_KB, A_VT = 0, 4096, 8192
            A_CS, A_RP, A_QB = 12288, 14336, 16384
            A_SSC, A_PPC, A_PTC, A_OSC = 16896, 17536, 17856, 18176
            A_WSC, A_RWC, A_MSC = 20224, 21248, 21312
            A_MSK = 21824

            XNQ = pv(A_XNQ, NF * 1024, BF16)
            KB = pv(A_KB, NF * 1024, BF16)
            VT = pv(A_VT, NF * 1024, BF16)
            CCS = pv(A_CS, 1024)
            SSS = pv(A_CS + 1024, 1024)
            QF = pv(A_RP, 1024)
            QRF = pv(A_RP + 1024, 1024)
            VWS = pv(A_RP, 4096, BF16)
            QB = pv(A_QB, NF * 128, BF16)
            SS = pv(A_SSC, 640)
            PPB = pv(A_PPC, 640, BF16)
            PT = pv(A_PTC, 640, BF16)
            OS = pv(A_OSC, NF * 512, BF16)
            WS = pv(A_WSC, 2048, BF16)
            NM = pv(A_RWC, 1)
            SM = pv(A_RWC + 4, 1)
            MS = pv(A_MSC, 512)
            MSK3 = pv(A_MSK, 640)

            bufs = {nm: BufT() for nm in
                    ("XNQ", "KB", "VT", "CS", "RP", "QB", "SS", "PP", "PT",
                     "OS", "WS", "RW", "MS")}
            for b in bufs.values():
                for tok in join_in:
                    b._note(tok)
            bXQ = bufs["XNQ"]; bKB = bufs["KB"]; bVT = bufs["VT"]
            bCS = bufs["CS"]; bRP = bufs["RP"]; bQB = bufs["QB"]
            bSS = bufs["SS"]; bPP = bufs["PP"]; bPT = bufs["PT"]
            bOS = bufs["OS"]; bWS = bufs["WS"]; bRW = bufs["RW"]
            bMS = bufs["MS"]

            t_ab = layernorm(A_XNQ, extra_deps=join_in)
            t_msk3 = Pr.dma("sync", MSK3[:], l3_mask, deps=list(join_in))

            def build_xnq(v0, span, extra):
                for si in range((span + 511) // 512):
                    c0 = v0 + si * 512
                    cw = min(512, v0 + span - c0)
                    abb = load_ab(c0, cw, t_ab)
                    for f in range(NF):
                        def fn(e, f=f, c0=c0, cw=cw, si=si, extra=extra):
                            e.tensor_tensor(out=MS[:, 0:cw],
                                            in0=X[:, f * T + c0:f * T + c0 + cw],
                                            in1=AB[:, 0:cw], op=OP.mult)
                            return e.tensor_tensor(
                                out=XNQ[:, f * 1024 + extra + si * 512:
                                        f * 1024 + extra + si * 512 + cw],
                                in0=MS[:, 0:cw], in1=AB[:, 512:512 + cw],
                                op=OP.add)
                        tk = Pr.run("vector", fn,
                                    deps=abb + bX.rd() + bMS.wr() +
                                    (bXQ.wr() if si == 0 and f == 0 else []))
                        bMS.wrote(tk)
                        bX.read(tk)
                        bAB.read(tk)
                        if si == 0 and f == 0:
                            bXQ.wrote(tk)
                        else:
                            bXQ.piece(tk)

            def rope3(wreg, wreg2, out_ap, cs0, cw, deps):
                def fn(e, cs0=cs0, cw=cw):
                    e.tensor_tensor(out=QF[:, 0:cw], in0=QF[:, 0:cw],
                                    in1=CCS[:, cs0:cs0 + cw], op=OP.mult)
                    e.tensor_tensor(out=QRF[:, 0:cw], in0=QRF[:, 0:cw],
                                    in1=SSS[:, cs0:cs0 + cw], op=OP.mult)
                    return e.tensor_tensor(out=out_ap, in0=QF[:, 0:cw],
                                           in1=QRF[:, 0:cw], op=OP.add)
                return Pr.run("vector", fn, deps=deps)

            WIN3 = 640
            for qr in range(4):
                t0 = qr * 512
                halo = 512
                span = 1024
                v0 = t0 - halo
                pad = -v0 if v0 < 0 else 0
                v0 = max(v0, 0)
                if pad > 0:
                    def zf(e, pad=pad):
                        last = None
                        for f in range(NF):
                            last = e.memset(XNQ[:, f * 1024:f * 1024 + pad], 0.0)
                        return last
                    tz = Pr.run("vector", zf, deps=bXQ.wr() + [t_ab])
                    bXQ.wrote(tz)
                build_xnq(v0, span - pad, pad)
                t_cs = Pr.dma("sync", CCS[:, pad:1024], l3_cc[:, v0:t0 + 512],
                              deps=bCS.wr())
                t_cs2 = Pr.dma("sync", SSS[:, pad:1024], l3_ssn[:, v0:t0 + 512],
                               deps=[t_cs])
                bCS.wrote(t_cs2)
                if pad > 0:
                    t_csz = Pr.run("vector", lambda e, pad=pad: (
                        e.memset(CCS[:, 0:pad], 0.0),
                        e.memset(SSS[:, 0:pad], 0.0))[-1],
                        deps=[t_cs2])
                    bCS.wrote(t_csz)

                for oc in range(NF):
                    t_w = Pr.dma("sync", WS[:, 0:1024].rearrange(
                        "p (k o) -> p k o", o=P),
                        l3qkv_bf.rearrange("(k p) o -> p k o", p=P)
                        [:, :, 2048 + oc * P:2048 + (oc + 1) * P],
                        deps=[cast_ready[id(l3qkv_bf)]] + bWS.wr())
                    t_w2 = Pr.dma("sync", WS[:, 1024:2048].rearrange(
                        "p (k o) -> p k o", o=P),
                        l3qkv_bf.rearrange("(k p) o -> p k o", p=P)
                        [:, :, 3072 + oc * P:3072 + (oc + 1) * P],
                        deps=[t_w])
                    bWS.wrote(t_w2)
                    ev_toks = []
                    for nt in range(2):
                        def mmk(e, nt=nt, kr=0):
                            last = None
                            for k in range(8):
                                last = e.matmul(
                                    PM[nt + 2 * kr][:, 0:512],
                                    WS[:, kr * 1024 + k * P:
                                       kr * 1024 + (k + 1) * P],
                                    XNQ.rearrange("p (f t) -> p f t", f=NF)
                                    [:, k, nt * 512:(nt + 1) * 512],
                                    start=(k == 0), stop=(k == 7),
                                    skip_group_check=True)
                            return last
                        for kr in range(2):
                            bank = nt + 2 * kr
                            t_mm = Pr.run("tensor",
                                          lambda e, nt=nt, kr=kr: mmk(e, nt, kr),
                                          deps=[t_w2, bXQ.w] +
                                          pm_busy[bank].wr())
                            pm_busy[bank].wrote(t_mm)
                            bWS.read(t_mm)
                            bXQ.read(t_mm)
                            dst = QF if kr == 0 else QRF
                            t_ev = Pr.run("scalar",
                                          lambda e, nt=nt, bank=bank, dst=dst:
                                          e.activation(
                                              out=dst[:, nt * 512:(nt + 1) * 512],
                                              in_=PM[bank][:, 0:512],
                                              func=AF.Copy),
                                          deps=[t_mm] +
                                          (bRP.wr() if nt == 0 and kr == 0
                                           else []))
                            pm_busy[bank].read(t_ev)
                            if nt == 0 and kr == 0:
                                bRP.wrote(t_ev)
                            else:
                                bRP.piece(t_ev)
                            ev_toks.append(t_ev)
                    t_rp = rope3(None, None, KB[:, oc * 1024:(oc + 1) * 1024],
                                 0, 1024,
                                 ev_toks + [bCS.w] +
                                 (bKB.wr() if oc == 0 else []))
                    bRP.wrote(t_rp)
                    bCS.read(t_rp)
                    if oc == 0:
                        bKB.wrote(t_rp)
                    else:
                        bKB.piece(t_rp)

                for half in range(2):
                    t_wv = Pr.dma("sync",
                                  VWS.rearrange("p (k o) -> p k o", o=512),
                                  l3qkv_bf.rearrange("(k p) o -> p k o", p=P)
                                  [:, :, 4096 + half * 512:4096 + (half + 1) * 512],
                                  deps=[cast_ready[id(l3qkv_bf)]] + bRP.wr())
                    bRP.wrote(t_wv)
                    for tch in range(8):
                        bank = tch % 2
                        def mmv(e, tch=tch, bank=bank):
                            last = None
                            for k in range(8):
                                last = e.matmul(
                                    PM[bank][:, 0:512],
                                    XNQ.rearrange("p (f t) -> p f t", f=NF)
                                    [:, k, tch * P:(tch + 1) * P],
                                    VWS[:, k * 512:(k + 1) * 512],
                                    start=(k == 0), stop=(k == 7),
                                    skip_group_check=True)
                            return last
                        t_mm = Pr.run("tensor", mmv,
                                      deps=[t_wv, bXQ.w] + pm_busy[bank].wr())
                        pm_busy[bank].wrote(t_mm)
                        bRP.read(t_mm)
                        bXQ.read(t_mm)
                        t_ev = Pr.run("scalar",
                                      lambda e, tch=tch, half=half, bank=bank:
                                      e.activation(
                                          out=VT[:, tch * 1024 + half * 512:
                                                 tch * 1024 + half * 512 + 512],
                                          in_=PM[bank][:, 0:512], func=AF.Copy),
                                      deps=[t_mm] +
                                      (bVT.wr() if tch == 0 and half == 0
                                       else []))
                        pm_busy[bank].read(t_ev)
                        if tch == 0 and half == 0:
                            bVT.wrote(t_ev)
                        else:
                            bVT.piece(t_ev)

                for ti in range(4):
                    qoff = halo + ti * P
                    for oc in range(NF):
                        t_w = Pr.dma("sync", WS[:, 0:1024].rearrange(
                            "p (k o) -> p k o", o=P),
                            l3qkv_bf.rearrange("(k p) o -> p k o", p=P)
                            [:, :, oc * P:(oc + 1) * P],
                            deps=[cast_ready[id(l3qkv_bf)]] + bWS.wr())
                        t_w2 = Pr.dma("sync", WS[:, 1024:2048].rearrange(
                            "p (k o) -> p k o", o=P),
                            l3qkv_bf.rearrange("(k p) o -> p k o", p=P)
                            [:, :, 1024 + oc * P:1024 + (oc + 1) * P],
                            deps=[t_w])
                        bWS.wrote(t_w2)
                        ev_toks = []
                        for kr in range(2):
                            bank = kr
                            def mmq(e, oc=oc, qoff=qoff, kr=kr, bank=bank):
                                last = None
                                for k in range(8):
                                    last = e.matmul(
                                        PM[bank][:, 0:P],
                                        WS[:, kr * 1024 + k * P:
                                           kr * 1024 + (k + 1) * P],
                                        XNQ.rearrange("p (f t) -> p f t", f=NF)
                                        [:, k, qoff:qoff + P],
                                        start=(k == 0), stop=(k == 7),
                                        skip_group_check=True)
                                return last
                            t_mm = Pr.run("tensor", mmq,
                                          deps=[t_w2, bXQ.w] +
                                          pm_busy[bank].wr())
                            pm_busy[bank].wrote(t_mm)
                            bWS.read(t_mm)
                            bXQ.read(t_mm)
                            dst = QF if kr == 0 else QRF
                            t_ev = Pr.run("scalar",
                                          lambda e, bank=bank, dst=dst:
                                          e.activation(out=dst[:, 0:P],
                                                       in_=PM[bank][:, 0:P],
                                                       func=AF.Copy),
                                          deps=[t_mm] + bRP.wr())
                            pm_busy[bank].read(t_ev)
                            bRP.wrote(t_ev)
                            ev_toks.append(t_ev)
                        t_rp = rope3(None, None, QB[:, oc * P:(oc + 1) * P],
                                     qoff, P,
                                     ev_toks + [bCS.w] +
                                     (bQB.wr() if oc == 0 else []))
                        bRP.wrote(t_rp)
                        bCS.read(t_rp)
                        if oc == 0:
                            bQB.wrote(t_rp)
                        else:
                            bQB.piece(t_rp)
                    for h in range(16):
                        ch = h // 2
                        base = (h % 2) * 64
                        s0 = qoff - WIN3 + P
                        for sc, (c0, cw) in enumerate(((0, 512), (512, 128))):
                            bank = sc % 2
                            def mms(e, ch=ch, base=base, s0=s0, c0=c0, cw=cw,
                                    bank=bank):
                                return e.matmul(
                                    PM[bank][:, 0:cw],
                                    QB[base:base + 64, ch * P:(ch + 1) * P],
                                    KB[base:base + 64,
                                       ch * 1024 + s0 + c0:
                                       ch * 1024 + s0 + c0 + cw],
                                    start=True, stop=True,
                                    skip_group_check=True)
                            t_mm = Pr.run("tensor", mms,
                                          deps=[bQB.w, bKB.w] +
                                          pm_busy[bank].wr())
                            pm_busy[bank].wrote(t_mm)
                            bQB.read(t_mm)
                            bKB.read(t_mm)
                            t_ms = Pr.run("vector",
                                          lambda e, c0=c0, cw=cw, bank=bank:
                                          e.scalar_tensor_tensor(
                                              out=SS[:, c0:c0 + cw],
                                              in0=PM[bank][:, 0:cw],
                                              scalar=0.125,
                                              in1=MSK3[:, c0:c0 + cw],
                                              op0=OP.mult, op1=OP.add),
                                          deps=[t_mm, t_msk3] +
                                          (bSS.wr() if sc == 0 else []))
                            pm_busy[bank].read(t_ms)
                            if sc == 0:
                                bSS.wrote(t_ms)
                            else:
                                bSS.piece(t_ms)
                        clean0 = max(0, pad - s0)
                        if clean0 > 0:
                            t_pz = Pr.run("vector",
                                          lambda e, clean0=clean0: e.memset(
                                              SS[:, 0:clean0], -1e30),
                                          deps=[bSS.w])
                            bSS.wrote(t_pz)
                        t_nm = Pr.run("vector", lambda e: e.tensor_reduce(
                            out=NM, in_=SS[:, 0:WIN3], axis=AX.X, op=OP.max,
                            negate=True), deps=[bSS.w] + bRW.wr())
                        bRW.wrote(t_nm)
                        bSS.read(t_nm)
                        t_ex = Pr.run("scalar", lambda e: e.activation(
                            out=SS[:, 0:WIN3], in_=SS[:, 0:WIN3], func=AF.Exp,
                            bias=NM, accum_out=SM), deps=[t_nm])
                        bSS.wrote(t_ex)
                        bRW.wrote(t_ex)
                        t_rs = Pr.run("vector", lambda e: e.reciprocal(
                            out=SM, in_=SM), deps=[t_ex])
                        bRW.wrote(t_rs)
                        t_pn = Pr.run("vector", lambda e: e.tensor_scalar(
                            out=PPB[:, 0:WIN3], in0=SS[:, 0:WIN3], scalar1=SM,
                            scalar2=None, op0=OP.mult),
                            deps=[t_rs] + bPP.wr())
                        bPP.wrote(t_pn)
                        bSS.read(t_pn)
                        bRW.read(t_pn)
                        def trp(e):
                            last = None
                            for j in range(5):
                                last = e.transpose(PTB[:, j * P:(j + 1) * P],
                                                   PPB[:, j * P:(j + 1) * P],
                                                   IDB[:])
                            return last
                        t_tr = Pr.run("tensor", trp,
                                      deps=[t_pn, t_idb] + ptb_busy[0].wr())
                        ptb_busy[0].wrote(t_tr)
                        bPP.read(t_tr)
                        t_pe = Pr.run("scalar", lambda e: e.activation(
                            out=PT[:, 0:WIN3], in_=PTB[:, 0:WIN3],
                            func=AF.Copy), deps=[t_tr] + bPT.wr())
                        ptb_busy[0].read(t_pe)
                        bPT.wrote(t_pe)
                        obank = 2 + ((h // 2) % 2)
                        obase = (h % 2) * 64
                        def mmo(e, h=h, s0=s0, obank=obank, obase=obase):
                            last = None
                            for j in range(5):
                                kt = (s0 + j * P) // P
                                last = e.matmul(
                                    PM[obank][obase:obase + 64, 0:P],
                                    VT[:, kt * 1024 + h * 64:
                                       kt * 1024 + h * 64 + 64],
                                    PT[:, j * P:(j + 1) * P],
                                    start=(j == 0), stop=(j == 4),
                                    skip_group_check=True)
                            return last
                        t_om = Pr.run("tensor", mmo,
                                      deps=[t_pe, bVT.w] + pm_busy[obank].wr())
                        pm_busy[obank].wrote(t_om)
                        bPT.read(t_om)
                        bVT.read(t_om)
                        t_oe = Pr.run("scalar",
                                      lambda e, h=h, ti=ti, obank=obank,
                                      obase=obase: e.activation(
                                          out=OS[obase:obase + 64,
                                                 (h // 2) * 512 + ti * P:
                                                 (h // 2) * 512 + ti * P + P],
                                          in_=PM[obank][obase:obase + 64, 0:P],
                                          func=AF.Copy),
                                      deps=[t_om] +
                                      (bOS.wr() if h == 0 and ti == 0 else []))
                        pm_busy[obank].read(t_oe)
                        if h == 0 and ti == 0:
                            bOS.wrote(t_oe)
                        else:
                            bOS.piece(t_oe)
                for oc in range(NF):
                    t_w = Pr.dma("sync", WS[:, 0:1024].rearrange(
                        "p (k o) -> p k o", o=P),
                        l3pj_bf.rearrange("(k p) o -> p k o", p=P)
                        [:, :, oc * P:(oc + 1) * P],
                        deps=[cast_ready[id(l3pj_bf)]] + bWS.wr())
                    bWS.wrote(t_w)
                    bank = oc % 2
                    def mmp(e, bank=bank):
                        last = None
                        for k in range(8):
                            last = e.matmul(PM[bank][:, 0:512],
                                            WS[:, k * P:(k + 1) * P],
                                            OS[:, k * 512:(k + 1) * 512],
                                            start=(k == 0), stop=(k == 7),
                                            skip_group_check=True)
                        return last
                    t_mm = Pr.run("tensor", mmp,
                                  deps=[t_w, bOS.w] + pm_busy[bank].wr())
                    pm_busy[bank].wrote(t_mm)
                    bWS.read(t_mm)
                    bOS.read(t_mm)
                    t_se = Pr.run("vector", lambda e, oc=oc, t0=t0, bank=bank:
                                  e.tensor_scalar(
                                      out=S[:, oc * T + t0:oc * T + t0 + 512],
                                      in0=PM[bank][:, 0:512],
                                      scalar1=BP3[:, oc:oc + 1], scalar2=None,
                                      op0=OP.add),
                                  deps=[t_mm, t_bp3] + bS.wr())
                    pm_busy[bank].read(t_se)
                    bS.piece(t_se)

            attn_join = joinb(*bufs.values())

            F_XN, F_UV, F_LG, F_FPK = 0, 2048, 4096, 6144
            F_RT, F_E1, F_RE, F_BF = 7168, 8192, 9216, 10240
            F_OHW, F_OH0, F_U13, F_XGC = 11264, 13312, 15360, 15872
            F_PKG, F_STR, F_HS, F_HRB = 15936, 16960, 19008, 19520
            F_GG, F_GFM, F_HB, F_RW = 19808, 20832, 20896, 22944
            assert F_RW + 120 <= 23100

            XN3 = pv(F_XN, NF * 512, BF16)
            LG3 = pv(F_LG, 2048)
            FPK = pv(F_FPK, 2048, BF16, 0, 32)
            RT = F_RT
            E0 = pv(F_RT, 2048, BF16, 0, 8)
            E1 = pv(F_E1, 2048, BF16, 0, 8)
            RE = pv(F_RE, 2048, BF16, 0, 64)
            BLKF = pv(F_BF, 2048, BF16, 0, 64)
            U1 = pv(F_U13, 512, BF16, 0, 64)
            U3 = pv(F_U13 + 256, 512, BF16, 0, 64)
            XGCB = pv(F_XGC, 64, BF16)
            PKG = pv(F_PKG, 1024)
            HS = pv(F_HS, 512)
            UT8 = pv(F_HS, 512, BF16, 0, 8)
            HRB = pv(F_HRB, 512, BF16)
            OTT = pv(F_HRB + 256, 64, BF16)
            GG = pv(F_GG, 1024)
            GFM = pv(F_GFM, 64)
            HB = pv(F_HB, NF * 512, BF16)
            CNT = pv(F_RW, 1, F32, 0, 8)
            SROW = pv(F_RW + 8, 8, F32, 0, 1)
            SHROW = pv(F_RW + 16, 8, F32, 0, 1)
            SREG = pv(F_RW + 24, 1, F32, 0, 8)
            SHREG = pv(F_RW + 32, 1, F32, 0, 8)
            CROW = pv(F_RW + 40, 8, F32, 0, 1)
            BIAS8 = pv(F_RW + 48, 8)
            GT3W = pv(F_RW + 56, 64, BF16)
            GS = pv(F_OHW, 512)
            MS3 = pv(F_OH0, 512)

            fb = {nm: BufT() for nm in
                  ("XN", "LG", "FPK", "RT", "E", "RE", "BF", "OH", "U", "PKG",
                   "STR", "HS", "HRB", "GG", "GFM", "HB", "RW", "GS", "MS")}
            for b in fb.values():
                for tok in attn_join:
                    b._note(tok)
            fXN = fb["XN"]; fLG = fb["LG"]; fFPK = fb["FPK"]; fRT = fb["RT"]
            fE = fb["E"]; fRE = fb["RE"]; fBF = fb["BF"]; fOH = fb["OH"]
            fU = fb["U"]; fPKG = fb["PKG"]; fSTR = fb["STR"]; fHS = fb["HS"]
            fHRB = fb["HRB"]; fGG = fb["GG"]; fGFM = fb["GFM"]; fHB = fb["HB"]
            fRW = fb["RW"]; fGS = fb["GS"]; fMS = fb["MS"]

            def rv(off, n=8):
                return pv(RT + off, n)

            t_zl = Pr.run("vector", lambda e: (
                e.memset(LG3, 0.0), e.memset(pv(RT, 256), 0.0),
                e.memset(PKG, 0.0), e.memset(GG, 0.0))[-1],
                deps=fLG.wr() + fRT.wr() + fPKG.wr() + fGG.wr())
            fLG.wrote(t_zl)
            fRT.wrote(t_zl)
            fPKG.wrote(t_zl)
            fGG.wrote(t_zl)

            t_b8 = Pr.dma("gpsimd", BIAS8[:],
                          l3_ebias[0:1, :].to_broadcast([P, 8]),
                          deps=fRW.wr())
            t_g3w = Pr.dma("gpsimd", GT3W.rearrange("p (k o) -> p k o", o=8),
                           l3_gate_T.rearrange("(k p) o -> p k o", p=P),
                           deps=[t_b8])
            fRW.wrote(t_g3w)
            fRW._note(t_b8)
            for tc in range(4):
                t0 = tc * 512
                abb = load_ab(t0, 512, t_ab)
                for f in range(NF):
                    def fn(e, f=f, t0=t0):
                        e.tensor_tensor(out=MS3,
                                        in0=X[:, f * T + t0:f * T + t0 + 512],
                                        in1=AB[:, 0:512], op=OP.mult)
                        return e.tensor_tensor(
                            out=XN3[:, f * 512:(f + 1) * 512],
                            in0=MS3, in1=AB[:, 512:1024], op=OP.add)
                    tk = Pr.run("vector", fn,
                                deps=abb + bX.rd() + fMS.wr() +
                                (fXN.wr() if f == 0 else []))
                    fMS.wrote(tk)
                    bX.read(tk)
                    bAB.read(tk)
                    if f == 0:
                        fXN.wrote(tk)
                    else:
                        fXN.piece(tk)
                def mmr(e):
                    last = None
                    for k in range(8):
                        last = e.matmul(PF0[0:8, 0:512],
                                        GT3W[:, k * 8:(k + 1) * 8],
                                        XN3[:, k * 512:(k + 1) * 512],
                                        start=(k == 0), stop=(k == 7),
                                        skip_group_check=True)
                    return last
                t_lg = Pr.run("tensor", mmr,
                              deps=[t_g3w, fXN.w] + pf_busy[0].wr())
                pf_busy[0].wrote(t_lg)
                fXN.read(t_lg)
                t_le = Pr.run("vector", lambda e, t0=t0: e.tensor_copy(
                    pv(F_LG, 2048, F32, 0, 8)[:, t0:t0 + 512],
                    PF0[0:8, 0:512]),
                    deps=[t_lg] + (fLG.wr() if tc == 0 else []))
                pf_busy[0].read(t_le)
                if tc == 0:
                    fLG.wrote(t_le)
                else:
                    fLG.piece(t_le)
                for tch in range(4):
                    col = t0 + tch * P
                    t_tr = Pr.run("tensor", lambda e, col=col: e.transpose(
                        PF1[0:P, 0:P], LG3[:, col:col + P], IDF[:]),
                        deps=[fLG.w, t_idf] + pf_busy[1].wr())
                    pf_busy[1].wrote(t_tr)
                    fLG.read(t_tr)

                    GT3 = rv(0); p_ = rv(8); m_ = rv(16, 1); z_ = rv(18, 1)
                    s8 = rv(24); gs4 = rv(32, 4); gs8 = rv(40); q_ = rv(48, 1)
                    eq1 = rv(56); gsx = rv(64); eq2 = rv(72); keep = rv(80)
                    madd = rv(88); msk = rv(96); cum = rv(104); tmp = rv(112)
                    rm1 = rv(120); rm2 = rv(128); pw1 = rv(136); pw2 = rv(144)
                    cacc = rv(152); tsum = rv(160)
                    PKT = pv(RT + 168, 32, BF16)

                    def st(fn_, deps):
                        tk = Pr.run("vector", fn_, deps=deps)
                        fRT.wrote(tk)
                        return tk

                    tk = st(lambda e: e.tensor_copy(GT3, PF1[0:P, 0:8]),
                            [t_tr] + fRT.wr())
                    pf_busy[1].read(tk)
                    tk = st(lambda e: e.tensor_reduce(
                        out=m_, in_=GT3, axis=AX.X, op=OP.max, negate=True),
                        [tk])
                    tk2 = Pr.run("scalar", lambda e: e.activation(
                        out=p_, in_=GT3, func=AF.Exp, bias=m_, accum_out=z_),
                        deps=[tk])
                    fRT.wrote(tk2)
                    tk = st(lambda e: e.reciprocal(out=z_, in_=z_), [tk2])
                    tk = st(lambda e: e.tensor_scalar(
                        out=p_, in0=p_, scalar1=z_, scalar2=None, op0=OP.mult),
                        [tk])
                    tk = st(lambda e: e.tensor_tensor(
                        out=s8, in0=p_, in1=BIAS8[:], op=OP.add),
                        [tk, t_b8])
                    tk = st(lambda e: e.tensor_reduce(
                        out=gs4, in_=s8.rearrange("p (a b) -> p a b", b=2),
                        axis=AX.X, op=OP.add), [tk])
                    tk = st(lambda e: e.tensor_tensor(
                        out=gs8.rearrange("p (a b) -> p a b", b=2),
                        in0=gs4.unsqueeze(-1).to_broadcast([P, 4, 2]),
                        in1=gs4.unsqueeze(-1).to_broadcast([P, 4, 2]),
                        op=OP.max), [tk])
                    tk = st(lambda e: e.tensor_reduce(
                        out=q_, in_=gs8, axis=AX.X, op=OP.max), [tk])
                    tk = st(lambda e: e.tensor_scalar(
                        out=eq1, in0=gs8, scalar1=q_, scalar2=None,
                        op0=OP.is_ge), [tk])
                    tk = st(lambda e: e.scalar_tensor_tensor(
                        out=gsx, in0=eq1, scalar=-1e30, in1=gs8,
                        op0=OP.mult, op1=OP.add), [tk])
                    tk = st(lambda e: e.tensor_reduce(
                        out=q_, in_=gsx, axis=AX.X, op=OP.max), [tk])
                    tk = st(lambda e: e.tensor_scalar(
                        out=eq2, in0=gsx, scalar1=q_, scalar2=None,
                        op0=OP.is_ge), [tk])
                    tk = st(lambda e: e.tensor_tensor(
                        out=keep, in0=eq1, in1=eq2, op=OP.add), [tk])
                    tk = st(lambda e: e.tensor_scalar(
                        out=madd, in0=keep, scalar1=1e30, scalar2=-1e30,
                        op0=OP.mult, op1=OP.add), [tk])
                    tk = st(lambda e: e.tensor_tensor(
                        out=msk, in0=s8, in1=madd, op=OP.add), [tk])
                    tk = st(lambda e: e.tensor_reduce(
                        out=q_, in_=msk, axis=AX.X, op=OP.max), [tk])
                    tk = st(lambda e: e.tensor_scalar(
                        out=eq1, in0=msk, scalar1=q_, scalar2=None,
                        op0=OP.is_ge), [tk])
                    tk = st(lambda e: e.tensor_tensor_scan(
                        out=cum, data0=ONES8[:, 0:8], data1=eq1, initial=0.0,
                        op0=OP.mult, op1=OP.add), [tk, t_o8])
                    tk = st(lambda e: e.tensor_scalar(
                        out=tmp, in0=cum, scalar1=-1.0, scalar2=2.0,
                        op0=OP.mult, op1=OP.add), [tk])
                    tk = st(lambda e: e.tensor_scalar(
                        out=tmp, in0=tmp, scalar1=0.0, scalar2=None,
                        op0=OP.max), [tk])
                    tk = st(lambda e: e.tensor_tensor(
                        out=rm1, in0=tmp, in1=eq1, op=OP.mult), [tk])
                    tk = st(lambda e: e.scalar_tensor_tensor(
                        out=gsx, in0=rm1, scalar=-1e30, in1=msk,
                        op0=OP.mult, op1=OP.add), [tk])
                    tk = st(lambda e: e.tensor_reduce(
                        out=q_, in_=gsx, axis=AX.X, op=OP.max), [tk])
                    tk = st(lambda e: e.tensor_scalar(
                        out=eq2, in0=gsx, scalar1=q_, scalar2=None,
                        op0=OP.is_ge), [tk])
                    tk = st(lambda e: e.tensor_tensor_scan(
                        out=cum, data0=ONES8[:, 0:8], data1=eq2, initial=0.0,
                        op0=OP.mult, op1=OP.add), [tk, t_o8])
                    tk = st(lambda e: e.tensor_scalar(
                        out=tmp, in0=cum, scalar1=-1.0, scalar2=2.0,
                        op0=OP.mult, op1=OP.add), [tk])
                    tk = st(lambda e: e.tensor_scalar(
                        out=tmp, in0=tmp, scalar1=0.0, scalar2=None,
                        op0=OP.max), [tk])
                    tk = st(lambda e: e.tensor_tensor(
                        out=rm2, in0=tmp, in1=eq2, op=OP.mult), [tk])
                    tk = st(lambda e: e.tensor_tensor(
                        out=pw1, in0=rm1, in1=p_, op=OP.mult), [tk])
                    tk = st(lambda e: e.tensor_tensor(
                        out=pw2, in0=rm2, in1=p_, op=OP.mult), [tk])
                    tk = st(lambda e: e.tensor_tensor(
                        out=tsum, in0=rm1, in1=rm2, op=OP.add), [tk])
                    tk = st(lambda e: e.tensor_tensor(
                        out=cacc, in0=cacc, in1=tsum, op=OP.add), [tk])
                    def pack(e):
                        e.tensor_copy(PKT[:, 0:8], rm1)
                        e.tensor_copy(PKT[:, 8:16], rm2)
                        e.tensor_copy(PKT[:, 16:24], pw1)
                        return e.tensor_copy(PKT[:, 24:32], pw2)
                    tk = st(pack, [tk])
                    t_tp = Pr.run("tensor", lambda e: e.transpose(
                        PTB[0:32, 0:P], PKT[:], IDB[:]),
                        deps=[tk, t_idb] + ptb_busy[0].wr())
                    ptb_busy[0].wrote(t_tp)
                    fRT.read(t_tp)
                    t_fe = Pr.run("vector", lambda e, col=col: e.tensor_copy(
                        FPK[:, col:col + P], PTB[0:32, 0:P]),
                        deps=[t_tp] + (fFPK.wr() if tc == 0 and tch == 0
                                       else []))
                    ptb_busy[0].read(t_fe)
                    if tc == 0 and tch == 0:
                        fFPK.wrote(t_fe)
                    else:
                        fFPK.piece(t_fe)

            t_cm = Pr.run("tensor", lambda e: e.matmul(
                PF0[0:8, 0:1], rv(152), ONE[:], start=True, stop=True,
                skip_group_check=True),
                deps=[fRT.w, t_one] + pf_busy[0].wr())
            pf_busy[0].wrote(t_cm)
            fRT.read(t_cm)
            t_ce = Pr.run("vector", lambda e: e.tensor_copy(
                CNT, PF0[0:8, 0:1]), deps=[t_cm] + fRW.wr())
            pf_busy[0].read(t_ce)
            fRW.wrote(t_ce)
            abb = load_ab(0, 8, t_ab)
            XG8F = pv(F_STR, 64)
            t_x8 = None
            for f in range(NF):
                t_m8 = Pr.run("vector", lambda e, f=f: e.tensor_tensor(
                    out=HS[:, 0:8], in0=X[:, f * T:f * T + 8],
                    in1=AB[:, 0:8], op=OP.mult),
                    deps=abb + bX.rd() + fHS.wr() +
                    (fSTR.wr() if f == 0 else []))
                fHS.wrote(t_m8)
                bX.read(t_m8)
                bAB.read(t_m8)
                t_x8 = Pr.run("vector", lambda e, f=f: e.tensor_tensor(
                    out=XG8F[:, f * 8:(f + 1) * 8], in0=HS[:, 0:8],
                    in1=AB[:, 512:520], op=OP.add), deps=[t_m8])
                fHS.read(t_x8)
                bAB.read(t_x8)
                if f == 0:
                    fSTR.wrote(t_x8)
                else:
                    fSTR.piece(t_x8)
            if DEBUG_L3:
                dx8 = Pr.dma("gpsimd", dbg_xg8, XG8F, deps=[fSTR.w])
                fSTR.read(dx8)
                def abcp(e):
                    e.tensor_copy(HS[:, 32:40], AB[:, 0:8])
                    return e.tensor_copy(HS[:, 40:48], AB[:, 512:520])
                tac = Pr.run("vector", abcp, deps=[fSTR.w] + fHS.wr())
                fHS.wrote(tac)
                bAB.read(tac)
                dab = Pr.dma("gpsimd", dbg_ab8, HS[:, 32:48], deps=[tac])
                fHS.read(dab)
            for k in range(8):
                t_tr = Pr.run("tensor", lambda e, k=k: e.transpose(
                    PF1[0:8, 0:P], XG8F[:, k * 8:(k + 1) * 8], IDF[:]),
                    deps=[fSTR.w, t_idf] + pf_busy[1].wr())
                pf_busy[1].wrote(t_tr)
                fSTR.read(t_tr)
                t_pe = Pr.run("vector", lambda e, k=k: e.tensor_scalar(
                    out=pv(F_PKG, 1024, F32, 0, 8)[:, k * P:(k + 1) * P],
                    in0=PF1[0:8, 0:P], scalar1=C0M[0:8, 0:1], scalar2=None,
                    op0=OP.mult),
                    deps=[t_tr, t_c0m] + (fPKG.wr() if k == 0 else []))
                pf_busy[1].read(t_pe)
                if k == 0:
                    fPKG.wrote(t_pe)
                else:
                    fPKG.piece(t_pe)
            t_d1 = Pr.dma("sync", pk3_dr[0:8, :], pv(F_PKG, 1024, F32, 0, 8),
                          deps=[fPKG.w])
            fPKG.read(t_d1)
            t_d2 = Pr.dma("sync", pk3_dr[8:9, 0:8].rearrange("a b -> b a"),
                          CNT, deps=[fRW.w])
            fRW.read(t_d2)
            t_ar1 = Pr.run("gpsimd", lambda e: e.collective_compute(
                "AllReduce", OP.add, replica_groups=[list(range(8))],
                ins=[pk3_dr.opt()], outs=[pko3_dr.opt()]),
                deps=[t_d1, t_d2])
            t_d3 = Pr.dma("sync", pv(F_PKG, 1024, F32, 0, 8), pko3_dr[0:8, :],
                          deps=[t_ar1] + fPKG.wr())
            fPKG.wrote(t_d3)
            t_d4 = Pr.dma("sync", CROW, pko3_dr[8:9, 0:8],
                          deps=[t_ar1] + fRW.wr())
            fRW.wrote(t_d4)

            t_sc = Pr.run("vector", lambda e: e.tensor_tensor_scan(
                out=SROW, data0=ONES8[0:1, 0:8], data1=CROW, initial=0.0,
                op0=OP.mult, op1=OP.add), deps=[t_d4, t_o8])
            fRW.wrote(t_sc)
            def shf(e):
                e.memset(SHROW[:, 0:1], 0.0)
                return e.tensor_copy(SHROW[:, 1:8], SROW[:, 0:7])
            t_sh = Pr.run("vector", shf, deps=[t_sc])
            fRW.wrote(t_sh)
            t_r1 = Pr.dma("sync", row_dr[2:3, 0:8], SROW, deps=[t_sh])
            t_r2 = Pr.dma("sync", row_dr[3:4, 0:8], SHROW, deps=[t_sh])
            fRW.read(t_r1)
            fRW.read(t_r2)
            t_r3 = Pr.dma("sync", SREG, row_dr[2:3, 0:8].rearrange("a b -> b a"),
                          deps=[t_r1] + fRW.wr())
            t_r4 = Pr.dma("sync", SHREG,
                          row_dr[3:4, 0:8].rearrange("a b -> b a"),
                          deps=[t_r2, t_r3])
            fRW.wrote(t_r4)
            IO = pv(F_LG, 2048, F32, 0, 8)
            UB = pv(F_UV, 2048, BF16, 0, 8)
            VB = pv(F_UV + 1024, 2048, BF16, 0, 8)
            t_io = Pr.dma("sync", IO, l3_iota[0:1, :].to_broadcast([8, T]),
                          deps=fLG.wr())
            fLG.wrote(t_io)
            e_toks = []
            for cls in range(2):
                if cls == 1:
                    t_io2 = Pr.run("vector", lambda e: e.tensor_scalar(
                        out=IO, in0=IO, scalar1=1.0, scalar2=None, op0=OP.add),
                        deps=fLG.wr())
                    fLG.wrote(t_io2)
                EDST = E0 if cls == 0 else E1
                edb = fRT if cls == 0 else fE
                t_u = Pr.run("vector", lambda e: e.tensor_scalar(
                    out=UB, in0=IO, scalar1=SHREG, scalar2=None, op0=OP.is_ge),
                    deps=[fLG.w, t_r4] + fXN.wr())
                fXN.wrote(t_u)
                fLG.read(t_u)
                fRW.read(t_u)
                t_v = Pr.run("vector", lambda e: e.tensor_scalar(
                    out=VB, in0=IO, scalar1=SREG, scalar2=None, op0=OP.is_ge),
                    deps=[t_u])
                fXN.wrote(t_v)
                fLG.read(t_v)
                t_e = Pr.run("vector", lambda e, EDST=EDST: e.tensor_tensor(
                    out=EDST, in0=UB, in1=VB, op=OP.subtract),
                    deps=[t_v] + edb.wr())
                edb.wrote(t_e)
                fXN.read(t_e)
                e_toks.append(t_e)
            t_be0 = Pr.dma("sync", fe3_dr[0:8, :], E0, deps=[e_toks[0]])
            t_be1 = Pr.dma("sync", fe3_dr[8:16, :], E1, deps=[e_toks[1]])
            fRT.read(t_be0)
            fE.read(t_be1)
            t_bf = Pr.dma("sync", ff3_dr, FPK, deps=[fFPK.w])
            fFPK.read(t_bf)
            for k in range(8):
                t_tr = Pr.run("tensor", lambda e, k=k: e.transpose(
                    PF1[0:P, 0:P], PKG[:, k * P:(k + 1) * P], IDF[:]),
                    deps=[fPKG.w, t_idf] + pf_busy[1].wr())
                pf_busy[1].wrote(t_tr)
                fPKG.read(t_tr)
                t_xc = Pr.run("vector", lambda e, k=k: e.tensor_copy(
                    XGCB[:, k * 8:(k + 1) * 8], PF1[0:P, 0:8]),
                    deps=[t_tr] + (fU.wr() if k == 0 else []))
                pf_busy[1].read(t_xc)
                if k == 0:
                    fU.wrote(t_xc)
                else:
                    fU.piece(t_xc)
            for src_bf, dst_rows in ((l3w1_bf, 0), (l3w3_bf, 64)):
                for e_ in range(8):
                    t_w = Pr.dma("sync", pv(F_STR, 4096, BF16).rearrange(
                        "p (k o) -> p k o", o=512),
                        src_bf.rearrange("(k p) o -> p k o", p=P)
                        [:, e_ * 8:(e_ + 1) * 8, :],
                        deps=[cast_ready[id(src_bf)]] + fSTR.wr())
                    fSTR.wrote(t_w)
                    def mmu(e, e_=e_):
                        last = None
                        for k in range(8):
                            last = e.matmul(
                                PF0[0:8, 0:512], XGCB[:, k * 8:(k + 1) * 8],
                                pv(F_STR, 4096, BF16)[:, k * 512:(k + 1) * 512],
                                start=(k == 0), stop=(k == 7),
                                skip_group_check=True)
                        return last
                    t_mm = Pr.run("tensor", mmu,
                                  deps=[t_w, fU.w] + pf_busy[0].wr())
                    pf_busy[0].wrote(t_mm)
                    fSTR.read(t_mm)
                    fU.read(t_mm)
                    t_ue = Pr.run("vector", lambda e: e.tensor_copy(
                        UT8, PF0[0:8, 0:512]), deps=[t_mm] + fHS.wr())
                    pf_busy[0].read(t_ue)
                    fHS.wrote(t_ue)
                    t_ud = Pr.dma("sync",
                                  u13_dr[dst_rows + e_ * 8:
                                         dst_rows + (e_ + 1) * 8, :],
                                  UT8, deps=[t_ue])
                    fHS.read(t_ud)
            t_u1 = Pr.dma("sync", U1, u13_dr[0:64, :],
                          deps=fHS.wr() + fU.wr())
            t_u3 = Pr.dma("sync", U3, u13_dr[64:128, :], deps=[t_u1])
            fU.wrote(t_u3)
            scat_first = True
            scat_last = None
            for cls in range(2):
                fsrc = fe3_dr[cls * 8:(cls + 1) * 8, :]
                def reload(dst, src_rows, bcast, deps):
                    toks = []
                    for g in range(8):
                        if bcast:
                            sl = fsrc[g:g + 1, :].to_broadcast([8, T])
                        else:
                            sl = src_rows
                        toks.append(Pr.dma(
                            "sync", pv(dst, 2048, BF16, 0, 64)
                            [g * 8:(g + 1) * 8, :], sl,
                            deps=(deps if g == 0 else [toks[-1]])))
                    return toks
                tre = reload(F_RE, None, True, [t_be0, t_be1] + fRE.wr())
                fRE.wrote(tre[-1])
                for wsel, woff in ((2, 0), (0, 1024)):
                    tbf = []
                    for g in range(8):
                        tbf.append(Pr.dma(
                            "sync", BLKF[g * 8:(g + 1) * 8, :],
                            ff3_dr[wsel * 8 + cls * 8:
                                   wsel * 8 + cls * 8 + 8, :],
                            deps=([t_bf] + fBF.wr()) if g == 0 else [tbf[-1]]))
                    fBF.wrote(tbf[-1])
                    dsto = (F_OHW if wsel == 2 else F_OH0) + cls * 1024
                    t_oh = Pr.run("vector", lambda e, dsto=dsto: e.tensor_tensor(
                        out=pv(dsto, 2048, BF16, 0, 64), in0=RE, in1=BLKF,
                        op=OP.mult),
                        deps=[fRE.w, tbf[-1]] + (fOH.wr()
                                                 if cls == 0 and wsel == 2
                                                 else []))
                    if cls == 0 and wsel == 2:
                        fOH.wrote(t_oh)
                    else:
                        fOH.piece(t_oh)
                    fRE.read(t_oh)
                    fBF.read(t_oh)
            for cls in range(2):
                for j in range(16):
                    OHW = pv(F_OHW + cls * 1024, 2048, BF16, 0, 64)
                    OH0 = pv(F_OH0 + cls * 1024, 2048, BF16, 0, 64)
                    t_g1 = Pr.run("tensor", lambda e, OHW=OHW, j=j: e.matmul(
                        PF0[:, 0:512], OHW[:, j * P:(j + 1) * P], U1,
                        start=True, stop=True, skip_group_check=True),
                        deps=[fOH.w, fU.w] + pf_busy[0].wr())
                    pf_busy[0].wrote(t_g1)
                    fOH.read(t_g1)
                    fU.read(t_g1)
                    t_g3 = Pr.run("tensor", lambda e, OHW=OHW, j=j: e.matmul(
                        PF1[:, 0:512], OHW[:, j * P:(j + 1) * P], U3,
                        start=True, stop=True, skip_group_check=True),
                        deps=[fOH.w, fU.w] + pf_busy[1].wr())
                    pf_busy[1].wrote(t_g3)
                    t_si = Pr.run("scalar", lambda e: e.activation(
                        out=HS, in_=PF0[:, 0:512], func=AF.Silu),
                        deps=[t_g1] + fHS.wr())
                    fHS.wrote(t_si)
                    pf_busy[0].read(t_si)
                    t_hr = Pr.run("vector", lambda e: e.tensor_tensor(
                        out=HRB, in0=HS, in1=PF1[:, 0:512], op=OP.mult),
                        deps=[t_si, t_g3] + fHRB.wr())
                    fHRB.wrote(t_hr)
                    fHS.read(t_hr)
                    pf_busy[1].read(t_hr)
                    t_tt = Pr.run("tensor", lambda e, OH0=OH0, j=j: e.transpose(
                        PTB[0:P, 0:64], OH0[:, j * P:(j + 1) * P],
                        IDB[0:64, 0:64]),
                        deps=[fOH.w, t_idb] + ptb_busy[0].wr())
                    ptb_busy[0].wrote(t_tt)
                    fOH.read(t_tt)
                    t_ot = Pr.run("vector", lambda e: e.tensor_copy(
                        OTT, PTB[0:P, 0:64]), deps=[t_tt] + fHRB.wr())
                    ptb_busy[0].read(t_ot)
                    fHRB.wrote(t_ot)
                    t_sc2 = Pr.run("tensor", lambda e, cls=cls, j=j: e.matmul(
                        PM[3][0:64, 0:512], OTT, HRB,
                        start=(cls == 0 and j == 0),
                        stop=(cls == 1 and j == 15),
                        skip_group_check=True),
                        deps=[t_ot, t_hr] +
                        (pm_busy[3].wr() if cls == 0 and j == 0 else []))
                    pm_busy[3].wrote(t_sc2)
                    fHRB.read(t_sc2)
                    scat_last = t_sc2
            HB64 = pv(F_RE, 512, BF16, 0, 64)
            t_he = Pr.run("vector", lambda e: e.tensor_copy(
                HB64, PM[3][0:64, 0:512]), deps=[scat_last] + fRE.wr())
            pm_busy[3].read(t_he)
            fRE.wrote(t_he)
            HTB = pv(F_BF, 512, BF16)
            for k in range(4):
                t_tr = Pr.run("tensor", lambda e, k=k: e.transpose(
                    PTB[0:P, 0:64], HB64[:, k * P:(k + 1) * P],
                    IDB[0:64, 0:64]),
                    deps=[fRE.w, t_idb] + ptb_busy[0].wr())
                ptb_busy[0].wrote(t_tr)
                fRE.read(t_tr)
                t_hv = Pr.run("vector", lambda e, k=k: e.tensor_copy(
                    HTB[:, k * 64:(k + 1) * 64], PTB[0:P, 0:64]),
                    deps=[t_tr] + (fBF.wr() if k == 0 else []))
                ptb_busy[0].read(t_hv)
                if k == 0:
                    fBF.wrote(t_hv)
                else:
                    fBF.piece(t_hv)
            for half in range(2):
                mm_last = None
                for e_ in range(8):
                    t_w = Pr.dma("sync", pv(F_STR, 2048, BF16).rearrange(
                        "p (k o) -> p k o", o=512),
                        l3w2_bf.rearrange("(k p) o -> p k o", p=P)
                        [:, e_ * 4:(e_ + 1) * 4,
                         half * 512:(half + 1) * 512],
                        deps=[cast_ready[id(l3w2_bf)]] + fSTR.wr())
                    fSTR.wrote(t_w)
                    def mmg(e, e_=e_):
                        last = None
                        for k in range(4):
                            last = e.matmul(
                                PF0[0:8, 0:512],
                                HTB[:, k * 64 + e_ * 8:k * 64 + e_ * 8 + 8],
                                pv(F_STR, 2048, BF16)[:, k * 512:(k + 1) * 512],
                                start=(e_ == 0 and k == 0),
                                stop=(e_ == 7 and k == 3),
                                skip_group_check=True)
                        return last
                    mm_last = Pr.run("tensor", mmg,
                                     deps=[t_w, fBF.w] +
                                     (pf_busy[0].wr() if e_ == 0 else []))
                    pf_busy[0].wrote(mm_last)
                    fSTR.read(mm_last)
                    fBF.read(mm_last)
                t_ge = Pr.run("vector", lambda e, half=half: e.tensor_copy(
                    pv(F_GG, 1024, F32, 0, 8)[:, half * 512:(half + 1) * 512],
                    PF0[0:8, 0:512]),
                    deps=[mm_last] + (fGG.wr() if half == 0 else []))
                pf_busy[0].read(t_ge)
                if half == 0:
                    fGG.wrote(t_ge)
                else:
                    fGG.piece(t_ge)
            t_gd = Pr.dma("sync", g3_dr, pv(F_GG, 1024, F32, 0, 8),
                          deps=[fGG.w])
            fGG.read(t_gd)
            t_ar2 = Pr.run("gpsimd", lambda e: e.collective_compute(
                "AllReduce", OP.add, replica_groups=[list(range(8))],
                ins=[g3_dr.opt()], outs=[go3_dr.opt()]), deps=[t_gd])
            t_gb = Pr.dma("sync", pv(F_GG, 1024, F32, 0, 8), go3_dr,
                          deps=[t_ar2] + fGG.wr())
            fGG.wrote(t_gb)
            for k in range(8):
                t_tr = Pr.run("tensor", lambda e, k=k: e.transpose(
                    PF1[0:P, 0:P], GG[:, k * P:(k + 1) * P], IDF[:]),
                    deps=[fGG.w, t_idf] + pf_busy[1].wr())
                pf_busy[1].wrote(t_tr)
                fGG.read(t_tr)
                t_gf = Pr.run("vector", lambda e, k=k: e.tensor_scalar(
                    out=GFM[:, k * 8:(k + 1) * 8], in0=PF1[0:P, 0:8],
                    scalar1=C0M[:, 0:1], scalar2=None, op0=OP.mult),
                    deps=[t_tr, t_c0m] + (fGFM.wr() if k == 0 else []))
                pf_busy[1].read(t_gf)
                if k == 0:
                    fGFM.wrote(t_gf)
                else:
                    fGFM.piece(t_gf)

            ph2 = joinb(fOH, fRE, fBF, fU, fHRB, fHS, fSTR, fGG)
            for b in (fGS, fMS, fHB):
                for tok in ph2:
                    b._note(tok)
            for tc in range(4):
                t0 = tc * 512
                abb = load_ab(t0, 512, t_ab)
                for f in range(NF):
                    def fn(e, f=f, t0=t0):
                        e.tensor_tensor(out=MS3,
                                        in0=X[:, f * T + t0:f * T + t0 + 512],
                                        in1=AB[:, 0:512], op=OP.mult)
                        return e.tensor_tensor(
                            out=XN3[:, f * 512:(f + 1) * 512],
                            in0=MS3, in1=AB[:, 512:1024], op=OP.add)
                    tk = Pr.run("vector", fn,
                                deps=abb + bX.rd() + fMS.wr() +
                                (fXN.wr() if f == 0 else []))
                    fMS.wrote(tk)
                    bX.read(tk)
                    bAB.read(tk)
                    if f == 0:
                        fXN.wrote(tk)
                    else:
                        fXN.piece(tk)
                for oc in range(NF):
                    t_w1s = Pr.dma("sync", pv(F_STR, 1024, BF16).rearrange(
                        "p (k o) -> p k o", o=P),
                        l3sw1_bf.rearrange("(k p) o -> p k o", p=P)
                        [:, :, oc * P:(oc + 1) * P],
                        deps=[cast_ready[id(l3sw1_bf)]] + fSTR.wr())
                    t_w3s = Pr.dma("sync",
                                   pv(F_STR + 512, 1024, BF16).rearrange(
                                       "p (k o) -> p k o", o=P),
                                   l3sw3_bf.rearrange("(k p) o -> p k o", p=P)
                                   [:, :, oc * P:(oc + 1) * P],
                                   deps=[cast_ready[id(l3sw3_bf)], t_w1s])
                    fSTR.wrote(t_w3s)
                    def mm1(e, off=0, bank=0):
                        last = None
                        for k in range(8):
                            last = e.matmul(
                                PM[bank][:, 0:512],
                                pv(F_STR + off, 1024, BF16)
                                [:, k * P:(k + 1) * P],
                                XN3[:, k * 512:(k + 1) * 512],
                                start=(k == 0), stop=(k == 7),
                                skip_group_check=True)
                        return last
                    t_m1 = Pr.run("tensor", lambda e: mm1(e, 0, 0),
                                  deps=[t_w3s, fXN.w] + pm_busy[0].wr())
                    pm_busy[0].wrote(t_m1)
                    fSTR.read(t_m1)
                    fXN.read(t_m1)
                    t_m3 = Pr.run("tensor", lambda e: mm1(e, 512, 1),
                                  deps=[t_w3s, fXN.w] + pm_busy[1].wr())
                    pm_busy[1].wrote(t_m3)
                    fSTR.read(t_m3)
                    t_sl = Pr.run("scalar", lambda e: e.activation(
                        out=HS, in_=PM[0][:, 0:512], func=AF.Silu),
                        deps=[t_m1] + fHS.wr())
                    fHS.wrote(t_sl)
                    pm_busy[0].read(t_sl)
                    t_hb = Pr.run("vector", lambda e, oc=oc: e.tensor_tensor(
                        out=HB[:, oc * 512:(oc + 1) * 512], in0=HS,
                        in1=PM[1][:, 0:512], op=OP.mult),
                        deps=[t_sl, t_m3] + (fHB.wr() if oc == 0 else []))
                    fHS.read(t_hb)
                    pm_busy[1].read(t_hb)
                    if oc == 0:
                        fHB.wrote(t_hb)
                    else:
                        fHB.piece(t_hb)
                for oc in range(NF):
                    t_w = Pr.dma("sync", pv(F_STR, 1024, BF16).rearrange(
                        "p (k o) -> p k o", o=P),
                        l3sw2_bf.rearrange("(k p) o -> p k o", p=P)
                        [:, :, oc * P:(oc + 1) * P],
                        deps=[cast_ready[id(l3sw2_bf)]] + fSTR.wr())
                    fSTR.wrote(t_w)
                    bank = 2 + oc % 2
                    def mmd(e, bank=bank):
                        last = None
                        for k in range(8):
                            last = e.matmul(
                                PM[bank][:, 0:512],
                                pv(F_STR, 1024, BF16)[:, k * P:(k + 1) * P],
                                HB[:, k * 512:(k + 1) * 512],
                                start=(k == 0), stop=(k == 7),
                                skip_group_check=True)
                        return last
                    t_mm = Pr.run("tensor", mmd,
                                  deps=[t_w, fHB.w] + pm_busy[bank].wr())
                    pm_busy[bank].wrote(t_mm)
                    fSTR.read(t_mm)
                    fHB.read(t_mm)
                    if tc == 0:
                        t_cp = Pr.run("vector", lambda e, bank=bank:
                                      e.tensor_copy(GS, PM[bank][:, 0:512]),
                                      deps=[t_mm] + fGS.wr())
                        fGS.wrote(t_cp)
                        pm_busy[bank].read(t_cp)
                        t_ad = Pr.run("vector", lambda e, oc=oc:
                                      e.tensor_tensor(
                                          out=GS[:, 0:8], in0=GS[:, 0:8],
                                          in1=GFM[:, oc * 8:(oc + 1) * 8],
                                          op=OP.add),
                                      deps=[t_cp, fGFM.w])
                        fGS.wrote(t_ad)
                        fGFM.read(t_ad)
                        t_gs = Pr.run("scalar", lambda e: e.activation(
                            out=MS3, in_=GS, func=AF.Silu),
                            deps=[t_ad] + fMS.wr())
                        fGS.read(t_gs)
                    else:
                        t_gs = Pr.run("scalar", lambda e, bank=bank:
                                      e.activation(
                                          out=MS3, in_=PM[bank][:, 0:512],
                                          func=AF.Silu),
                                      deps=[t_mm] + fMS.wr())
                        pm_busy[bank].read(t_gs)
                    fMS.wrote(t_gs)
                    def upd(e, oc=oc, t0=t0):
                        e.tensor_tensor(out=MS3, in0=MS3,
                                        in1=S[:, oc * T + t0:oc * T + t0 + 512],
                                        op=OP.mult)
                        return e.tensor_tensor(
                            out=X[:, oc * T + t0:oc * T + t0 + 512],
                            in0=X[:, oc * T + t0:oc * T + t0 + 512],
                            in1=MS3, op=OP.add)
                    t_up = Pr.run("vector", upd, deps=[t_gs, bS.w] + bX.rd())
                    fMS.wrote(t_up)
                    bS.read(t_up)
                    bX.piece(t_up)
            if DEBUG_L3:
                d1 = Pr.dma("gpsimd", dbg_g3, pv(F_GG, 1024, F32, 0, 8),
                            deps=[fGG.w])
                d2 = Pr.dma("gpsimd", dbg_cnt, CROW, deps=[fRW.w, d1])
                d3 = Pr.dma("gpsimd", dbg_fpk, FPK, deps=[fFPK.w, d2])
                d4 = Pr.dma("gpsimd", dbg_e3[0:8, :], E0, deps=[fRT.w, d3])
                d5 = Pr.dma("gpsimd", dbg_e3[8:16, :], E1, deps=[fE.w, d4])
                d6 = Pr.dma("gpsimd", dbg_u13[0:64, :], U1, deps=[fU.w, d5])
                d7 = Pr.dma("gpsimd", dbg_u13[64:128, :], U3, deps=[d6])
                d8 = Pr.dma("gpsimd", dbg_pkg, pv(F_PKG, 1024, F32, 0, 8),
                            deps=[fPKG.w, d7])
                d9 = Pr.dma("gpsimd", dbg_xgc, XGCB, deps=[fU.w, d8])
                fPKG.read(d8)
                fU.read(d9)
                fGG.read(d1)
                fRW.read(d2)
                fFPK.read(d3)
                fRT.read(d4)
                fE.read(d5)
                fU.read(d7)
            return joinb(*fb.values())

        if n_layers >= 1:
            l0j = layer0()
        if n_layers >= 2:
            l1j = layer1(l0j)
        if n_layers >= 3:
            l2j = layer2(l1j)
        if n_layers >= 4:
            layer3(l2j)

        last = None
        for f in range(NF):
            last = Pr.dma("sync", y_out[f * P:(f + 1) * P, :],
                          X[:, f * T:(f + 1) * T], deps=bX.wr())
        if DEBUG_S:
            for f in range(NF):
                last = Pr.dma("sync", dbg_s[f * P:(f + 1) * P, :],
                              S[:, f * T:(f + 1) * T], deps=bS.rd())
        if DEBUG_DUMP:
            for r in range(NF):
                tq1 = Pr.dma("sync", pv(A_WIB, FI, BF16), wqg_dr[r * P:(r + 1) * P, :],
                             deps=[last])
                last = Pr.dma("sync", dbg_wqg[r * P:(r + 1) * P, :], pv(A_WIB, FI, BF16),
                              deps=[tq1])
        Pr.final_wait("sync", last)

        Pr.replay(nc, esems, dsems)
    return nc


_NC_CACHE = {}


def kernel(**inputs):
    n_layers = N_LAYERS
    if n_layers not in _NC_CACHE:
        _NC_CACHE[n_layers] = build_program(n_layers)
    nc = _NC_CACHE[n_layers]

    f32 = np.float32
    x = np.asarray(inputs["L0_attn_x"], f32)

    common = {
        "c_idf32": np.eye(P, dtype=f32),
        "c_idbf": np.eye(P, dtype=ml_dtypes.bfloat16),
        "c_ones": np.ones((P, 1), f32),
        "l0_wi_T": np.ascontiguousarray(np.asarray(inputs["L0_attn_W_in"], f32).T),
        "l0_wo_T": np.ascontiguousarray(np.asarray(inputs["L0_attn_W_out"], f32).T),
        "l0_wx_T": np.ascontiguousarray(np.asarray(inputs["L0_attn_W_x"], f32).T
                                        .reshape(NDI, P, 96).transpose(1, 0, 2)
                                        .reshape(P, NDI * 96)),
        "l0_wdt_T": np.ascontiguousarray(np.asarray(inputs["L0_attn_W_dt"], f32).T),
        "l0_convw": np.ascontiguousarray(
            np.asarray(inputs["L0_attn_conv_w"], f32).reshape(NDI, P, 4)
            .transpose(1, 0, 2).reshape(P, NDI * 4)),
        "l0_convb": np.ascontiguousarray(
            np.asarray(inputs["L0_attn_conv_b"], f32).reshape(NDI, P).T),
        "l0_bdt": np.ascontiguousarray(
            np.asarray(inputs["L0_attn_b_dt"], f32).reshape(NDI, P).T),
        "l0_alog": np.ascontiguousarray(
            np.asarray(inputs["L0_attn_A_log"], f32).reshape(NDI, P, DST)
            .transpose(1, 0, 2).reshape(P, NDI * DST)),
        "l0_dvec": np.ascontiguousarray(
            np.asarray(inputs["L0_attn_D"], f32).reshape(NDI, P).T),
        "l0_wg": np.ascontiguousarray(np.asarray(inputs["L0_ffn_w_gate"], f32)),
        "l0_wu": np.ascontiguousarray(np.asarray(inputs["L0_ffn_w_up"], f32)),
        "l0_wd": np.ascontiguousarray(np.asarray(inputs["L0_ffn_w_down"], f32)),
    }

    common.update({
        "l1_qkv_T": np.ascontiguousarray(np.asarray(inputs["L1_attn_qkv_w"], f32).T),
        "l1_oproj_T": np.ascontiguousarray(np.asarray(inputs["L1_attn_out_proj_w"], f32).T),
        "l1_ogate_T": np.ascontiguousarray(np.asarray(inputs["L1_attn_out_gate_w"], f32).T),
        "l1_ogate_b": np.ascontiguousarray(
            np.asarray(inputs["L1_attn_out_gate_b"], f32).reshape(NF, P).T),
        "l1_decay": np.ascontiguousarray(
            np.asarray(inputs["L1_attn_decay_param"], f32).reshape(1, 8)),
        "l1_convw": np.ascontiguousarray(
            np.asarray(inputs["L1_attn_conv_w"], f32).reshape(NF, P, 4)
            .transpose(1, 0, 2).reshape(P, NF * 4)),
        "l1_convb": np.ascontiguousarray(
            np.asarray(inputs["L1_attn_conv_b"], f32).reshape(NF, P).T),
        "l1_pw_T": np.ascontiguousarray(np.asarray(inputs["L1_attn_pw_w"], f32).T),
        "l1_gate_T": np.ascontiguousarray(np.asarray(inputs["L1_attn_gate_w"], f32).T),
        "l1_w1_T": np.ascontiguousarray(np.asarray(inputs["L1_attn_w1"], f32).T),
        "l1_w2_T": np.ascontiguousarray(np.asarray(inputs["L1_attn_w2"], f32).T),
        "l1_w3_T": np.ascontiguousarray(np.asarray(inputs["L1_attn_w3"], f32).T),
        "l1_n1": np.ascontiguousarray(
            np.asarray(inputs["L1_attn_norm1_w"], f32).reshape(NF, P).T),
        "l1_wr_T": np.ascontiguousarray(np.asarray(inputs["L1_ffn_W_r"], f32).T),
        "l1_wk_T": np.ascontiguousarray(np.asarray(inputs["L1_ffn_W_k"], f32).T),
        "l1_wv_T": np.ascontiguousarray(np.asarray(inputs["L1_ffn_W_v"], f32).T),
        "l1_mur": np.ascontiguousarray(
            np.asarray(inputs["L1_ffn_mu_r"], f32).reshape(NF, P).T),
        "l1_muk": np.ascontiguousarray(
            np.asarray(inputs["L1_ffn_mu_k"], f32).reshape(NF, P).T),
        "l1_n2": np.ascontiguousarray(
            np.asarray(inputs["L1_attn_norm2_w"], f32).reshape(NF, P).T),
    })

    qks = np.asarray(inputs["L2_attn_Wqk_short"], f32)
    qpad = np.zeros((DM, 1024), f32)
    for h in range(8):
        qpad[:, h * 64:h * 64 + 32] = qks[:, h * 32:(h + 1) * 32]
        qpad[:, 512 + h * 64:512 + h * 64 + 32] = \
            qks[:, 256 + h * 32:256 + (h + 1) * 32]
    ii = np.arange(P)[:, None]
    dd = np.arange(384)[None, :] - ii
    msk_s = np.where((dd >= 1) & (dd <= 256), 0.0, -1e30).astype(f32)
    dd = np.arange(1152)[None, :] - ii
    msk_l = np.where((dd >= 1) & (dd <= 1024), 0.0, -1e30).astype(f32)
    common.update({
        "l2_qks": np.ascontiguousarray(qpad),
        "l2_vs": np.ascontiguousarray(np.asarray(inputs["L2_attn_Wv_short"], f32)),
        "l2_qkl": np.ascontiguousarray(np.asarray(inputs["L2_attn_Wqk_long"], f32)),
        "l2_vl": np.ascontiguousarray(np.asarray(inputs["L2_attn_Wv_long"], f32)),
        "l2_proj": np.ascontiguousarray(np.asarray(inputs["L2_attn_Wproj"], f32)),
        "l2_rout_T": np.ascontiguousarray(
            np.asarray(inputs["L2_ffn_w_router"], f32).T),
        "l2_rout_b": np.ascontiguousarray(
            np.asarray(inputs["L2_ffn_b_router"], f32).reshape(8, 1)),
        "l2_wexp_T": np.ascontiguousarray(
            np.asarray(inputs["L2_ffn_w_experts"], f32)
            .transpose(0, 2, 1).reshape(8 * DM, DM)),
        "l2_bexp_T": np.ascontiguousarray(
            np.asarray(inputs["L2_ffn_b_experts"], f32)),
        "l2_mask_s": msk_s,
        "l2_mask_l": msk_l,
    })

    Wqkv = np.asarray(inputs["L3_attn_Wqkv"], f32)
    perm = np.empty(1024, np.int64)
    for h in range(16):
        perm[h * 64:h * 64 + 32] = h * 64 + 2 * np.arange(32)
        perm[h * 64 + 32:h * 64 + 64] = h * 64 + 2 * np.arange(32) + 1
    Wq = Wqkv[0:1024][perm]
    Wk = Wqkv[1024:2048][perm]
    Wv = Wqkv[2048:3072]
    Wqr = np.empty_like(Wq)
    Wkr = np.empty_like(Wk)
    for h in range(16):
        eix = slice(h * 64, h * 64 + 32)
        oix = slice(h * 64 + 32, h * 64 + 64)
        Wqr[eix] = -Wq[oix]
        Wqr[oix] = Wq[eix]
        Wkr[eix] = -Wk[oix]
        Wkr[oix] = Wk[eix]
    qkvT = np.concatenate([Wq, Wqr, Wk, Wkr, Wv], axis=0).T
    inv_freq = 1.0 / 10000.0 ** (np.arange(32, dtype=np.float64) / 32.0)
    freqs = np.arange(T)[:, None] * inv_freq[None, :]
    ccp = np.tile(np.cos(freqs).T, (4, 1)).astype(f32)
    ssp = np.tile(np.sin(freqs).T, (4, 1)).astype(f32)
    dd = np.arange(640)[None, :] - np.arange(P)[:, None]
    msk3 = np.where((dd >= 0) & (dd <= 512), 0.0, -1e30).astype(f32)
    common.update({
        "l3_qkv_T": np.ascontiguousarray(qkvT),
        "l3_proj_T": np.ascontiguousarray(
            np.asarray(inputs["L3_attn_Wproj"], f32).T),
        "l3_bproj": np.ascontiguousarray(
            np.asarray(inputs["L3_attn_bproj"], f32).reshape(NF, P).T),
        "l3_cc": ccp,
        "l3_ssn": ssp,
        "l3_mask": msk3,
        "l3_gate_T": np.ascontiguousarray(
            np.asarray(inputs["L3_ffn_w_gate"], f32).T),
        "l3_ebias": np.ascontiguousarray(
            np.asarray(inputs["L3_ffn_expert_bias"], f32).reshape(1, 8)),
        "l3_sw1_T": np.ascontiguousarray(np.asarray(inputs["L3_ffn_sw1"], f32).T),
        "l3_sw3_T": np.ascontiguousarray(np.asarray(inputs["L3_ffn_sw3"], f32).T),
        "l3_sw2_T": np.ascontiguousarray(np.asarray(inputs["L3_ffn_sw2"], f32).T),
        "l3_w1": np.ascontiguousarray(
            np.asarray(inputs["L3_ffn_w1"], f32).reshape(8 * DM, 512)),
        "l3_w3": np.ascontiguousarray(
            np.asarray(inputs["L3_ffn_w3"], f32).reshape(8 * DM, 512)),
        "l3_w2": np.ascontiguousarray(
            np.asarray(inputs["L3_ffn_w2"], f32).reshape(8 * 512, DM)),
    })

    in_maps = []
    for b in range(8):
        m = dict(common)
        m["x"] = np.ascontiguousarray(x[b].T)
        m["l3_iota"] = (2.0 * (np.arange(T, dtype=np.float64) + b * T)) \
            .astype(f32).reshape(1, T)
        m["l3_c0"] = np.full((P, 1), 1.0 if b == 0 else 0.0, f32)
        in_maps.append(m)

    res = run_bass_kernel_spmd(nc, in_maps, core_ids=list(range(8)))
    global _LAST_RES
    _LAST_RES = res
    out = np.stack([np.asarray(r["y"]).T for r in res.results], axis=0)
    return out.astype(np.float32)


# version 04
# speedup vs baseline: 1.0121x; 1.0121x over previous
import contextlib
import numpy as np
import ml_dtypes
import concourse.bass as bass
import concourse.mybir as mybir
from concourse.bass_utils import run_bass_kernel_spmd

F32 = mybir.dt.float32
BF16 = mybir.dt.bfloat16
I32 = mybir.dt.int32
AF = mybir.ActivationFunctionType
OP = mybir.AluOpType
AX = mybir.AxisListType

P = 128
T = 2048
DM = 1024
NF = DM // P
N_LAYERS = 4
DEBUG_DUMP = False
DEBUG_L1 = False
DEBUG_S = False
DEBUG_L3 = False

DI = 2048
NDI = DI // P
DST = 16
DTR = 64
TC0 = 256
NTC0 = T // TC0
FI = 2816
NFI = FI // P

MASK_HI = int(np.int32(np.uint32(0xFFC00000).view(np.int32)))
ADD_LO = (1 << 21) - 1

ENGS = ("sync", "gpsimd", "vector", "scalar", "tensor")


class Prog:
    def __init__(self):
        self.items = {e: [] for e in ENGS}
        self.tick = {e: 0 for e in ENGS}
        self.dmacnt = {e: 0 for e in ENGS}
        self.hw = {e: {} for e in ENGS}

    def _wait(self, eng, tok):
        kind, src, val = tok
        key = (kind, src)
        if self.hw[eng].get(key, 0) >= val:
            return
        self.hw[eng][key] = val
        self.items[eng].append(("w", key, val))

    def run(self, eng, fn, deps=()):
        for d in deps:
            if d is not None:
                self._wait(eng, d)
        self.items[eng].append(("x", fn))
        self.tick[eng] += 1
        return ("e", eng, self.tick[eng])

    def dma(self, q, out, in_, deps=()):
        for d in deps:
            if d is not None:
                self._wait(q, d)
        self.items[q].append(("dma", out, in_))
        self.dmacnt[q] += 16
        return ("d", q, self.dmacnt[q])

    def final_wait(self, eng, tok):
        kind, src, val = tok
        self.items[eng].append(("w", (kind, src), val))

    def replay(self, nc, esems, dsems):
        with nc.Block() as block:
            for ename in ENGS:
                items = self.items[ename]
                esem = esems[ename]
                dsem = dsems.get(ename)

                def body(eng, items=items, esem=esem, dsem=dsem):
                    for it in items:
                        if it[0] == "w":
                            _, key, val = it
                            sem = esems[key[1]] if key[0] == "e" else dsems[key[1]]
                            eng.wait_ge(sem, val)
                        elif it[0] == "x":
                            inst = it[1](eng)
                            inst.then_inc(esem, 1)
                        else:
                            _, out, in_ = it
                            eng.dma_start(out=out, in_=in_).then_inc(dsem, 16)

                getattr(block, ename)(body)


class BufT:

    def __init__(self):
        self.w = None
        self.u = {}

    def rd(self):
        return [self.w] if self.w is not None else []

    def wr(self):
        return list(self.u.values())

    def wrote(self, tok):
        self.w = tok
        self.u = {}
        self._note(tok)

    def piece(self, tok):
        self.w = tok
        self._note(tok)

    def read(self, tok):
        self._note(tok)

    def _note(self, tok):
        key = (tok[0], tok[1])
        old = self.u.get(key)
        if old is None or tok[2] > old[2]:
            self.u[key] = tok


def joinb(*bufs):
    out = []
    for b in bufs:
        out += b.wr()
    return out


def build_program(n_layers=N_LAYERS):
    nc = bass.Bass("TRN2", target_bir_lowering=False)
    Pr = Prog()
    st = contextlib.ExitStack()

    def din(name, shape, dtype=F32):
        return nc.dram_tensor(name, list(shape), dtype, kind="ExternalInput").ap()

    def dout(name, shape, dtype=F32):
        return nc.dram_tensor(name, list(shape), dtype, kind="ExternalOutput").ap()

    def dtmp(name, shape, dtype=F32):
        return nc.dram_tensor(name, list(shape), dtype).ap()

    def sbuf(name, shape, dtype=F32):
        return st.enter_context(nc.sbuf_tensor(name, list(shape), dtype))

    x_in = din("x", [DM, T])
    y_out = dout("y", [DM, T])
    dbg_s = dout("dbg_s", [DM, T], BF16) if DEBUG_S else None
    dbg_xs = dout("dbg_xs", [DI, T], BF16) if DEBUG_L1 else None
    dbg_yz = dout("dbg_yz", [DI, T], BF16) if DEBUG_L1 else None
    dbg_dt = dout("dbg_dt", [DI, T]) if DEBUG_L1 else None
    dbg_bc = dout("dbg_bc", [32, T], BF16) if DEBUG_L1 else None
    dbg_g3 = dout("dbg_g3", [8, DM]) if DEBUG_L3 else None
    dbg_cnt = dout("dbg_cnt", [1, 8]) if DEBUG_L3 else None
    dbg_fpk = dout("dbg_fpk", [32, T], BF16) if DEBUG_L3 else None
    dbg_e3 = dout("dbg_e3", [16, T], BF16) if DEBUG_L3 else None
    dbg_u13 = dout("dbg_u13", [P, 512], BF16) if DEBUG_L3 else None
    dbg_pkg = dout("dbg_pkg", [8, DM]) if DEBUG_L3 else None
    dbg_xgc = dout("dbg_xgc", [P, 64], BF16) if DEBUG_L3 else None
    dbg_xg8 = dout("dbg_xg8", [P, 64]) if DEBUG_L3 else None
    dbg_ab8 = dout("dbg_ab8", [P, 16]) if DEBUG_L3 else None
    dbg_xq = dout("dbg_xq", [DM, T], BF16) if DEBUG_DUMP else None
    dbg_hq = dout("dbg_hq", [NFI * P, T], BF16) if DEBUG_DUMP else None
    dbg_wqg = dout("dbg_wqg", [DM, FI], BF16) if DEBUG_DUMP else None
    dbg_hs = dout("dbg_hs", [T, FI], BF16) if DEBUG_DUMP else None
    idf32 = din("c_idf32", [P, P])
    idbf = din("c_idbf", [P, P], BF16)
    ones = din("c_ones", [P, 1])

    wi_T = din("l0_wi_T", [DM, 2 * DI])
    wo_T = din("l0_wo_T", [DI, DM])
    wx_T = din("l0_wx_T", [P, NDI * 96])
    wdt_T = din("l0_wdt_T", [DTR, DI])
    convw = din("l0_convw", [P, NDI * 4])
    convb = din("l0_convb", [P, NDI])
    bdt = din("l0_bdt", [P, NDI])
    alog = din("l0_alog", [P, NDI * DST])
    dvec = din("l0_dvec", [P, NDI])
    wg_f = din("l0_wg", [FI, DM])
    wu_f = din("l0_wu", [FI, DM])
    wd_f = din("l0_wd", [DM, FI])

    l1_qkv_T = din("l1_qkv_T", [DM, 3 * DM])
    l1_oproj_T = din("l1_oproj_T", [DM, DM])
    l1_ogate_T = din("l1_ogate_T", [DM, DM])
    l1_ogate_b = din("l1_ogate_b", [P, NF])
    l1_decay = din("l1_decay", [1, 8])
    l1_convw = din("l1_convw", [P, NF * 4])
    l1_convb = din("l1_convb", [P, NF])
    l1_pw_T = din("l1_pw_T", [DM, DM])
    l1_gate_T = din("l1_gate_T", [2 * DM, DM])
    l1_w1_T = din("l1_w1_T", [DM, FI])
    l1_w2_T = din("l1_w2_T", [DM, FI])
    l1_w3_T = din("l1_w3_T", [FI, DM])
    l1_n1 = din("l1_n1", [P, NF])
    l1_n2 = din("l1_n2", [P, NF])
    l1_wr_T = din("l1_wr_T", [DM, DM])
    l1_wk_T = din("l1_wk_T", [DM, DM])
    l1_wv_T = din("l1_wv_T", [DM, DM])
    l1_mur = din("l1_mur", [P, NF])
    l1_muk = din("l1_muk", [P, NF])
    l2_qks = din("l2_qks", [DM, 1024])
    l2_vs = din("l2_vs", [DM, 512])
    l2_qkl = din("l2_qkl", [DM, 2048])
    l2_vl = din("l2_vl", [DM, 512])
    l2_proj = din("l2_proj", [DM, DM])
    l2_rout_T = din("l2_rout_T", [DM, 8])
    l2_rout_b = din("l2_rout_b", [8, 1])
    l2_wexp_T = din("l2_wexp_T", [8 * DM, DM])
    l2_bexp_T = din("l2_bexp_T", [8, DM])
    l2_mask_s = din("l2_mask_s", [P, 384])
    l2_mask_l = din("l2_mask_l", [P, 1152])

    l3_qkv_T = din("l3_qkv_T", [DM, 5120])
    l3_proj_T = din("l3_proj_T", [DM, DM])
    l3_bproj = din("l3_bproj", [P, 8])
    l3_cc = din("l3_cc", [P, T])
    l3_ssn = din("l3_ssn", [P, T])
    l3_mask = din("l3_mask", [P, 640])
    l3_gate_T = din("l3_gate_T", [DM, 8])
    l3_ebias = din("l3_ebias", [1, 8])
    l3_sw1_T = din("l3_sw1_T", [DM, 1024])
    l3_sw3_T = din("l3_sw3_T", [DM, 1024])
    l3_sw2_T = din("l3_sw2_T", [DM, DM])
    l3_w1 = din("l3_w1", [8 * DM, 512])
    l3_w3 = din("l3_w3", [8 * DM, 512])
    l3_w2 = din("l3_w2", [8 * 512, DM])
    l3_iota = din("l3_iota", [1, T])
    l3_c0 = din("l3_c0", [P, 1])

    wi_bf = dtmp("wi_bf", [DM, 2 * DI], BF16)
    wo_bf = dtmp("wo_bf", [DI, DM], BF16)
    ab_dr = dtmp("ab_dr", [2, T])
    bc_dr = dtmp("bc_dr", [32, T], BF16)
    row_dr = dtmp("row_dr", [4, T])
    dec_dr = dtmp("dec_dr", [1, 8])
    l1qkv_bf = dtmp("l1qkv_bf", [DM, 3 * DM], BF16)
    l1pw_bf = dtmp("l1pw_bf", [DM, DM], BF16)
    l1og_bf = dtmp("l1og_bf", [DM, DM], BF16)
    l1op_bf = dtmp("l1op_bf", [DM, DM], BF16)
    l1gt_bf = dtmp("l1gt_bf", [2 * DM, DM], BF16)
    l1w1_bf = dtmp("l1w1_bf", [DM, FI], BF16)
    l1w2_bf = dtmp("l1w2_bf", [DM, FI], BF16)
    l1w3_bf = dtmp("l1w3_bf", [FI, DM], BF16)
    l1wr_bf = dtmp("l1wr_bf", [DM, DM], BF16)
    l1wk_bf = dtmp("l1wk_bf", [DM, DM], BF16)
    l1wv_bf = dtmp("l1wv_bf", [DM, DM], BF16)
    l2qks_bf = dtmp("l2qks_bf", [DM, 1024], BF16)
    l2vs_bf = dtmp("l2vs_bf", [DM, 512], BF16)
    l2qkl_bf = dtmp("l2qkl_bf", [DM, 2048], BF16)
    l2vl_bf = dtmp("l2vl_bf", [DM, 512], BF16)
    l2pj_bf = dtmp("l2pj_bf", [DM, DM], BF16)
    l2we_bf = dtmp("l2we_bf", [8 * DM, DM], BF16)
    selm_dr = dtmp("selm_dr", [8, T], BF16)
    wqg_dr = dtmp("wqg_dr", [DM, FI], BF16)
    wqu_dr = dtmp("wqu_dr", [DM, FI], BF16)
    wqd_dr = dtmp("wqd_dr", [FI, DM], BF16)
    l3qkv_bf = dtmp("l3qkv_bf", [DM, 5120], BF16)
    l3pj_bf = dtmp("l3pj_bf", [DM, DM], BF16)
    l3sw1_bf = dtmp("l3sw1_bf", [DM, 1024], BF16)
    l3sw3_bf = dtmp("l3sw3_bf", [DM, 1024], BF16)
    l3sw2_bf = dtmp("l3sw2_bf", [DM, DM], BF16)
    l3w1_bf = dtmp("l3w1_bf", [8 * DM, 512], BF16)
    l3w3_bf = dtmp("l3w3_bf", [8 * DM, 512], BF16)
    l3w2_bf = dtmp("l3w2_bf", [8 * 512, DM], BF16)
    pk3_dr = dtmp("pk3_dr", [9, DM])
    pko3_dr = dtmp("pko3_dr", [9, DM])
    g3_dr = dtmp("g3_dr", [8, DM])
    go3_dr = dtmp("go3_dr", [8, DM])
    fe3_dr = dtmp("fe3_dr", [16, T], BF16)
    ff3_dr = dtmp("ff3_dr", [32, T], BF16)
    u13_dr = dtmp("u13_dr", [P, 512], BF16)

    X = sbuf("X", [P, NF * T])
    S = sbuf("S", [P, NF * T], BF16)
    IDF = sbuf("IDF", [P, P])
    IDB = sbuf("IDB", [P, P], BF16)
    ONE = sbuf("ONE", [P, 1])
    AB = sbuf("AB", [P, 2 * 512])
    CW = sbuf("CW", [P, NDI * 4])
    CBs = sbuf("CBs", [P, NDI])
    BDTs = sbuf("BDTs", [P, NDI])
    AMAT = sbuf("AMAT", [P, NDI * DST])
    DVs = sbuf("DVs", [P, NDI])
    WXs = sbuf("WXs", [P, NDI * 96], BF16)
    WDTs = sbuf("WDTs", [DTR, DI], BF16)
    STATE = sbuf("STATE", [P, NDI * DST])
    TAILS = sbuf("TAILS", [P, NDI * 3])
    L1CW = sbuf("L1CW", [P, NF * 4])
    L1CB = sbuf("L1CB", [P, NF])
    L1N1 = sbuf("L1N1", [P, NF])
    L1N2 = sbuf("L1N2", [P, NF])
    L1OGB = sbuf("L1OGB", [P, NF])
    DECB = sbuf("DECB", [P, 8])
    KVST = sbuf("KVST", [P, 8])
    KSST = sbuf("KSST", [P, 8])
    L1TL = sbuf("L1TL", [P, NF * 3])
    MUR = sbuf("MUR", [P, NF])
    MUK = sbuf("MUK", [P, NF])
    MUR1 = sbuf("MUR1", [P, NF])
    MUK1 = sbuf("MUK1", [P, NF])
    XPRV = sbuf("XPRV", [P, NF])
    MSK_S = sbuf("MSK_S", [P, 384])
    MSK_L = sbuf("MSK_L", [P, 1152])
    RBIAS = sbuf("RBIAS", [8, 1])
    ONES8 = sbuf("ONES8", [P, 8])
    BP3 = sbuf("BP3", [P, 8])
    C0M = sbuf("C0M", [P, 1])

    POOL = sbuf("POOL", [P, 23100])

    def pv(off_w, n, dtype=F32, p0=0, pn=P):
        if dtype == F32:
            return POOL[p0:p0 + pn, off_w:off_w + n]
        if dtype == BF16:
            return POOL.bitcast(BF16)[p0:p0 + pn, 2 * off_w:2 * off_w + n]
        raise ValueError(dtype)

    class Alloc:
        def __init__(self):
            self.o = 0

        def take(self, nw):
            v = self.o
            self.o += nw
            return v

    aa = Alloc()
    A_XNB = aa.take(NF * TC0 // 2)
    A_WIB = aa.take(8 * 512 // 2)
    A_XS1 = aa.take(TC0 + 4)
    A_XSB = aa.take(NDI * TC0 // 2)
    A_ZB = aa.take(NDI * TC0 // 2)
    A_DT = aa.take(NDI * TC0)
    A_XDB = aa.take(TC0 // 2)
    A_XDF = aa.take(TC0)
    A_BCB = aa.take(DST * TC0 // 2)
    A_BCC = aa.take(DST * TC0 // 2)
    A_DA = aa.take(4 * TC0)
    A_DB = aa.take(4 * TC0)
    A_H4 = aa.take(4 * TC0)
    A_W1 = aa.take(TC0)
    A_Y1 = aa.take(TC0)
    A_YT = aa.take(TC0)
    A_SZ = aa.take(TC0)
    A_YZ = aa.take(NDI * TC0 // 2)
    A_WOB = aa.take(DM // 2)
    assert aa.o <= 23100, aa.o

    fa = Alloc()
    F_QSC = fa.take(DM)
    F_QA = fa.take(DM)
    F_QB = fa.take(DM)
    F_QT = fa.take(DM)
    F_QS = fa.take(DM)
    F_QAM = fa.take(DM // 16)
    F_QRS = fa.take(DM // 16)
    F_WROW = fa.take(DM)
    F_WQR = fa.take(DM // 2)
    F_TSTG = fa.take(22 * 128 // 2)
    F_XNT = fa.take(DM)
    F_X32 = fa.take(256)
    F_XQF = fa.take(NF * 512 // 2)
    F_WQB = fa.take(2 * 8 * 512 // 2)
    F_HT1 = fa.take(512)
    F_HQF = fa.take(NFI * 512 // 2)
    F_WDB = fa.take(DM // 2)
    F_G1 = fa.take(256)
    assert fa.o <= 23100, fa.o

    SR0 = pv(F_WQB, T, F32, 0, 1)
    SR1 = pv(F_WQB + T, T, F32, 0, 1)
    SRT = pv(F_HQF, T, F32, 0, 1)

    PM = [st.enter_context(nc.psum_tensor(f"pm{i}", [P, 512], F32)) for i in range(4)]
    PTB = st.enter_context(nc.psum_tensor("ptb", [P, 1024], BF16))
    PTB2 = st.enter_context(nc.psum_tensor("ptb2", [P, 1024], BF16))
    PF0 = st.enter_context(nc.psum_tensor("pf0", [P, 512], F32))
    PF1 = st.enter_context(nc.psum_tensor("pf1", [P, 512], F32))
    pm_busy = [BufT() for _ in range(4)]
    pf_busy = [BufT() for _ in range(2)]
    ptb_busy = [BufT() for _ in range(2)]

    esems = {e: st.enter_context(nc.semaphore("se_" + e)) for e in ENGS}
    dsems = {q: st.enter_context(nc.semaphore("sd_" + q))
             for q in ("sync", "gpsimd", "scalar")}

    with st:
        bX = BufT()
        xt = [Pr.dma("sync", X[:, f * T:(f + 1) * T], x_in[f * P:(f + 1) * P, :])
              for f in range(NF)]
        bX.wrote(xt[-1])
        for tok in xt:
            bX._note(tok)
        t_idf = Pr.dma("sync", IDF[:], idf32)
        t_idb = Pr.dma("sync", IDB[:], idbf)
        t_one = Pr.dma("sync", ONE[:], ones)
        t_cw = Pr.dma("sync", CW[:], convw)
        t_cb = Pr.dma("sync", CBs[:], convb)
        t_bdt = Pr.dma("sync", BDTs[:], bdt)
        t_alog = Pr.dma("sync", AMAT[:], alog)
        t_dv = Pr.dma("sync", DVs[:], dvec)
        t_wx = Pr.dma("gpsimd", WXs[:], wx_T)
        t_cw1 = Pr.dma("sync", L1CW[:], l1_convw)
        t_cb1 = Pr.dma("sync", L1CB[:], l1_convb)
        t_n1 = Pr.dma("sync", L1N1[:], l1_n1)
        t_n2 = Pr.dma("sync", L1N2[:], l1_n2)
        t_ogb = Pr.dma("sync", L1OGB[:], l1_ogate_b)
        t_mur = Pr.dma("sync", MUR[:], l1_mur)
        t_msks = Pr.dma("sync", MSK_S[:], l2_mask_s)
        t_mskl = Pr.dma("sync", MSK_L[:], l2_mask_l)
        t_rb = Pr.dma("sync", RBIAS[:], l2_rout_b)
        t_o8 = Pr.run("vector", lambda e: e.memset(ONES8[:], 1.0))
        t_bp3 = Pr.dma("sync", BP3[:], l3_bproj)
        t_c0m = Pr.dma("sync", C0M[:], l3_c0)
        t_muk = Pr.dma("sync", MUK[:], l1_muk)
        t_mu1 = Pr.run("vector", lambda e: (
            e.tensor_scalar(out=MUR1[:], in0=MUR[:], scalar1=-1.0, scalar2=1.0,
                            op0=OP.mult, op1=OP.add),
            e.tensor_scalar(out=MUK1[:], in0=MUK[:], scalar1=-1.0, scalar2=1.0,
                            op0=OP.mult, op1=OP.add))[-1], deps=[t_mur, t_muk])
        t_wdt = Pr.dma("gpsimd", WDTs[:], wdt_T)

        t_A = Pr.run("scalar", lambda e: e.activation(out=AMAT[:], in_=AMAT[:],
                                                      func=AF.Exp), deps=[t_alog])
        bA = BufT()
        bA.wrote(t_A)
        t_nb = Pr.run("vector", lambda e: e.tensor_single_scalar(
            out=BDTs[:], in_=BDTs[:], scalar=-1.0, op=OP.mult), deps=[t_bdt])

        WCB = pv(A_WIB, 4096, BF16)
        bWIB = BufT()
        prev_w = None
        wi_ready = None
        wo_ready = None
        cast_list = [(wi_T, wi_bf, DM, 2 * DI), (wo_T, wo_bf, DI, DM)]
        if n_layers >= 2:
            cast_list += [(l1_qkv_T, l1qkv_bf, DM, 3 * DM),
                          (l1_pw_T, l1pw_bf, DM, DM),
                          (l1_ogate_T, l1og_bf, DM, DM),
                          (l1_oproj_T, l1op_bf, DM, DM),
                          (l1_gate_T, l1gt_bf, 2 * DM, DM),
                          (l1_w1_T, l1w1_bf, DM, FI),
                          (l1_w2_T, l1w2_bf, DM, FI),
                          (l1_w3_T, l1w3_bf, FI, DM),
                          (l1_wr_T, l1wr_bf, DM, DM),
                          (l1_wk_T, l1wk_bf, DM, DM),
                          (l1_wv_T, l1wv_bf, DM, DM)]
        if n_layers >= 3:
            cast_list += [(l2_qks, l2qks_bf, DM, 1024),
                          (l2_vs, l2vs_bf, DM, 512),
                          (l2_qkl, l2qkl_bf, DM, 2048),
                          (l2_vl, l2vl_bf, DM, 512),
                          (l2_proj, l2pj_bf, DM, DM),
                          (l2_wexp_T, l2we_bf, 8 * DM, DM)]
        if n_layers >= 4:
            qkv_s1, qkv_d1 = l3_qkv_T[:, 0:2560], l3qkv_bf[:, 0:2560]
            qkv_s2, qkv_d2 = l3_qkv_T[:, 2560:5120], l3qkv_bf[:, 2560:5120]
            cast_list += [(qkv_s1, qkv_d1, DM, 2560),
                          (qkv_s2, qkv_d2, DM, 2560),
                          (l3_proj_T, l3pj_bf, DM, DM),
                          (l3_sw1_T, l3sw1_bf, DM, 1024),
                          (l3_sw3_T, l3sw3_bf, DM, 1024),
                          (l3_sw2_T, l3sw2_bf, DM, DM),
                          (l3_w1, l3w1_bf, 8 * DM, 512),
                          (l3_w3, l3w3_bf, 8 * DM, 512),
                          (l3_w2, l3w2_bf, 8 * 512, DM)]
        cast_ready = {}
        pend = [prev_w, prev_w]
        pi = 0
        for src, dst, rows, cols in cast_list:
            last_tw = None
            for r in range(rows // P):
                c0 = 0
                while c0 < cols:
                    cw = min(2048, cols - c0)
                    h = pi % 2
                    pi += 1
                    off = h * 2048
                    tl = Pr.dma("gpsimd", WCB[:, off:off + cw],
                                src[r * P:(r + 1) * P, c0:c0 + cw],
                                deps=[pend[h]])
                    tw = Pr.dma("sync", dst[r * P:(r + 1) * P, c0:c0 + cw],
                                WCB[:, off:off + cw], deps=[tl])
                    pend[h] = tw
                    last_tw = tw
                    c0 += cw
            cast_ready[id(dst)] = last_tw
        prev_w = last_tw
        if n_layers >= 4:
            cast_ready[id(l3qkv_bf)] = cast_ready[id(qkv_d2)]
        wi_ready = cast_ready[id(wi_bf)]
        wo_ready = cast_ready[id(wo_bf)]
        bWIB.wrote(pend[0])
        bWIB._note(pend[1])

        bS = BufT()
        bSTAB = BufT()
        bAB = BufT()
        bSTATE = BufT()
        bSTATE.wrote(Pr.run("vector", lambda e: e.memset(STATE[:], 0.0)))
        bTAILS = BufT()
        bTAILS.wrote(Pr.run("vector", lambda e: e.memset(TAILS[:], 0.0)))
        bQ = BufT()

        def layernorm(sq_off, extra_deps=()):
            SQ = pv(sq_off, T)
            mm_last = None
            for f in range(NF):
                def mmfn(e, f=f):
                    last = None
                    for ct in range(4):
                        last = e.matmul(PM[ct][0:1, 0:512], ONE[:],
                                        X[:, f * T + ct * 512: f * T + (ct + 1) * 512],
                                        start=(f == 0), stop=(f == NF - 1), skip_group_check=True)
                    return last
                mm_last = Pr.run("tensor", mmfn,
                                 deps=[t_one] + bX.rd() +
                                 (sum((pm_busy[i].wr() for i in range(4)), [])
                                  if f == 0 else []) + (list(extra_deps) if f == 0 else []))
                bX.read(mm_last)
            def evac0(e):
                last = None
                for ct in range(4):
                    last = e.tensor_copy(SR0[:, ct * 512:(ct + 1) * 512],
                                         PM[ct][0:1, 0:512])
                return last
            tev0 = Pr.run("vector", evac0, deps=[mm_last] + bSTAB.wr())
            mm_last2 = None
            tsq = None
            for f in range(NF):
                tsq = Pr.run("scalar", lambda e, f=f: e.activation(
                    out=SQ, in_=X[:, f * T:(f + 1) * T], func=AF.Square),
                    deps=bX.rd() + ([mm_last2] if mm_last2 else list(extra_deps)))
                def mmfn2(e, f=f):
                    last = None
                    for ct in range(4):
                        last = e.matmul(PM[ct][0:1, 0:512], ONE[:],
                                        SQ[:, ct * 512:(ct + 1) * 512],
                                        start=(f == 0), stop=(f == NF - 1), skip_group_check=True)
                    return last
                mm_last2 = Pr.run("tensor", mmfn2,
                                  deps=[tsq, t_one] + ([tev0] if f == 0 else []))
            for i in range(4):
                pm_busy[i].wrote(mm_last2)
            def evac(e):
                last = None
                for ct in range(4):
                    last = e.tensor_copy(SR1[:, ct * 512:(ct + 1) * 512],
                                         PM[ct][0:1, 0:512])
                return last
            tev = Pr.run("vector", evac, deps=[mm_last2, tev0])
            for i in range(4):
                pm_busy[i].read(tev)
            def small(e):
                e.tensor_single_scalar(out=SR0, in_=SR0, scalar=1.0 / DM, op=OP.mult)
                e.tensor_single_scalar(out=SR1, in_=SR1, scalar=1.0 / DM, op=OP.mult)
                e.tensor_tensor(out=SRT, in0=SR0, in1=SR0, op=OP.mult)
                e.tensor_tensor(out=SR1, in0=SR1, in1=SRT, op=OP.subtract)
                return e.tensor_single_scalar(out=SR1, in_=SR1, scalar=1e-5, op=OP.add)
            tsm = Pr.run("vector", small, deps=[tev])
            tsq2 = Pr.run("scalar", lambda e: e.activation(
                out=SR1, in_=SR1, func=AF.Sqrt), deps=[tsm])
            def small2(e):
                e.reciprocal(out=SR1, in_=SR1)
                e.tensor_tensor(out=SRT, in0=SR0, in1=SR1, op=OP.mult)
                return e.tensor_single_scalar(out=SRT, in_=SRT,
                                              scalar=-1.0, op=OP.mult)
            tsm2 = Pr.run("vector", small2, deps=[tsq2])
            bSTAB.wrote(tsm2)
            t_a = Pr.dma("sync", ab_dr[0:1, :], SR1, deps=[tsm2])
            t_ab = Pr.dma("sync", ab_dr[1:2, :], SRT, deps=[tsm2])
            bSTAB.read(t_a)
            bSTAB.read(t_ab)
            return t_ab

        def load_ab(t0, tw, t_ab):
            d = bAB.wr()
            ta = Pr.dma("gpsimd", AB[:, 0:tw],
                        ab_dr[0:1, t0:t0 + tw].to_broadcast([P, tw]),
                        deps=[t_ab] + d)
            tb = Pr.dma("gpsimd", AB[:, 512:512 + tw],
                        ab_dr[1:2, t0:t0 + tw].to_broadcast([P, tw]),
                        deps=[t_ab] + d)
            bAB.wrote(tb)
            bAB._note(ta)
            return [ta, tb]

        def quant(src_ap, dst_ap, w, deps):
            nseg = w // 16
            QSC = pv(F_QSC, w)
            QA = pv(F_QA, w)
            QB = pv(F_QB, w)
            QT = pv(F_QT, w)
            QSg = pv(F_QS, w)
            QAM = pv(F_QAM, nseg)
            QRS = pv(F_QRS, nseg)
            scr = bQ.wr()
            t0_ = Pr.run("vector", lambda e: e.tensor_reduce(
                out=QAM, in_=src_ap.rearrange("p (s k) -> p s k", k=16),
                axis=AX.X, op=OP.max, apply_absolute_value=True),
                deps=list(deps) + scr)
            t1a = Pr.run("vector", lambda e: e.tensor_scalar(
                out=QRS, in0=QAM, scalar1=1.0 / 6.0, scalar2=1e-12,
                op0=OP.mult, op1=OP.max), deps=[t0_])
            t1_ = Pr.run("vector", lambda e: e.reciprocal(out=QAM, in_=QRS),
                         deps=[t1a])
            tabs = Pr.run("scalar", lambda e: e.activation(out=QSC, in_=src_ap,
                                                           func=AF.Abs),
                          deps=list(deps) + scr)
            tsgn = Pr.run("scalar", lambda e: e.sign(out=QSg, in_=src_ap),
                          deps=list(deps) + scr)
            def main(e):
                e.tensor_tensor(out=QSC.rearrange("p (s k) -> p s k", k=16),
                                in0=QSC.rearrange("p (s k) -> p s k", k=16),
                                in1=QAM.unsqueeze(-1).to_broadcast([P, nseg, 16]),
                                op=OP.mult)
                e.tensor_single_scalar(out=QT.bitcast(I32), in_=QSC.bitcast(I32),
                                       scalar=ADD_LO, op=OP.add)
                e.tensor_single_scalar(out=QT.bitcast(I32), in_=QT.bitcast(I32),
                                       scalar=MASK_HI, op=OP.bitwise_and)
                e.scalar_tensor_tensor(out=QT, in0=QT, scalar=1.0, in1=QT,
                                       op0=OP.is_ge, op1=OP.mult)
                e.tensor_scalar(out=QA, in0=QSC, scalar1=0.25, scalar2=0.5,
                                op0=OP.is_gt, op1=OP.mult)
                e.tensor_scalar(out=QB, in0=QSC, scalar1=0.75, scalar2=0.5,
                                op0=OP.is_gt, op1=OP.mult)
                e.tensor_tensor(out=QA, in0=QA, in1=QB, op=OP.add)
                e.tensor_tensor(out=QT, in0=QT, in1=QA, op=OP.max)
                e.tensor_tensor(out=QSg.rearrange("p (s k) -> p s k", k=16),
                                in0=QSg.rearrange("p (s k) -> p s k", k=16),
                                in1=QRS.unsqueeze(-1).to_broadcast([P, nseg, 16]),
                                op=OP.mult)
                return e.tensor_tensor(out=dst_ap, in0=QT, in1=QSg, op=OP.mult)
            tmain = Pr.run("vector", main, deps=[t1_, tabs, tsgn])
            bQ.wrote(tmain)
            return tmain

        def layer0():
            XNB = pv(A_XNB, NF * TC0, BF16)
            WIB = pv(A_WIB, 8 * 512, BF16)
            XS1 = pv(A_XS1, TC0 + 4)
            XSB = pv(A_XSB, NDI * TC0, BF16)
            ZB = pv(A_ZB, NDI * TC0, BF16)
            DT = pv(A_DT, NDI * TC0)
            XDB = pv(A_XDB, TC0, BF16, 0, 96)
            XDF = pv(A_XDF, TC0, F32, 64, 32)
            BCB = pv(A_BCB, DST * TC0, BF16)
            BCC = pv(A_BCC, DST * TC0, BF16)
            DA = pv(A_DA, 4 * TC0)
            DB = pv(A_DB, 4 * TC0)
            H4 = pv(A_H4, 4 * TC0)
            W1 = pv(A_W1, TC0)
            Y1 = pv(A_Y1, TC0)
            YT = pv(A_YT, TC0)
            SZ = pv(A_SZ, TC0)
            YZ = pv(A_YZ, NDI * TC0, BF16)
            WOB = pv(A_WOB, DM, BF16)

            bXNB = BufT()
            bXSB = BufT()
            bZB = BufT()
            bXS1 = BufT()
            bXDB = BufT()
            bXDF = BufT()
            bBC = BufT()
            bDA = BufT()
            bDB = BufT()
            bH4 = BufT()
            bW1 = BufT()
            bY1 = BufT()
            bSZ = BufT()
            bYZ = BufT()
            bWOB = BufT()
            bDT = BufT()
            bDT.read(bWIB.w)

            t_ab = layernorm(A_DT, extra_deps=bWIB.wr())
            bDT.wrote(t_ab)

            for tc in range(NTC0):
                t0 = tc * TC0
                ab_bc = load_ab(t0, TC0, t_ab)

                def xnb_fn(e, t0=t0):
                    last = None
                    for f in range(NF):
                        e.tensor_tensor(out=W1,
                                        in0=X[:, f * T + t0:f * T + t0 + TC0],
                                        in1=AB[:, 0:TC0], op=OP.mult)
                        last = e.tensor_tensor(
                            out=XNB[:, f * TC0:(f + 1) * TC0],
                            in0=W1, in1=AB[:, 512:512 + TC0], op=OP.add)
                    return last
                t_xnb = Pr.run("vector", xnb_fn,
                               deps=ab_bc + bX.rd() + bXNB.wr() + bW1.wr())
                bXNB.wrote(t_xnb)
                bW1.wrote(t_xnb)
                bAB.read(t_xnb)
                bX.read(t_xnb)

                for og in range(8):
                    t_wld = Pr.dma(
                        "sync", WIB,
                        wi_bf.rearrange("(k p) o -> p k o", p=P)[:, :, og * 512:(og + 1) * 512],
                        deps=[wi_ready] + bWIB.wr())
                    bWIB.wrote(t_wld)
                    for m4 in range(4):
                        d = og * 4 + m4
                        bank = d % 4
                        def mmfn(e, m4=m4, bank=bank):
                            last = None
                            for k in range(8):
                                last = e.matmul(
                                    PM[bank][:, 0:TC0],
                                    WIB[:, k * 512 + m4 * 128: k * 512 + (m4 + 1) * 128],
                                    XNB[:, k * TC0:(k + 1) * TC0],
                                    start=(k == 0), stop=(k == 7), skip_group_check=True)
                            return last
                        t_mm = Pr.run("tensor", mmfn,
                                      deps=[t_wld, t_xnb] + pm_busy[bank].wr())
                        pm_busy[bank].wrote(t_mm)
                        bWIB.read(t_mm)
                        bXNB.read(t_mm)
                        if d < NDI:
                            def evconv(e, d=d, bank=bank):
                                e.tensor_copy(XS1[:, 0:3], TAILS[:, d * 3:(d + 1) * 3])
                                e.tensor_copy(XS1[:, 3:3 + TC0], PM[bank][:, 0:TC0])
                                e.tensor_copy(TAILS[:, d * 3:(d + 1) * 3],
                                              PM[bank][:, TC0 - 3:TC0])
                                e.tensor_scalar(out=W1, in0=XS1[:, 0:TC0],
                                                scalar1=CW[:, d * 4:d * 4 + 1],
                                                scalar2=None, op0=OP.mult)
                                last = None
                                for j in range(1, 4):
                                    last = e.scalar_tensor_tensor(
                                        out=W1, in0=XS1[:, j:j + TC0],
                                        scalar=CW[:, d * 4 + j:d * 4 + j + 1],
                                        in1=W1, op0=OP.mult, op1=OP.add)
                                return last
                            t_cv = Pr.run("vector", evconv,
                                          deps=[t_mm, t_cw, bTAILS.w] +
                                          bXS1.wr() + bW1.wr())
                            bXS1.wrote(t_cv)
                            bTAILS.wrote(t_cv)
                            bW1.wrote(t_cv)
                            pm_busy[bank].read(t_cv)
                            t_si = Pr.run("scalar", lambda e, d=d: e.activation(
                                out=XSB[:, d * TC0:(d + 1) * TC0], in_=W1,
                                func=AF.Silu, bias=CBs[:, d:d + 1]),
                                deps=[t_cv, t_cb] + (bXSB.wr() if d == 0 else []))
                            bW1.read(t_si)
                            if d == 0:
                                bXSB.wrote(t_si)
                            else:
                                bXSB.piece(t_si)
                        else:
                            d2 = d - NDI
                            t_z = Pr.run("scalar", lambda e, d2=d2, bank=bank:
                                         e.activation(out=ZB[:, d2 * TC0:(d2 + 1) * TC0],
                                                      in_=PM[bank][:, 0:TC0],
                                                      func=AF.Silu),
                                         deps=[t_mm] + (bZB.wr() if d2 == 0 else []))
                            pm_busy[bank].read(t_z)
                            if d2 == 0:
                                bZB.wrote(t_z)
                            else:
                                bZB.piece(t_z)

                def xdbl(e):
                    last = None
                    for k in range(NDI):
                        last = e.matmul(PF0[0:96, 0:TC0],
                                        WXs[:, k * 96:(k + 1) * 96],
                                        XSB[:, k * TC0:(k + 1) * TC0],
                                        start=(k == 0), stop=(k == NDI - 1), skip_group_check=True)
                    return last
                t_xd = Pr.run("tensor", xdbl,
                              deps=[t_wx, bXSB.w] + pf_busy[0].wr())
                pf_busy[0].wrote(t_xd)
                bXSB.read(t_xd)
                def xdev(e):
                    e.tensor_copy(XDB, PF0[0:96, 0:TC0])
                    return e.tensor_copy(XDF, PF0[64:96, 0:TC0])
                t_xde = Pr.run("vector", xdev, deps=[t_xd] + bXDB.wr() + bXDF.wr())
                bXDB.wrote(t_xde)
                bXDF.wrote(t_xde)
                pf_busy[0].read(t_xde)
                t_bcw = Pr.dma("gpsimd", bc_dr[0:32, t0:t0 + TC0], XDF,
                               deps=[t_xde])
                bXDF.read(t_bcw)
                d_bc = bBC.wr()
                t_bb = Pr.dma("sync", BCB.rearrange("p (n t) -> p n t", n=DST),
                              bc_dr[0:16, t0:t0 + TC0]
                              .unsqueeze(0).to_broadcast([P, DST, TC0]),
                              deps=[t_bcw] + d_bc)
                t_cc = Pr.dma("sync", BCC.rearrange("p (n t) -> p n t", n=DST),
                              bc_dr[16:32, t0:t0 + TC0]
                              .unsqueeze(0).to_broadcast([P, DST, TC0]),
                              deps=[t_bcw] + d_bc)
                bBC.wrote(t_cc)
                bBC._note(t_bb)

                dt_toks = []
                for dd in range(NDI):
                    bank = dd % 4
                    t_m = Pr.run("tensor", lambda e, dd=dd, bank=bank: e.matmul(
                        PM[bank][:, 0:TC0], WDTs[:, dd * P:(dd + 1) * P],
                        XDB[0:DTR, :], start=True, stop=True, skip_group_check=True),
                        deps=[t_wdt, t_xde] + pm_busy[bank].wr())
                    pm_busy[bank].wrote(t_m)
                    bXDB.read(t_m)
                    t_sp = Pr.run("scalar", lambda e, dd=dd, bank=bank: e.activation(
                        out=DT[:, dd * TC0:(dd + 1) * TC0], in_=PM[bank][:, 0:TC0],
                        func=AF.Sigmoid, scale=-1.0, bias=BDTs[:, dd:dd + 1]),
                        deps=[t_m, t_nb] + (bDT.wr() if dd == 0 else []))
                    pm_busy[bank].read(t_sp)
                    if dd == 0:
                        bDT.wrote(t_sp)
                    else:
                        bDT.piece(t_sp)
                t_ln = Pr.run("scalar", lambda e: e.activation(
                    out=DT[:, 0:NDI * TC0], in_=DT[:, 0:NDI * TC0], func=AF.Ln),
                    deps=[bDT.w])
                bDT.wrote(t_ln)
                for dd in range(NDI):
                    dt_toks.append(t_ln)

                for dd in range(NDI):
                    t_w1 = Pr.run("vector", lambda e, dd=dd: e.scalar_tensor_tensor(
                        out=W1, in0=DT[:, dd * TC0:(dd + 1) * TC0], scalar=-1.0,
                        in1=XSB[:, dd * TC0:(dd + 1) * TC0], op0=OP.mult, op1=OP.mult),
                        deps=[dt_toks[dd], bXSB.w] + bW1.wr())
                    bW1.wrote(t_w1)
                    bDT.read(t_w1)
                    bXSB.read(t_w1)
                    last_rd = None
                    for q in range(4):
                        nb = q * 4
                        def exps(e, dd=dd, nb=nb):
                            last = None
                            for j in range(4):
                                last = e.activation(
                                    out=DA[:, j * TC0:(j + 1) * TC0],
                                    in_=DT[:, dd * TC0:(dd + 1) * TC0],
                                    func=AF.Exp,
                                    scale=AMAT[:, dd * DST + nb + j:dd * DST + nb + j + 1])
                            return last
                        t_da = Pr.run("scalar", exps,
                                      deps=[dt_toks[dd]] + bA.rd() + bDA.wr())
                        bDA.wrote(t_da)
                        bA.read(t_da)
                        bDT.read(t_da)
                        t_db = Pr.run("vector", lambda e, nb=nb: e.tensor_tensor(
                            out=DB.rearrange("p (n t) -> p n t", n=4),
                            in0=W1.unsqueeze(1).to_broadcast([P, 4, TC0]),
                            in1=BCB[:, nb * TC0:(nb + 4) * TC0]
                            .rearrange("p (n t) -> p n t", n=4),
                            op=OP.mult),
                            deps=[t_w1, bBC.w] + bDB.wr())
                        bDB.wrote(t_db)
                        bW1.read(t_db)
                        bBC.read(t_db)
                        def scans(e, dd=dd, nb=nb):
                            for j in range(4):
                                e.tensor_tensor_scan(
                                    out=H4[:, j * TC0:(j + 1) * TC0],
                                    data0=DA[:, j * TC0:(j + 1) * TC0],
                                    data1=DB[:, j * TC0:(j + 1) * TC0],
                                    initial=STATE[:, dd * DST + nb + j:dd * DST + nb + j + 1],
                                    op0=OP.mult, op1=OP.add)
                            for j in range(4):
                                e.tensor_copy(
                                    STATE[:, dd * DST + nb + j:dd * DST + nb + j + 1],
                                    H4[:, j * TC0 + TC0 - 1:j * TC0 + TC0])
                            return e.tensor_tensor(
                                out=H4.rearrange("p (n t) -> p n t", n=4),
                                in0=H4.rearrange("p (n t) -> p n t", n=4),
                                in1=BCC[:, nb * TC0:(nb + 4) * TC0]
                                .rearrange("p (n t) -> p n t", n=4),
                                op=OP.mult)
                        t_sc = Pr.run("vector", scans,
                                      deps=[t_da, t_db, bSTATE.w] + bH4.wr())
                        bH4.wrote(t_sc)
                        bDA.read(t_sc)
                        bDB.read(t_sc)
                        bSTATE.wrote(t_sc)
                        bBC.read(t_sc)
                        def red(e, q=q):
                            if q == 0:
                                return e.tensor_reduce(
                                    out=Y1, in_=H4.rearrange("p (n t) -> p t n", n=4),
                                    axis=AX.X, op=OP.add)
                            e.tensor_reduce(out=YT,
                                            in_=H4.rearrange("p (n t) -> p t n", n=4),
                                            axis=AX.X, op=OP.add)
                            return e.tensor_tensor(out=Y1, in0=Y1, in1=YT, op=OP.add)
                        t_rd = Pr.run("vector", red,
                                      deps=[t_sc] + (bY1.wr() if q == 0 else []))
                        bY1.wrote(t_rd)
                        bH4.read(t_rd)
                        last_rd = t_rd
                    t_yd = Pr.run("vector", lambda e, dd=dd: e.scalar_tensor_tensor(
                        out=Y1, in0=XSB[:, dd * TC0:(dd + 1) * TC0],
                        scalar=DVs[:, dd:dd + 1], in1=Y1, op0=OP.mult, op1=OP.add),
                        deps=[last_rd, t_dv, bXSB.w])
                    bY1.wrote(t_yd)
                    bXSB.read(t_yd)
                    t_yz = Pr.run("vector", lambda e, dd=dd: e.tensor_tensor(
                        out=YZ[:, dd * TC0:(dd + 1) * TC0], in0=Y1,
                        in1=ZB[:, dd * TC0:(dd + 1) * TC0], op=OP.mult),
                        deps=[t_yd, bZB.w] + (bYZ.wr() if dd == 0 else []))
                    bZB.read(t_yz)
                    bY1.read(t_yz)
                    if dd == 0:
                        bYZ.wrote(t_yz)
                    else:
                        bYZ.piece(t_yz)

                if False:
                    for c_ in range(NDI):
                        Pr.dma("gpsimd", dbg_xs[c_ * P:(c_ + 1) * P, t0:t0 + TC0],
                               XSB[:, c_ * TC0:(c_ + 1) * TC0], deps=[bXSB.w, bYZ.w])
                        Pr.dma("gpsimd", dbg_yz[c_ * P:(c_ + 1) * P, t0:t0 + TC0],
                               YZ[:, c_ * TC0:(c_ + 1) * TC0], deps=[bYZ.w])
                        tdm = Pr.dma("gpsimd", dbg_dt[c_ * P:(c_ + 1) * P, t0:t0 + TC0],
                                     DT[:, c_ * TC0:(c_ + 1) * TC0], deps=[bDT.w])
                        bXSB.read(tdm)
                        bYZ.read(tdm)
                        bDT.read(tdm)

                for half in range(2):
                    mm_last = None
                    for k in range(NDI):
                        t_wo = Pr.dma("sync", WOB[:, 0:512],
                                      wo_bf[k * P:(k + 1) * P, half * 512:(half + 1) * 512],
                                      deps=[wo_ready] + bWOB.wr())
                        bWOB.wrote(t_wo)
                        def mmo(e, k=k):
                            last = None
                            for oi in range(4):
                                last = e.matmul(
                                    PM[oi][:, 0:TC0],
                                    WOB[:, oi * P:(oi + 1) * P],
                                    YZ[:, k * TC0:(k + 1) * TC0],
                                    start=(k == 0), stop=(k == NDI - 1), skip_group_check=True)
                            return last
                        mm_last = Pr.run("tensor", mmo,
                                         deps=[t_wo, bYZ.w] +
                                         (sum((pm_busy[i].wr() for i in range(4)), [])
                                          if k == 0 else []))
                        bWOB.read(mm_last)
                        bYZ.read(mm_last)
                    for i in range(4):
                        pm_busy[i].wrote(mm_last)
                    def sevac(e, t0=t0, half=half):
                        last = None
                        for oi in range(4):
                            oo = half * 4 + oi
                            last = e.activation(
                                out=S[:, oo * T + t0: oo * T + t0 + TC0],
                                in_=PM[oi][:, 0:TC0], func=AF.Copy)
                        return last
                    t_se = Pr.run("scalar", sevac,
                                  deps=[mm_last] + (bS.wr() if tc == 0 and half == 0 else []))
                    for i in range(4):
                        pm_busy[i].read(t_se)
                    if tc == 0 and half == 0:
                        bS.wrote(t_se)
                    else:
                        bS.piece(t_se)

            attn_join = joinb(bSTAB, bXNB, bXSB, bZB, bDT, bBC, bDA, bDB, bH4, bW1,
                              bY1, bSZ, bYZ, bWOB, bXS1, bXDB, bXDF, bWIB)

            WROW = pv(F_WROW, DM)
            WQR = pv(F_WQR, DM, BF16)
            TSTG = pv(F_TSTG, 22 * 128, BF16)
            XNT = pv(F_XNT, DM)
            X32 = pv(F_X32, 256)
            XQF = pv(F_XQF, NF * 512, BF16)
            WQB = pv(F_WQB, 2 * 8 * 512, BF16)
            HT1 = pv(F_HT1, 512)
            HQF = pv(F_HQF, NFI * 512, BF16)
            WDB = pv(F_WDB, DM, BF16)
            G1 = pv(F_G1, 256)
            bWROW = BufT()
            bWQR = BufT()
            bTSTG = BufT()
            bXNT = BufT()
            bX32 = BufT()
            bXQF = BufT()
            bWQB = BufT()
            bHT1 = BufT()
            bHQF = BufT()
            bWDB = BufT()
            bG1 = BufT()
            bQ.read(attn_join[0] if attn_join else None)
            for tok in attn_join:
                bQ._note(tok)
                bWROW._note(tok)
                bTSTG._note(tok)
                bXNT._note(tok)
                bX32._note(tok)
                bXQF._note(tok)
                bWQB._note(tok)
                bHT1._note(tok)
                bHQF._note(tok)
                bWDB._note(tok)
                bG1._note(tok)
                bWQR._note(tok)

            wq_ready = []
            for wsrc, wdst, nrc, win in ((wg_f, wqg_dr, NFI, DM),
                                         (wu_f, wqu_dr, NFI, DM),
                                         (wd_f, wqd_dr, NF, FI)):
                npc = (win + DM - 1) // DM
                for rc in range(nrc):
                    for pc in range(npc):
                        c0 = pc * DM
                        cw = min(DM, win - c0)
                        ntl = cw // P
                        t_ld = Pr.dma("sync", WROW[:, 0:cw],
                                      wsrc[rc * P:(rc + 1) * P, c0:c0 + cw],
                                      deps=bWROW.wr())
                        bWROW.wrote(t_ld)
                        t_q = quant(WROW[:, 0:cw], WQR[:, 0:cw], cw,
                                    deps=[t_ld] + bWQR.wr())
                        bWROW.read(t_q)
                        bWQR.wrote(t_q)
                        def trp(e, ntl=ntl):
                            last = None
                            for j in range(ntl):
                                last = e.transpose(PTB[:, j * P:(j + 1) * P],
                                                   WQR[:, j * P:(j + 1) * P], IDB[:])
                            return last
                        t_tr = Pr.run("tensor", trp,
                                      deps=[t_q, t_idb] + ptb_busy[0].wr())
                        ptb_busy[0].wrote(t_tr)
                        bWQR.read(t_tr)
                        t_ev = Pr.run("vector", lambda e, ntl=ntl: e.tensor_copy(
                            TSTG[:, 0:ntl * P], PTB[:, 0:ntl * P]),
                            deps=[t_tr] + bTSTG.wr())
                        ptb_busy[0].read(t_ev)
                        bTSTG.wrote(t_ev)
                        t_w = Pr.dma(
                            "gpsimd",
                            wdst.rearrange("(j p) o -> p j o", p=P)
                            [:, pc * 8:pc * 8 + ntl, rc * P:(rc + 1) * P],
                            TSTG[:, 0:ntl * P].rearrange("p (j o) -> p j o", o=P),
                            deps=[t_ev])
                        bTSTG.read(t_w)
                        wq_ready.append(t_w)
            t_wq = wq_ready[-1]
            t_wqg = wq_ready[NFI - 1]
            t_wqu = wq_ready[2 * NFI - 1]

            for sub in range(4):
                tbase = sub * 512
                xq_done = None
                for hh in range(2):
                    h0 = tbase + hh * 256
                    ab_bc = load_ab(h0, TC0, t_ab)
                    tr_toks = []
                    for f in range(NF):
                        def x32fn(e, f=f, h0=h0):
                            e.tensor_tensor(out=X32,
                                            in0=X[:, f * T + h0:f * T + h0 + 256],
                                            in1=AB[:, 0:256], op=OP.mult)
                            return e.tensor_tensor(out=X32, in0=X32,
                                                   in1=AB[:, 512:512 + 256], op=OP.add)
                        t_x32 = Pr.run("vector", x32fn,
                                       deps=ab_bc + bX.rd() + bX32.wr())
                        bX32.wrote(t_x32)
                        bAB.read(t_x32)
                        bX.read(t_x32)
                        def trp2(e, f=f):
                            lo = (f % 4) * P
                            if f < 4:
                                e.transpose(PF0[:, lo:lo + P], X32[:, 0:P], IDF[:])
                                return e.transpose(PM[0][:, lo:lo + P],
                                                   X32[:, P:2 * P], IDF[:])
                            e.transpose(PF1[:, lo:lo + P], X32[:, 0:P], IDF[:])
                            return e.transpose(PM[1][:, lo:lo + P],
                                               X32[:, P:2 * P], IDF[:])
                        t_tr = Pr.run("tensor", trp2,
                                      deps=[t_x32, t_idf] +
                                      (pf_busy[0].wr() + pf_busy[1].wr() +
                                       pm_busy[0].wr() + pm_busy[1].wr()
                                       if f == 0 else []))
                        bX32.read(t_tr)
                        tr_toks.append(t_tr)
                    pf_busy[0].wrote(tr_toks[-1])
                    pf_busy[1].wrote(tr_toks[-1])
                    pm_busy[0].wrote(tr_toks[-1])
                    pm_busy[1].wrote(tr_toks[-1])
                    for tj2 in range(2):
                        tj = hh * 2 + tj2
                        if tj2 == 0:
                            def evx(e):
                                e.tensor_copy(XNT[:, 0:512], PF0[:, 0:512])
                                return e.tensor_copy(XNT[:, 512:1024], PF1[:, 0:512])
                            t_ev = Pr.run("vector", evx,
                                          deps=[tr_toks[-1]] + bXNT.wr())
                            pf_busy[0].read(t_ev)
                            pf_busy[1].read(t_ev)
                        else:
                            def evx(e):
                                e.tensor_copy(XNT[:, 0:512], PM[0][:, 0:512])
                                return e.tensor_copy(XNT[:, 512:1024], PM[1][:, 0:512])
                            t_ev = Pr.run("vector", evx,
                                          deps=[tr_toks[-1]] + bXNT.wr())
                            pm_busy[0].read(t_ev)
                            pm_busy[1].read(t_ev)
                        bXNT.wrote(t_ev)
                        XT, bXT = XNT, bXNT
                        t_q = quant(XT, WQR[:, 0:DM], DM, deps=bXT.rd() + bWQR.wr())
                        bXT.read(t_q)
                        bWQR.wrote(t_q)
                        def trq(e):
                            last = None
                            for j in range(NF):
                                last = e.transpose(PTB[:, j * P:(j + 1) * P],
                                                   WQR[:, j * P:(j + 1) * P], IDB[:])
                            return last
                        t_tr = Pr.run("tensor", trq,
                                      deps=[t_q, t_idb] + ptb_busy[0].wr())
                        ptb_busy[0].wrote(t_tr)
                        bWQR.read(t_tr)
                        t_ev2 = Pr.run("vector", lambda e, tj=tj: e.tensor_copy(
                            XQF.rearrange("p (k t) -> p k t", k=NF)
                            [:, :, tj * P:(tj + 1) * P],
                            PTB[:, 0:NF * P].rearrange("p (k t) -> p k t", k=NF)),
                            deps=[t_tr] + (bXQF.wr() if tj == 0 else []))
                        ptb_busy[0].read(t_ev2)
                        if tj == 0:
                            bXQF.wrote(t_ev2)
                        else:
                            bXQF.piece(t_ev2)
                        xq_done = t_ev2

                if DEBUG_DUMP:
                    for k_ in range(NF):
                        tq_ = Pr.dma("gpsimd", dbg_xq[k_ * P:(k_ + 1) * P, tbase:tbase + 512],
                                     XQF[:, k_ * 512:(k_ + 1) * 512], deps=[xq_done])
                        bXQF.read(tq_)

                ofw_list = [512, 512, 512, 512, 512, 256]
                for of in range(6):
                    ofw = ofw_list[of]
                    t_wg = Pr.dma("sync", WQB[:, 0:8 * ofw].rearrange(
                        "p (k o) -> p k o", k=8),
                        wqg_dr.rearrange("(k p) o -> p k o", p=P)
                        [:, :, of * 512:of * 512 + ofw],
                        deps=[t_wqg] + bWQB.wr())
                    t_wu = Pr.dma("sync", WQB[:, 8 * 512:8 * 512 + 8 * ofw].rearrange(
                        "p (k o) -> p k o", k=8),
                        wqu_dr.rearrange("(k p) o -> p k o", p=P)
                        [:, :, of * 512:of * 512 + ofw],
                        deps=[t_wqu] + bWQB.wr())
                    bWQB.wrote(t_wu)
                    bWQB._note(t_wg)
                    for tj in range(4):
                        def mmg(e, tj=tj, ofw=ofw):
                            last = None
                            for k in range(8):
                                last = e.matmul(
                                    PM[0][:, 0:ofw],
                                    XQF[:, k * 512 + tj * P: k * 512 + (tj + 1) * P],
                                    WQB[:, k * ofw:(k + 1) * ofw],
                                    start=(k == 0), stop=(k == 7), skip_group_check=True)
                            return last
                        t_g = Pr.run("tensor", mmg,
                                     deps=[t_wg, xq_done] + pm_busy[0].wr())
                        pm_busy[0].wrote(t_g)
                        bXQF.read(t_g)
                        bWQB.read(t_g)
                        def mmu(e, tj=tj, ofw=ofw):
                            last = None
                            for k in range(8):
                                last = e.matmul(
                                    PM[1][:, 0:ofw],
                                    XQF[:, k * 512 + tj * P: k * 512 + (tj + 1) * P],
                                    WQB[:, 8 * 512 + k * ofw:8 * 512 + (k + 1) * ofw],
                                    start=(k == 0), stop=(k == 7), skip_group_check=True)
                            return last
                        t_u = Pr.run("tensor", mmu,
                                     deps=[t_wu, xq_done] + pm_busy[1].wr())
                        pm_busy[1].wrote(t_u)
                        bXQF.read(t_u)
                        bWQB.read(t_u)
                        t_si = Pr.run("scalar", lambda e, ofw=ofw: e.activation(
                            out=HT1[:, 0:ofw], in_=PM[0][:, 0:ofw], func=AF.Silu),
                            deps=[t_g] + bHT1.wr())
                        pm_busy[0].read(t_si)
                        bHT1.wrote(t_si)
                        t_h = Pr.run("vector", lambda e, ofw=ofw: e.tensor_tensor(
                            out=HT1[:, 0:ofw], in0=HT1[:, 0:ofw],
                            in1=PM[1][:, 0:ofw], op=OP.mult),
                            deps=[t_si, t_u])
                        if DEBUG_DUMP:
                            t_hd = Pr.dma("gpsimd",
                                          dbg_hs[tbase + tj * P:tbase + (tj + 1) * P,
                                                 of * 512:of * 512 + ofw],
                                          HT1[:, 0:ofw], deps=[t_h])
                            bHT1.read(t_hd)
                            bHT1.w = t_h
                        pm_busy[1].read(t_h)
                        bHT1.wrote(t_h)
                        t_q = quant(HT1[:, 0:ofw], WQR[:, 0:ofw], ofw,
                                    deps=[t_h] + bWQR.wr())
                        bHT1.read(t_q)
                        bWQR.wrote(t_q)
                        ntl = ofw // P
                        def trh(e, ntl=ntl):
                            last = None
                            for j in range(ntl):
                                last = e.transpose(PTB2[:, j * P:(j + 1) * P],
                                                   WQR[:, j * P:(j + 1) * P], IDB[:])
                            return last
                        t_tr = Pr.run("tensor", trh,
                                      deps=[t_q, t_idb] + ptb_busy[1].wr())
                        ptb_busy[1].wrote(t_tr)
                        bWQR.read(t_tr)
                        c0 = of * 4
                        first_piece = (of == 0 and tj == 0)
                        t_ev3 = Pr.run("vector", lambda e, ntl=ntl, c0=c0, tj=tj:
                                       e.tensor_copy(
                                           HQF.rearrange("p (c t) -> p c t", c=NFI)
                                           [:, c0:c0 + ntl, tj * P:(tj + 1) * P],
                                           PTB2[:, 0:ntl * P]
                                           .rearrange("p (c t) -> p c t", c=ntl)),
                                       deps=[t_tr] + (bHQF.wr() if first_piece else []))
                        ptb_busy[1].read(t_ev3)
                        if first_piece:
                            bHQF.wrote(t_ev3)
                        else:
                            bHQF.piece(t_ev3)

                if DEBUG_DUMP:
                    for c_ in range(NFI):
                        th_ = Pr.dma("gpsimd", dbg_hq[c_ * P:(c_ + 1) * P, tbase:tbase + 512],
                                     HQF[:, c_ * 512:(c_ + 1) * 512], deps=[bHQF.w])
                        bHQF.read(th_)

                for hh in range(2):
                    col = hh * 256
                    for half in range(2):
                        mm_last = None
                        for c in range(NFI):
                            t_wd = Pr.dma("sync", WDB[:, 0:512],
                                          wqd_dr[c * P:(c + 1) * P, half * 512:(half + 1) * 512],
                                          deps=[t_wq] + bWDB.wr())
                            bWDB.wrote(t_wd)
                            def mmd(e, c=c, col=col):
                                last = None
                                for oi in range(4):
                                    last = e.matmul(
                                        PM[oi][:, 0:256],
                                        WDB[:, oi * P:(oi + 1) * P],
                                        HQF[:, c * 512 + col: c * 512 + col + 256],
                                        start=(c == 0), stop=(c == NFI - 1), skip_group_check=True)
                                return last
                            mm_last = Pr.run("tensor", mmd,
                                             deps=[t_wd, bHQF.w] +
                                             (sum((pm_busy[i].wr() for i in range(4)), [])
                                              if c == 0 else []))
                            bWDB.read(mm_last)
                            bHQF.read(mm_last)
                        for i in range(4):
                            pm_busy[i].wrote(mm_last)
                        for oi in range(4):
                            oo = half * 4 + oi
                            t_gs = Pr.run("scalar", lambda e, oi=oi: e.activation(
                                out=G1, in_=PM[oi][:, 0:256], func=AF.Silu),
                                deps=[mm_last] + bG1.wr())
                            pm_busy[oi].read(t_gs)
                            bG1.wrote(t_gs)
                            tcol = tbase + col
                            def upd(e, oo=oo, tcol=tcol):
                                e.tensor_tensor(out=G1, in0=G1,
                                                in1=S[:, oo * T + tcol:oo * T + tcol + 256],
                                                op=OP.mult)
                                return e.tensor_tensor(
                                    out=X[:, oo * T + tcol:oo * T + tcol + 256],
                                    in0=X[:, oo * T + tcol:oo * T + tcol + 256],
                                    in1=G1, op=OP.add)
                            t_up = Pr.run("vector", upd, deps=[t_gs, bS.w] + bX.rd())
                            bG1.wrote(t_up)
                            bS.read(t_up)
                            bX.piece(t_up)

            return joinb(bSTAB, bXNB, bXSB, bZB, bDT, bBC, bDA, bDB, bH4, bW1,
                         bY1, bYZ, bWOB, bXS1, bXDB, bXDF, bWIB, bQ, bWROW,
                         bWQR, bTSTG, bXNT, bX32, bXQF, bWQB, bHT1, bHQF, bWDB,
                         bG1)

        def layer1(join_in):
            la = Alloc()
            L_RSTD = la.take(512)
            L_ROW = la.take(512)
            L_ROW2 = la.take(512)
            L_ROW3 = la.take(512)
            L_HB = la.take(NF * 516 // 2)
            L_CVSV = la.take(2048)
            L_LOC = la.take(2048)
            L_O = la.take(2048)
            L_GB = la.take(2048)
            L_XI = la.take(4096)
            L_SH = la.take(6144)
            L_WS = la.take(512)
            assert la.o <= 23100, la.o

            RSTD = pv(L_RSTD, 512)
            ROW0 = pv(L_ROW, 512, F32, 0, 1)
            ROW1 = pv(L_ROW2, 512, F32, 0, 1)
            ROWB = pv(L_ROW3, 512, F32, 0, 1)
            HB = pv(L_HB, NF * 516, BF16)
            AT = pv(L_HB, NF * 512, BF16)
            CVSV = pv(L_CVSV, 4096, BF16)
            LOC = pv(L_LOC, 4096, BF16)
            H2B = pv(L_LOC, 4096, BF16)
            OB = pv(L_O, 4096, BF16)
            GB = pv(L_GB, 4096, BF16)
            XI = pv(L_XI, 4096)
            QH = pv(L_SH, 512)
            KH = pv(L_SH + 512, 512)
            T1 = pv(L_SH + 1024, 512)
            T2 = pv(L_SH + 1536, 512)
            KVB = pv(L_SH + 2048, 512)
            MB = pv(L_SH, 22 * 512, BF16)
            M1 = pv(L_SH + 5632, 512)
            WS = pv(L_WS, 1024, BF16)
            WSg = pv(L_SH + 3072, 4096, BF16)
            W1S = pv(L_GB, 4096, BF16)
            W2S = pv(L_O, 4096, BF16)

            bufs = {}
            for nm in ("RSTD", "ROW", "HB", "CVSV", "LOC", "OB", "GB", "XI",
                       "SH", "WS", "ROWD"):
                bufs[nm] = BufT()
                for tok in join_in:
                    bufs[nm]._note(tok)
            bR = bufs["RSTD"]; bRW = bufs["ROW"]; bHB = bufs["HB"]
            bCV = bufs["CVSV"]; bLC = bufs["LOC"]; bO = bufs["OB"]
            bG = bufs["GB"]; bXI = bufs["XI"]; bSH = bufs["SH"]
            bWS = bufs["WS"]; bRD = bufs["ROWD"]

            t_ab = layernorm(L_SH, extra_deps=join_in)
            t_dc0 = Pr.dma("sync", pv(L_ROW, 8, F32, 0, 1), l1_decay,
                           deps=bRW.wr())
            t_dc1 = Pr.run("scalar", lambda e: e.activation(
                out=pv(L_ROW, 8, F32, 0, 1), in_=pv(L_ROW, 8, F32, 0, 1),
                func=AF.Sigmoid), deps=[t_dc0])
            t_dc2 = Pr.run("vector", lambda e: e.tensor_single_scalar(
                out=pv(L_ROW, 8, F32, 0, 1), in_=pv(L_ROW, 8, F32, 0, 1),
                scalar=1e-8, op=OP.max), deps=[t_dc1])
            t_dc3 = Pr.dma("sync", dec_dr, pv(L_ROW, 8, F32, 0, 1), deps=[t_dc2])
            t_dc = Pr.dma("gpsimd", DECB[:], dec_dr.to_broadcast([P, 8]),
                          deps=[t_dc3])
            bRW.wrote(t_dc3)
            t_st0 = Pr.run("vector", lambda e: (
                e.memset(KVST[:], 0.0), e.memset(KSST[:], 0.0),
                e.memset(L1TL[:], 0.0), e.memset(XPRV[:], 0.0))[-1], deps=join_in)
            bKV = BufT(); bKV.wrote(t_st0)
            t_pad = Pr.run("vector", lambda e: e.memset(HB, 0.0),
                           deps=join_in + bHB.wr())
            bHB.wrote(t_pad)

            def r1(e):
                e.tensor_tensor(out=ROW1, in0=ROW0, in1=ROW0, op=OP.mult)
                return e.tensor_scalar(out=ROW1, in0=ROW1, scalar1=-1e-5,
                                       scalar2=1.0 + 1e-6, op0=OP.mult, op1=OP.add)

            for tc in range(4):
                t0 = tc * 512
                ab_bc = load_ab(t0, 512, t_ab)
                t_ra = Pr.dma("sync", ROW0, ab_dr[0:1, t0:t0 + 512], deps=bRW.wr())
                bRW.wrote(t_ra)
                t_r1 = Pr.run("vector", r1, deps=[t_ra])
                t_r2 = Pr.run("scalar", lambda e: e.activation(
                    out=ROW1, in_=ROW1, func=AF.Sqrt), deps=[t_r1])
                t_r3 = Pr.run("vector", lambda e: e.reciprocal(out=ROW1, in_=ROW1),
                              deps=[t_r2])
                bRW._note(t_r3); bRW.w = t_r3
                t_rb = Pr.dma("sync", ROWB, ab_dr[1:2, t0:t0 + 512], deps=[t_r3])
                def r3(e):
                    e.tensor_tensor(out=ROW0, in0=ROW0, in1=ROW1, op=OP.mult)
                    return e.tensor_tensor(out=ROWB, in0=ROWB, in1=ROW1, op=OP.mult)
                t_r4 = Pr.run("vector", r3, deps=[t_rb, t_r3])
                t_r5 = Pr.dma("sync", row_dr[0:1, 0:512], ROW0, deps=[t_r4])
                t_r6 = Pr.dma("sync", row_dr[1:2, 0:512], ROWB, deps=[t_r4])
                bRW.wrote(t_r6); bRW._note(t_r5)
                d_r = bR.wr()
                t_a2 = Pr.dma("gpsimd", RSTD, row_dr[0:1, 0:512].to_broadcast([P, 512]),
                              deps=[t_r5] + d_r)
                t_b2 = Pr.dma("gpsimd", pv(L_XI, 512),
                              row_dr[1:2, 0:512].to_broadcast([P, 512]),
                              deps=[t_r6] + bXI.wr())
                bR.wrote(t_a2)
                bXI.wrote(t_b2)

                hb_toks = []
                for f in range(NF):
                    def hfn(e, f=f, t0=t0):
                        e.tensor_tensor(out=M1, in0=X[:, f * T + t0:f * T + t0 + 512],
                                        in1=RSTD, op=OP.mult)
                        e.tensor_tensor(out=M1, in0=M1, in1=pv(L_XI, 512), op=OP.add)
                        return e.tensor_scalar(out=HB[:, f * 516 + 4:f * 516 + 516],
                                               in0=M1, scalar1=L1N1[:, f:f + 1],
                                               scalar2=None, op0=OP.mult)
                    tk = Pr.run("vector", hfn,
                                deps=[t_a2, t_b2, t_n1] + bX.rd() + bSH.wr() +
                                ([t_pad] if tc == 0 else bHB.wr() if f == 0 else []))
                    bSH.wrote(tk)
                    bX.read(tk)
                    if f == 0 and tc == 0:
                        bHB.wrote(tk)
                    else:
                        bHB.piece(tk)
                    hb_toks.append(tk)
                def tfn(e):
                    last = None
                    for f in range(NF):
                        last = e.tensor_copy(HB[:, f * 516 + 1:f * 516 + 4],
                                             L1TL[:, f * 3:(f + 1) * 3].bitcast(BF16)[:, 0:3])
                    return last
                t_tl = Pr.run("vector", tfn, deps=hb_toks + [bKV.w] + bHB.wr())
                bHB.piece(t_tl)
                def tsv(e):
                    last = None
                    for f in range(NF):
                        last = e.tensor_copy(
                            L1TL[:, f * 3:(f + 1) * 3].bitcast(BF16)[:, 0:3],
                            HB[:, f * 516 + 513:f * 516 + 516])
                    return last
                t_ts = Pr.run("vector", tsv, deps=[t_tl])
                bKV.wrote(t_ts)
                bHB.read(t_ts)

                cv_toks = []
                for f in range(NF):
                    def cfn(e, f=f):
                        e.tensor_scalar(out=M1, in0=HB[:, f * 516 + 1:f * 516 + 513],
                                        scalar1=L1CW[:, f * 4:f * 4 + 1],
                                        scalar2=None, op0=OP.mult)
                        last = None
                        for j in range(1, 4):
                            last = e.scalar_tensor_tensor(
                                out=M1, in0=HB[:, f * 516 + 1 + j:f * 516 + 513 + j],
                                scalar=L1CW[:, f * 4 + j:f * 4 + j + 1],
                                in1=M1, op0=OP.mult, op1=OP.add)
                        return last
                    tk = Pr.run("vector", cfn,
                                deps=[t_ts, t_cw1] + bSH.wr())
                    bSH.wrote(tk)
                    bHB.read(tk)
                    tk2 = Pr.run("scalar", lambda e, f=f: e.activation(
                        out=CVSV[:, f * 512:(f + 1) * 512], in_=M1, func=AF.Silu,
                        bias=L1CB[:, f:f + 1]),
                        deps=[tk, t_cb1] + (bCV.wr() if f == 0 else []))
                    bSH.read(tk2)
                    if f == 0:
                        bCV.wrote(tk2)
                    else:
                        bCV.piece(tk2)
                    cv_toks.append(tk2)

                if DEBUG_L1:
                    for f_ in range(NF):
                        th_ = Pr.dma("gpsimd",
                                     dbg_yz[DI // 2 + f_ * P:DI // 2 + (f_ + 1) * P,
                                            t0:t0 + 512],
                                     HB[:, f_ * 516 + 4:f_ * 516 + 516],
                                     deps=[bHB.w, bCV.w])
                        bHB.read(th_)
                        bCV.read(th_)

                if DEBUG_L1:
                    for f_ in range(NF):
                        tc_ = Pr.dma("gpsimd",
                                     dbg_dt[DM + f_ * P:DM + (f_ + 1) * P, t0:t0 + 512],
                                     CVSV[:, f_ * 512:(f_ + 1) * 512],
                                     deps=[bCV.w])
                        bCV.read(tc_)

                def mm_set(wbf, wrows, o_list, rhs_fn, evac_fn, strip_buf, strip_bt,
                           extra_w_dep=None):
                    toks = []
                    nk = wrows // P
                    for oi, oc in enumerate(o_list):
                        t_w = Pr.dma("sync",
                                     strip_buf[:, 0:nk * P].rearrange(
                                         "p (k o) -> p k o", o=P),
                                     wbf.rearrange("(k p) o -> p k o", p=P)
                                     [:, :, oc * P:(oc + 1) * P],
                                     deps=([extra_w_dep] if extra_w_dep else []) +
                                     strip_bt.wr())
                        strip_bt.wrote(t_w)
                        bank = oi % 2
                        def mmfn(e, nk=nk, bank=bank):
                            last = None
                            for k in range(nk):
                                last = e.matmul(PM[bank][:, 0:512],
                                                strip_buf[:, k * P:(k + 1) * P],
                                                rhs_fn(k),
                                                start=(k == 0), stop=(k == nk - 1),
                                                skip_group_check=True)
                            return last
                        rhs_deps = [rhs_fn("dep")]
                        t_mm = Pr.run("tensor", mmfn,
                                      deps=[t_w] + rhs_deps + pm_busy[bank].wr())
                        pm_busy[bank].wrote(t_mm)
                        strip_bt.read(t_mm)
                        t_ev = evac_fn(oc, bank, t_mm)
                        pm_busy[bank].read(t_ev)
                        toks.append(t_ev)
                    return toks

                def loc_ev(oc, bank, t_mm):
                    tk = Pr.run("vector", lambda e, oc=oc, bank=bank: e.tensor_copy(
                        LOC[:, oc * 512:(oc + 1) * 512], PM[bank][:, 0:512]),
                        deps=[t_mm] + (bLC.wr() if oc == 0 else []))
                    if oc == 0:
                        bLC.wrote(tk)
                    else:
                        bLC.piece(tk)
                    return tk
                def cv_rhs(k):
                    if k == "dep":
                        return bCV.w
                    return CVSV[:, k * 512:(k + 1) * 512]
                loc_toks = mm_set(l1pw_bf, DM, list(range(8)), cv_rhs, loc_ev,
                                  WS, bWS, cast_ready[id(l1pw_bf)])
                for tk in loc_toks:
                    bCV.read(tk)

                def hb_rhs(k):
                    if k == "dep":
                        return bHB.w
                    return HB[:, k * 516 + 4:k * 516 + 516]
                o_piece = [None]
                for h in range(8):
                    def make_ev(dst, kind, h=h):
                        def ev(oc, bank, t_mm):
                            if kind == "v":
                                tk = Pr.run("vector", lambda e: e.tensor_copy(
                                    CVSV[:, h * 512:(h + 1) * 512],
                                    PM[bank][:, 0:512]),
                                    deps=[t_mm] + (bCV.wr() if h == 0 else []))
                                if h == 0:
                                    bCV.wrote(tk)
                                else:
                                    bCV.piece(tk)
                                return tk
                            t1 = Pr.run("vector", lambda e: e.tensor_single_scalar(
                                out=T1, in_=PM[bank][:, 0:512], scalar=0.0, op=OP.min),
                                deps=[t_mm] + bSH.wr())
                            bSH.wrote(t1)
                            t2 = Pr.run("scalar", lambda e: e.activation(
                                out=T1, in_=T1, func=AF.Exp), deps=[t1])
                            bSH.wrote(t2)
                            t3 = Pr.run("vector", lambda e, dst=dst: e.scalar_tensor_tensor(
                                out=dst, in0=PM[bank][:, 0:512], scalar=0.0,
                                in1=T1, op0=OP.max, op1=OP.add), deps=[t2, t_mm])
                            bSH.wrote(t3)
                            return t3
                        return ev
                    tq = mm_set(l1qkv_bf, DM, [h], hb_rhs, make_ev(QH, "q"),
                                WS, bWS, cast_ready[id(l1qkv_bf)])[0]
                    tk_ = mm_set(l1qkv_bf, DM, [8 + h], hb_rhs, make_ev(KH, "k"),
                                 WS, bWS, cast_ready[id(l1qkv_bf)])[0]
                    tv = mm_set(l1qkv_bf, DM, [16 + h], hb_rhs, make_ev(None, "v"),
                                WS, bWS, cast_ready[id(l1qkv_bf)])[0]
                    def kvf(e, h=h):
                        return e.tensor_tensor(out=KVB, in0=KH,
                                               in1=CVSV[:, h * 512:(h + 1) * 512],
                                               op=OP.mult)
                    t_kv = Pr.run("vector", kvf, deps=[tk_, tv] + bSH.wr())
                    bSH.wrote(t_kv)
                    def scf(e, h=h):
                        e.tensor_tensor_scan(out=T1, data0=DECB[:, h:h + 1]
                                             .broadcast_to([P, 512]), data1=KVB,
                                             initial=KVST[:, h:h + 1],
                                             op0=OP.mult, op1=OP.add)
                        e.tensor_copy(KVST[:, h:h + 1], T1[:, 511:512])
                        e.tensor_tensor_scan(out=T2, data0=DECB[:, h:h + 1]
                                             .broadcast_to([P, 512]), data1=KH,
                                             initial=KSST[:, h:h + 1],
                                             op0=OP.mult, op1=OP.add)
                        return e.tensor_copy(KSST[:, h:h + 1], T2[:, 511:512])
                    t_sc = Pr.run("vector", scf, deps=[t_kv, t_dc, bKV.w])
                    bKV.wrote(t_sc)
                    bSH.wrote(t_sc)
                    t_d1 = Pr.run("vector", lambda e: e.tensor_tensor(
                        out=KVB, in0=QH, in1=T2, op=OP.mult), deps=[t_sc])
                    bSH.wrote(t_d1)
                    t_d2 = Pr.run("tensor", lambda e: e.matmul(
                        PF0[0:1, 0:512], ONE[:], KVB, start=True, stop=True,
                        skip_group_check=True),
                        deps=[t_d1, t_one] + pf_busy[0].wr())
                    pf_busy[0].wrote(t_d2)
                    t_d3 = Pr.run("vector", lambda e: e.tensor_copy(
                        ROW0, PF0[0:1, 0:512]), deps=[t_d2] + bRW.wr())
                    pf_busy[0].read(t_d3)
                    t_d4 = Pr.run("vector", lambda e: e.tensor_single_scalar(
                        out=ROW0, in_=ROW0, scalar=1e-6, op=OP.max), deps=[t_d3])
                    t_d5 = Pr.run("vector", lambda e: e.reciprocal(
                        out=ROW0, in_=ROW0), deps=[t_d4])
                    t_d6 = Pr.dma("sync", row_dr[2:3, 0:512], ROW0, deps=[t_d5])
                    bRW.wrote(t_d6)
                    t_d7 = Pr.dma("gpsimd", RSTD,
                                  row_dr[2:3, 0:512].to_broadcast([P, 512]),
                                  deps=[t_d6] + bR.wr())
                    bR.wrote(t_d7)
                    def ofn(e, h=h):
                        e.tensor_tensor(out=T1, in0=QH, in1=T1, op=OP.mult)
                        return e.tensor_tensor(out=OB[:, h * 512:(h + 1) * 512],
                                               in0=T1, in1=RSTD, op=OP.mult)
                    t_o = Pr.run("vector", ofn,
                                 deps=[t_sc, t_d7, tq] + (bO.wr() if h == 0 else []))
                    bSH.wrote(t_o)
                    bR.read(t_o)
                    if h == 0:
                        bO.wrote(t_o)
                    else:
                        bO.piece(t_o)

                def ob_rhs(k):
                    if k == "dep":
                        return bO.w
                    return OB[:, k * 512:(k + 1) * 512]
                def g_ev(oc, bank, t_mm):
                    tk = Pr.run("scalar", lambda e, oc=oc, bank=bank: e.activation(
                        out=GB[:, oc * 512:(oc + 1) * 512], in_=PM[bank][:, 0:512],
                        func=AF.Sigmoid, bias=L1OGB[:, oc:oc + 1]),
                        deps=[t_mm, t_ogb] + (bG.wr() if oc == 0 else []))
                    if oc == 0:
                        bG.wrote(tk)
                    else:
                        bG.piece(tk)
                    return tk
                g_toks = mm_set(l1og_bf, DM, list(range(8)), ob_rhs, g_ev,
                                WS, bWS, cast_ready[id(l1og_bf)])
                for tk in g_toks:
                    bO.read(tk)
                o2_toks = []
                for f in range(NF):
                    def o2f(e, f=f):
                        e.tensor_tensor(out=M1, in0=OB[:, f * 512:(f + 1) * 512],
                                        in1=CVSV[:, f * 512:(f + 1) * 512],
                                        op=OP.subtract)
                        e.tensor_tensor(out=M1, in0=M1,
                                        in1=GB[:, f * 512:(f + 1) * 512], op=OP.mult)
                        return e.tensor_tensor(out=OB[:, f * 512:(f + 1) * 512],
                                               in0=M1,
                                               in1=CVSV[:, f * 512:(f + 1) * 512],
                                               op=OP.add)
                    tk = Pr.run("vector", o2f,
                                deps=[g_toks[f], bO.w, bCV.w] + bSH.wr())
                    bSH.wrote(tk)
                    bG.read(tk)
                    bCV.read(tk)
                    bO.piece(tk)
                    o2_toks.append(tk)
                def at_ev(oc, bank, t_mm):
                    tk = Pr.run("vector", lambda e, oc=oc, bank=bank: e.tensor_copy(
                        AT[:, oc * 512:(oc + 1) * 512], PM[bank][:, 0:512]),
                        deps=[t_mm] + (bHB.wr() if oc == 0 else []))
                    if oc == 0:
                        bHB.wrote(tk)
                    else:
                        bHB.piece(tk)
                    return tk
                at_toks = mm_set(l1op_bf, DM, list(range(8)), ob_rhs, at_ev,
                                 WS, bWS, cast_ready[id(l1op_bf)])
                for tk in at_toks:
                    bO.read(tk)

                def comb_rhs(k):
                    if k == "dep":
                        return bHB.w
                    if k < 8:
                        return LOC[:, k * 512:(k + 1) * 512]
                    return AT[:, (k - 8) * 512:(k - 8) * 512 + 512]
                def gm_ev(oc, bank, t_mm):
                    t1 = Pr.run("scalar", lambda e, bank=bank: e.activation(
                        out=M1, in_=PM[bank][:, 0:512], func=AF.Sigmoid),
                        deps=[t_mm] + bSH.wr())
                    bSH.wrote(t1)
                    def mixf(e, oc=oc, t0=t0):
                        e.tensor_tensor(out=QH, in0=LOC[:, oc * 512:(oc + 1) * 512],
                                        in1=AT[:, oc * 512:(oc + 1) * 512],
                                        op=OP.subtract)
                        e.tensor_tensor(out=QH, in0=QH, in1=M1, op=OP.mult)
                        e.tensor_tensor(out=QH, in0=QH,
                                        in1=AT[:, oc * 512:(oc + 1) * 512], op=OP.add)
                        e.tensor_tensor(out=T2, in0=X[:, oc * T + t0:oc * T + t0 + 512],
                                        in1=AB[:, 0:512], op=OP.mult)
                        e.tensor_tensor(out=T2, in0=T2, in1=AB[:, 512:1024], op=OP.add)
                        return e.tensor_tensor(out=XI[:, oc * 512:(oc + 1) * 512],
                                               in0=QH, in1=T2, op=OP.add)
                    tk = Pr.run("vector", mixf,
                                deps=[t1, bHB.w, bLC.w] + ab_bc + bX.rd() +
                                (bXI.wr() if oc == 0 else []))
                    bSH.wrote(tk)
                    bHB.read(tk)
                    bLC.read(tk)
                    bX.read(tk)
                    bAB.read(tk)
                    if oc == 0:
                        bXI.wrote(tk)
                    else:
                        bXI.piece(tk)
                    return tk
                mix_toks = mm_set(l1gt_bf, 2 * DM, list(range(8)), comb_rhs, gm_ev,
                                  WSg, bufs["WS"] if False else bSH,
                                  cast_ready[id(l1gt_bf)])

                sq_toks = []
                mmrow = [None]
                for f in range(NF):
                    tsq = Pr.run("scalar", lambda e, f=f: e.activation(
                        out=QH, in_=XI[:, f * 512:(f + 1) * 512], func=AF.Square),
                        deps=[bXI.w] + bSH.wr())
                    bSH.wrote(tsq)
                    tmm = Pr.run("tensor", lambda e, f=f: e.matmul(
                        PF0[0:1, 0:512], ONE[:], QH, start=(f == 0),
                        stop=(f == NF - 1), skip_group_check=True),
                        deps=[tsq, t_one] + (pf_busy[0].wr() if f == 0 else []))
                    bSH.read(tmm)
                    mmrow[0] = tmm
                pf_busy[0].wrote(mmrow[0])
                t_q1 = Pr.run("vector", lambda e: e.tensor_copy(
                    ROW0, PF0[0:1, 0:512]), deps=[mmrow[0]] + bRW.wr())
                pf_busy[0].read(t_q1)
                t_q2 = Pr.run("vector", lambda e: e.tensor_scalar(
                    out=ROW0, in0=ROW0, scalar1=1.0 / DM, scalar2=1e-6,
                    op0=OP.mult, op1=OP.add), deps=[t_q1])
                t_q3 = Pr.run("scalar", lambda e: e.activation(
                    out=ROW0, in_=ROW0, func=AF.Sqrt), deps=[t_q2])
                t_q4 = Pr.run("vector", lambda e: e.reciprocal(out=ROW0, in_=ROW0),
                              deps=[t_q3])
                t_q5 = Pr.dma("sync", row_dr[3:4, 0:512], ROW0, deps=[t_q4])
                bRW.wrote(t_q5)
                t_q6 = Pr.dma("gpsimd", RSTD, row_dr[3:4, 0:512].to_broadcast([P, 512]),
                              deps=[t_q5] + bR.wr())
                bR.wrote(t_q6)
                h2_toks = []
                for f in range(NF):
                    def h2f(e, f=f):
                        e.tensor_tensor(out=QH, in0=XI[:, f * 512:(f + 1) * 512],
                                        in1=RSTD, op=OP.mult)
                        return e.tensor_scalar(out=H2B[:, f * 512:(f + 1) * 512],
                                               in0=QH, scalar1=L1N2[:, f:f + 1],
                                               scalar2=None, op0=OP.mult)
                    tk = Pr.run("vector", h2f,
                                deps=[t_q6, t_n2, bXI.w] + bSH.wr() +
                                (bLC.wr() if f == 0 else []))
                    bSH.wrote(tk)
                    bR.read(tk)
                    bXI.read(tk)
                    if f == 0:
                        bLC.wrote(tk)
                    else:
                        bLC.piece(tk)
                    h2_toks.append(tk)

                for c in range(NFI):
                    t_w1 = Pr.dma("sync", W1S[:, 0:1024].rearrange(
                        "p (k o) -> p k o", o=P),
                        l1w1_bf.rearrange("(k p) o -> p k o", p=P)
                        [:, :, c * P:(c + 1) * P],
                        deps=[cast_ready[id(l1w1_bf)]] + bG.wr())
                    bG.wrote(t_w1)
                    t_w2 = Pr.dma("sync", W2S[:, 0:1024].rearrange(
                        "p (k o) -> p k o", o=P),
                        l1w2_bf.rearrange("(k p) o -> p k o", p=P)
                        [:, :, c * P:(c + 1) * P],
                        deps=[cast_ready[id(l1w2_bf)]] + bO.wr())
                    bO.wrote(t_w2)
                    def mm1(e):
                        last = None
                        for k in range(8):
                            last = e.matmul(PM[2][:, 0:512],
                                            W1S[:, k * P:(k + 1) * P],
                                            H2B[:, k * 512:(k + 1) * 512],
                                            start=(k == 0), stop=(k == 7),
                                            skip_group_check=True)
                        return last
                    t_m1 = Pr.run("tensor", mm1,
                                  deps=[t_w1, bLC.w] + pm_busy[2].wr())
                    pm_busy[2].wrote(t_m1)
                    bG.read(t_m1)
                    bLC.read(t_m1)
                    def mm2(e):
                        last = None
                        for k in range(8):
                            last = e.matmul(PM[3][:, 0:512],
                                            W2S[:, k * P:(k + 1) * P],
                                            H2B[:, k * 512:(k + 1) * 512],
                                            start=(k == 0), stop=(k == 7),
                                            skip_group_check=True)
                        return last
                    t_m2 = Pr.run("tensor", mm2,
                                  deps=[t_w2, bLC.w] + pm_busy[3].wr())
                    pm_busy[3].wrote(t_m2)
                    bO.read(t_m2)
                    bLC.read(t_m2)
                    t_si = Pr.run("scalar", lambda e: e.activation(
                        out=M1, in_=PM[2][:, 0:512], func=AF.Silu),
                        deps=[t_m1] + bSH.wr())
                    bSH.wrote(t_si)
                    pm_busy[2].read(t_si)
                    tk = Pr.run("vector", lambda e, c=c: e.tensor_tensor(
                        out=MB[:, c * 512:(c + 1) * 512],
                        in0=M1, in1=PM[3][:, 0:512], op=OP.mult),
                        deps=[t_si, t_m2] + (bSH.wr() if c == 0 else []))
                    pm_busy[3].read(tk)
                    if c == 0:
                        bSH.wrote(tk)
                    else:
                        bSH.piece(tk)
                for half in range(2):
                    mm_last = None
                    for c in range(NFI):
                        t_w3 = Pr.dma("sync", WS[:, 0:512],
                                      l1w3_bf[c * P:(c + 1) * P,
                                              half * 512:(half + 1) * 512],
                                      deps=[cast_ready[id(l1w3_bf)]] + bWS.wr())
                        bWS.wrote(t_w3)
                        def mmd(e, c=c):
                            last = None
                            for oi in range(4):
                                last = e.matmul(PM[oi][:, 0:512],
                                                WS[:, oi * P:(oi + 1) * P],
                                                MB[:, c * 512:(c + 1) * 512],
                                                start=(c == 0), stop=(c == NFI - 1),
                                                skip_group_check=True)
                            return last
                        mm_last = Pr.run("tensor", mmd,
                                         deps=[t_w3, bSH.w] +
                                         (sum((pm_busy[i].wr() for i in range(4)), [])
                                          if c == 0 else []))
                        bWS.read(mm_last)
                        bSH.read(mm_last)
                    for i in range(4):
                        pm_busy[i].wrote(mm_last)
                    for oi in range(4):
                        oc = half * 4 + oi
                        tk = Pr.run("vector", lambda e, oc=oc, oi=oi, t0=t0:
                                    e.tensor_tensor(
                                        out=S[:, oc * T + t0:oc * T + t0 + 512],
                                        in0=PM[oi][:, 0:512],
                                        in1=XI[:, oc * 512:(oc + 1) * 512], op=OP.add),
                                    deps=[mm_last, bXI.w] + bS.wr())
                        pm_busy[oi].read(tk)
                        bXI.read(tk)
                        bS.piece(tk)

                if DEBUG_L1:
                    for f_ in range(NF):
                        Pr.dma("gpsimd", dbg_xs[f_ * P:(f_ + 1) * P, t0:t0 + 512],
                               LOC[:, f_ * 512:(f_ + 1) * 512], deps=[bLC.w, bHB.w])
                        Pr.dma("gpsimd", dbg_xs[DM + f_ * P:DM + (f_ + 1) * P, t0:t0 + 512],
                               AT[:, f_ * 512:(f_ + 1) * 512], deps=[bHB.w])
                        Pr.dma("gpsimd", dbg_yz[f_ * P:(f_ + 1) * P, t0:t0 + 512],
                               OB[:, f_ * 512:(f_ + 1) * 512], deps=[bO.w])
                        tdm_ = Pr.dma("gpsimd", dbg_dt[f_ * P:(f_ + 1) * P, t0:t0 + 512],
                                      XI[:, f_ * 512:(f_ + 1) * 512], deps=[bXI.w])
                        bLC.read(tdm_)
                        bHB.read(tdm_)
                        bO.read(tdm_)
                        bXI.read(tdm_)

                XRB = pv(L_HB, 4096, BF16)
                XKB = pv(L_CVSV, 4096, BF16)
                RBB = pv(L_LOC, 4096, BF16)
                KAB = pv(L_GB, 4096, BF16)
                SHF = pv(L_XI, 516)
                for f in range(NF):
                    def shfn(e, f=f, t0=t0):
                        e.tensor_copy(SHF[:, 0:1], XPRV[:, f:f + 1])
                        e.tensor_tensor(out=SHF[:, 1:513],
                                        in0=X[:, f * T + t0:f * T + t0 + 512],
                                        in1=AB[:, 0:512], op=OP.mult)
                        e.tensor_tensor(out=SHF[:, 1:513], in0=SHF[:, 1:513],
                                        in1=AB[:, 512:1024], op=OP.add)
                        e.tensor_copy(XPRV[:, f:f + 1], SHF[:, 512:513])
                        e.tensor_scalar(out=M1, in0=SHF[:, 0:512],
                                        scalar1=MUR1[:, f:f + 1], scalar2=None,
                                        op0=OP.mult)
                        e.scalar_tensor_tensor(out=XRB[:, f * 512:(f + 1) * 512],
                                               in0=SHF[:, 1:513],
                                               scalar=MUR[:, f:f + 1], in1=M1,
                                               op0=OP.mult, op1=OP.add)
                        e.tensor_scalar(out=M1, in0=SHF[:, 0:512],
                                        scalar1=MUK1[:, f:f + 1], scalar2=None,
                                        op0=OP.mult)
                        return e.scalar_tensor_tensor(
                            out=XKB[:, f * 512:(f + 1) * 512], in0=SHF[:, 1:513],
                            scalar=MUK[:, f:f + 1], in1=M1, op0=OP.mult, op1=OP.add)
                    tk = Pr.run("vector", shfn,
                                deps=ab_bc + bX.rd() + [t_mu1, bKV.w] + bSH.wr() +
                                (bHB.wr() + bCV.wr() + bXI.wr() if f == 0 else []))
                    bSH.wrote(tk)
                    bKV.wrote(tk)
                    bX.read(tk)
                    bAB.read(tk)
                    bXI.wrote(tk)
                    if f == 0:
                        bHB.wrote(tk)
                        bCV.wrote(tk)
                    else:
                        bHB.piece(tk)
                        bCV.piece(tk)
                def xr_rhs(k):
                    if k == "dep":
                        return bHB.w
                    return XRB[:, k * 512:(k + 1) * 512]
                def r_ev(oc, bank, t_mm):
                    tk = Pr.run("scalar", lambda e, oc=oc, bank=bank: e.activation(
                        out=RBB[:, oc * 512:(oc + 1) * 512], in_=PM[bank][:, 0:512],
                        func=AF.Sigmoid),
                        deps=[t_mm] + (bLC.wr() if oc == 0 else []))
                    if oc == 0:
                        bLC.wrote(tk)
                    else:
                        bLC.piece(tk)
                    return tk
                r_toks = mm_set(l1wr_bf, DM, list(range(8)), xr_rhs, r_ev,
                                WS, bWS, cast_ready[id(l1wr_bf)])
                for tk in r_toks:
                    bHB.read(tk)
                def xk_rhs(k):
                    if k == "dep":
                        return bCV.w
                    return XKB[:, k * 512:(k + 1) * 512]
                def k_ev(oc, bank, t_mm):
                    t1 = Pr.run("scalar", lambda e, bank=bank: e.activation(
                        out=M1, in_=PM[bank][:, 0:512], func=AF.Relu),
                        deps=[t_mm] + bSH.wr())
                    bSH.wrote(t1)
                    tk = Pr.run("scalar", lambda e, oc=oc: e.activation(
                        out=KAB[:, oc * 512:(oc + 1) * 512], in_=M1, func=AF.Square),
                        deps=[t1] + (bG.wr() if oc == 0 else []))
                    bSH.read(tk)
                    if oc == 0:
                        bG.wrote(tk)
                    else:
                        bG.piece(tk)
                    return tk
                k_toks = mm_set(l1wk_bf, DM, list(range(8)), xk_rhs, k_ev,
                                WS, bWS, cast_ready[id(l1wk_bf)])
                for tk in k_toks:
                    bCV.read(tk)
                def ka_rhs(k):
                    if k == "dep":
                        return bG.w
                    return KAB[:, k * 512:(k + 1) * 512]
                def v_ev(oc, bank, t_mm):
                    t1 = Pr.run("vector", lambda e, oc=oc, bank=bank: e.tensor_tensor(
                        out=M1, in0=PM[bank][:, 0:512],
                        in1=RBB[:, oc * 512:(oc + 1) * 512], op=OP.mult),
                        deps=[t_mm, bLC.w] + bSH.wr())
                    bSH.wrote(t1)
                    bLC.read(t1)
                    t2 = Pr.run("scalar", lambda e: e.activation(
                        out=M1, in_=M1, func=AF.Silu), deps=[t1])
                    bSH.wrote(t2)
                    def upd(e, oc=oc, t0=t0):
                        e.tensor_tensor(out=M1, in0=M1,
                                        in1=S[:, oc * T + t0:oc * T + t0 + 512],
                                        op=OP.mult)
                        return e.tensor_tensor(
                            out=X[:, oc * T + t0:oc * T + t0 + 512],
                            in0=X[:, oc * T + t0:oc * T + t0 + 512],
                            in1=M1, op=OP.add)
                    tk = Pr.run("vector", upd, deps=[t2, bS.w] + bX.rd())
                    bSH.wrote(tk)
                    bS.read(tk)
                    bX.piece(tk)
                    return tk
                v_toks = mm_set(l1wv_bf, DM, list(range(8)), ka_rhs, v_ev,
                                WS, bWS, cast_ready[id(l1wv_bf)])
                for tk in v_toks:
                    bG.read(tk)

            return joinb(*bufs.values())


        def layer2(join_in):
            la = Alloc()
            C_XNQ = la.take(NF * 1536 // 2)
            C_KL = la.take(NF * 1536 // 2)
            C_VL = la.take(12 * 512 // 2)
            C_QT = la.take(512)
            C_SS = la.take(1152)
            C_PP = la.take(576)
            C_PT = la.take(576)
            C_OS = la.take(2048)
            C_WS = la.take(1024)
            C_RS = la.take(512)
            C_MS = la.take(512)
            assert la.o <= 23100, la.o

            XNQ = pv(C_XNQ, NF * 1536, BF16)
            KF = pv(C_KL, NF * 1536, BF16)
            VT = pv(C_VL, 12 * 512, BF16)
            QT = pv(C_QT, NF * 128, BF16)
            SS = pv(C_SS, 1152)
            PPB = pv(C_PP, 1152, BF16)
            PT = pv(C_PT, 1152, BF16)
            OSS = pv(C_OS, 2048, BF16)
            OSL = pv(C_OS + 1024, 2048, BF16)
            WS = pv(C_WS, 2048, BF16)
            VW = pv(C_SS, 4096, BF16)
            NM = pv(C_RS, 1)
            SM = pv(C_RS + 4, 1)
            MS = pv(C_MS, 512)
            XN5 = pv(C_XNQ, NF * 512, BF16)
            ME = pv(C_KL, NF * 512, BF16)
            MASKE = pv(C_KL + 4096, 512, BF16)
            LG = pv(C_RS, 512, F32, 0, 8)
            GT = pv(C_MS + 64, 8)
            ROUT = pv(C_PP, 64, BF16)
            BEXB = pv(C_QT, 1024, BF16, 0, 8)
            SELFB = pv(C_VL + 512, 512, BF16, 0, 8)

            bufs = {nm: BufT() for nm in
                    ("XNQ", "KF", "VT", "QT", "SS", "PP", "PT", "OS", "WS",
                     "RS", "MS")}
            for b in bufs.values():
                for tok in join_in:
                    b._note(tok)
            bXQ = bufs["XNQ"]; bKF = bufs["KF"]; bVT = bufs["VT"]
            bQT = bufs["QT"]; bSS = bufs["SS"]; bPP = bufs["PP"]
            bPT = bufs["PT"]; bOS = bufs["OS"]; bWS = bufs["WS"]
            bRS = bufs["RS"]; bMS = bufs["MS"]

            t_ab = layernorm(C_XNQ, extra_deps=join_in)

            def build_xnq(v0, span, extra):
                toks = []
                nsub = (span + 511) // 512
                for si in range(nsub):
                    c0 = v0 + si * 512
                    cw = min(512, v0 + span - c0)
                    abb = load_ab(c0, cw, t_ab)
                    for f in range(NF):
                        def fn(e, f=f, c0=c0, cw=cw, si=si, extra=extra):
                            e.tensor_tensor(out=MS[:, 0:cw],
                                            in0=X[:, f * T + c0:f * T + c0 + cw],
                                            in1=AB[:, 0:cw], op=OP.mult)
                            return e.tensor_tensor(
                                out=XNQ[:, f * 1536 + extra + si * 512:
                                        f * 1536 + extra + si * 512 + cw],
                                in0=MS[:, 0:cw], in1=AB[:, 512:512 + cw], op=OP.add)
                        tk = Pr.run("vector", fn,
                                    deps=abb + bX.rd() + bMS.wr() +
                                    (bXQ.wr() if si == 0 and f == 0 else []))
                        bMS.wrote(tk)
                        bX.read(tk)
                        bAB.read(tk)
                        if si == 0 and f == 0:
                            bXQ.wrote(tk)
                        else:
                            bXQ.piece(tk)
                        toks.append(tk)
                return toks

            def attn_pass(qr, mode):
                t0 = qr * 512
                halo = 256 if mode == "s" else 1024
                span = halo + 512
                v0 = t0 - halo
                pad = 0
                if v0 < 0:
                    pad = -v0
                    v0 = 0
                if pad > 0:
                    def zf(e, pad=pad):
                        last = None
                        for f in range(NF):
                            last = e.memset(XNQ[:, f * 1536:f * 1536 + pad], 0.0)
                        return last
                    tz = Pr.run("vector", zf, deps=bXQ.wr() + [t_ab])
                    bXQ.wrote(tz)
                build_xnq(v0, span - pad, pad)
                nspan = span
                wq_bf = l2qks_bf if mode == "s" else l2qkl_bf
                wv_bf = l2vs_bf if mode == "s" else l2vl_bf
                qcols = 512 if mode == "s" else 1024
                dqk = 64 if mode == "s" else 128
                win = 384 if mode == "s" else 1152
                MSK = MSK_S if mode == "s" else MSK_L
                scale = float(1.0 / np.sqrt(32.0 if mode == "s" else 128.0))
                OSX = OSS if mode == "s" else OSL
                nkc = qcols // P
                for oc in range(nkc):
                    t_w = Pr.dma("sync", WS[:, 0:1024].rearrange(
                        "p (k o) -> p k o", o=P),
                        wq_bf.rearrange("(k p) o -> p k o", p=P)
                        [:, :, qcols + oc * P:qcols + (oc + 1) * P],
                        deps=[cast_ready[id(wq_bf)]] + bWS.wr())
                    bWS.wrote(t_w)
                    for nt in range((nspan + 511) // 512):
                        nw = min(512, nspan - nt * 512)
                        bank = nt % 2
                        def mmfn(e, oc=oc, nt=nt, nw=nw, bank=bank):
                            last = None
                            for k in range(8):
                                last = e.matmul(
                                    PM[bank][:, 0:nw],
                                    WS[:, k * P:(k + 1) * P],
                                    XNQ.rearrange("p (f t) -> p f t", f=NF)
                                    [:, k, nt * 512:nt * 512 + nw],
                                    start=(k == 0), stop=(k == 7),
                                    skip_group_check=True)
                            return last
                        t_mm = Pr.run("tensor", mmfn,
                                      deps=[t_w, bXQ.w] + pm_busy[bank].wr())
                        pm_busy[bank].wrote(t_mm)
                        bWS.read(t_mm)
                        bXQ.read(t_mm)
                        t_ev = Pr.run("vector", lambda e, oc=oc, nt=nt, nw=nw,
                                      bank=bank: e.tensor_copy(
                                          KF[:, oc * 1536 + nt * 512:
                                             oc * 1536 + nt * 512 + nw],
                                          PM[bank][:, 0:nw]),
                                      deps=[t_mm] +
                                      (bKF.wr() if oc == 0 and nt == 0 else []))
                        pm_busy[bank].read(t_ev)
                        if oc == 0 and nt == 0:
                            bKF.wrote(t_ev)
                        else:
                            bKF.piece(t_ev)
                nvc = (nspan + P - 1) // P
                t_wv = Pr.dma("sync", VW.rearrange("p (k o) -> p k o", o=512),
                              wv_bf.rearrange("(k p) o -> p k o", p=P),
                              deps=[cast_ready[id(wv_bf)]] + bSS.wr() +
                              bPP.wr() + bPT.wr())
                bSS.wrote(t_wv)
                bPP.wrote(t_wv)
                bPT.wrote(t_wv)
                for tc_ in range(nvc):
                    bank = 2 + (tc_ % 2)
                    def mmv(e, tc_=tc_, bank=bank):
                        last = None
                        for k in range(8):
                            last = e.matmul(
                                PM[bank][:, 0:512],
                                XNQ.rearrange("p (f t) -> p f t", f=NF)
                                [:, k, tc_ * P:(tc_ + 1) * P],
                                VW[:, k * 512:(k + 1) * 512],
                                start=(k == 0), stop=(k == 7),
                                skip_group_check=True)
                        return last
                    t_mm = Pr.run("tensor", mmv,
                                  deps=[t_wv, bXQ.w] + pm_busy[bank].wr())
                    pm_busy[bank].wrote(t_mm)
                    bSS.read(t_mm)
                    bPP.read(t_mm)
                    bPT.read(t_mm)
                    bXQ.read(t_mm)
                    t_ev = Pr.run("vector", lambda e, tc_=tc_, bank=bank:
                                  e.tensor_copy(VT[:, tc_ * 512:(tc_ + 1) * 512],
                                                PM[bank][:, 0:512]),
                                  deps=[t_mm] + (bVT.wr() if tc_ == 0 else []))
                    pm_busy[bank].read(t_ev)
                    if tc_ == 0:
                        bVT.wrote(t_ev)
                    else:
                        bVT.piece(t_ev)
                for ti in range(4):
                    qoff = halo + ti * P
                    for oc in range(nkc):
                        t_w = Pr.dma("sync", WS[:, 0:1024].rearrange(
                            "p (k o) -> p k o", o=P),
                            wq_bf.rearrange("(k p) o -> p k o", p=P)
                            [:, :, oc * P:(oc + 1) * P],
                            deps=[cast_ready[id(wq_bf)]] + bWS.wr())
                        bWS.wrote(t_w)
                        bank = oc % 2
                        def mmq(e, oc=oc, qoff=qoff, bank=bank):
                            last = None
                            for k in range(8):
                                last = e.matmul(
                                    PM[bank][:, 0:P],
                                    WS[:, k * P:(k + 1) * P],
                                    XNQ.rearrange("p (f t) -> p f t", f=NF)
                                    [:, k, qoff:qoff + P],
                                    start=(k == 0), stop=(k == 7),
                                    skip_group_check=True)
                            return last
                        t_mm = Pr.run("tensor", mmq,
                                      deps=[t_w, bXQ.w] + pm_busy[bank].wr())
                        pm_busy[bank].wrote(t_mm)
                        bWS.read(t_mm)
                        bXQ.read(t_mm)
                        t_ev = Pr.run("vector", lambda e, oc=oc, bank=bank:
                                      e.tensor_copy(QT[:, oc * P:(oc + 1) * P],
                                                    PM[bank][:, 0:P]),
                                      deps=[t_mm] + (bQT.wr() if oc == 0 else []))
                        pm_busy[bank].read(t_ev)
                        if oc == 0:
                            bQT.wrote(t_ev)
                        else:
                            bQT.piece(t_ev)
                    nheads = 8
                    for h in range(nheads):
                        if mode == "s":
                            ch = h // 2
                            base = (h % 2) * 64
                        else:
                            ch = h
                            base = 0
                        for sc3 in range(win // 384):
                            bank = sc3 % 2
                            def mms(e, ch=ch, base=base, sc3=sc3, qoff=qoff,
                                    bank=bank, dqk=dqk):
                                return e.matmul(
                                    PM[bank][:, 0:384],
                                    QT[base:base + dqk, ch * P:(ch + 1) * P],
                                    KF[base:base + dqk,
                                       ch * 1536 + qoff - win + P + sc3 * 384:
                                       ch * 1536 + qoff - win + P + sc3 * 384 + 384],
                                    start=True, stop=True,
                                    skip_group_check=True)
                            t_mm = Pr.run("tensor", mms,
                                          deps=[bQT.w, bKF.w] + pm_busy[bank].wr())
                            pm_busy[bank].wrote(t_mm)
                            bQT.read(t_mm)
                            bKF.read(t_mm)
                            t_ms = Pr.run("vector", lambda e, sc3=sc3, bank=bank,
                                          scale=scale: e.scalar_tensor_tensor(
                                              out=SS[:, sc3 * 384:(sc3 + 1) * 384],
                                              in0=PM[bank][:, 0:384], scalar=scale,
                                              in1=MSK[:, sc3 * 384:(sc3 + 1) * 384],
                                              op0=OP.mult, op1=OP.add),
                                          deps=[t_mm, t_msks, t_mskl] +
                                          (bSS.wr() if sc3 == 0 else []))
                            pm_busy[bank].read(t_ms)
                            if sc3 == 0:
                                bSS.wrote(t_ms)
                            else:
                                bSS.piece(t_ms)
                        clean0 = max(0, pad - (qoff - win + P))
                        if clean0 > 0:
                            t_pz = Pr.run("vector",
                                          lambda e, clean0=clean0: e.memset(
                                              SS[:, 0:clean0], -1e30),
                                          deps=[bSS.w])
                            bSS.wrote(t_pz)
                        t_nm = Pr.run("vector", lambda e: e.tensor_reduce(
                            out=NM, in_=SS[:, 0:win], axis=AX.X, op=OP.max,
                            negate=True), deps=[bSS.w] + bRS.wr())
                        bRS.wrote(t_nm)
                        bSS.read(t_nm)
                        t_ex = Pr.run("scalar", lambda e: e.activation(
                            out=SS[:, 0:win], in_=SS[:, 0:win], func=AF.Exp,
                            bias=NM, accum_out=SM), deps=[t_nm])
                        bSS.wrote(t_ex)
                        bRS.wrote(t_ex)
                        t_rs = Pr.run("vector", lambda e: e.reciprocal(
                            out=SM, in_=SM), deps=[t_ex])
                        bRS.wrote(t_rs)
                        t_pn = Pr.run("vector", lambda e: e.tensor_scalar(
                            out=PPB[:, 0:win], in0=SS[:, 0:win], scalar1=SM,
                            scalar2=None, op0=OP.mult),
                            deps=[t_rs] + bPP.wr())
                        bPP.wrote(t_pn)
                        bSS.read(t_pn)
                        bRS.read(t_pn)
                        def trp(e):
                            last = None
                            for j in range(win // P):
                                dst = (PTB[:, j * P:(j + 1) * P] if j < 8
                                       else PTB2[:, 0:P])
                                last = e.transpose(dst,
                                                   PPB[:, j * P:(j + 1) * P],
                                                   IDB[:])
                            return last
                        t_tr = Pr.run("tensor", trp,
                                      deps=[t_pn, t_idb] + ptb_busy[0].wr() +
                                      (ptb_busy[1].wr() if win > 1024 else []))
                        ptb_busy[0].wrote(t_tr)
                        if win > 1024:
                            ptb_busy[1].wrote(t_tr)
                        bPP.read(t_tr)
                        def pev(e):
                            r = e.tensor_copy(PT[:, 0:min(win, 1024)],
                                              PTB[:, 0:min(win, 1024)])
                            if win > 1024:
                                r = e.tensor_copy(PT[:, 1024:win],
                                                  PTB2[:, 0:P])
                            return r
                        t_pe = Pr.run("vector", pev, deps=[t_tr] + bPT.wr())
                        ptb_busy[0].read(t_pe)
                        if win > 1024:
                            ptb_busy[1].read(t_pe)
                        bPT.wrote(t_pe)
                        obank = 2 + ((h // 2) % 2)
                        obase = (h % 2) * 64
                        def mmo(e, h=h, qoff=qoff, obank=obank, obase=obase):
                            last = None
                            nkt = win // P
                            for j in range(nkt):
                                kt = (qoff - win + P) // P + j
                                last = e.matmul(
                                    PM[obank][obase:obase + 64, 0:P],
                                    VT[:, kt * 512 + h * 64:kt * 512 + h * 64 + 64],
                                    PT[:, j * P:(j + 1) * P],
                                    start=(j == 0), stop=(j == nkt - 1),
                                    skip_group_check=True)
                            return last
                        t_om = Pr.run("tensor", mmo,
                                      deps=[t_pe, bVT.w] + pm_busy[obank].wr())
                        pm_busy[obank].wrote(t_om)
                        bPT.read(t_om)
                        bVT.read(t_om)
                        t_oe = Pr.run("vector", lambda e, h=h, ti=ti, obank=obank,
                                      obase=obase: e.tensor_copy(
                                          OSX[obase:obase + 64,
                                              (h // 2) * 512 + ti * P:
                                              (h // 2) * 512 + ti * P + P],
                                          PM[obank][obase:obase + 64, 0:P]),
                                      deps=[t_om] +
                                      (bOS.wr() if h == 0 and ti == 0 and
                                       mode == "s" else []))
                        pm_busy[obank].read(t_oe)
                        if h == 0 and ti == 0 and mode == "s":
                            bOS.wrote(t_oe)
                        else:
                            bOS.piece(t_oe)

            for qr in range(4):
                t0 = qr * 512
                attn_pass(qr, "s")
                attn_pass(qr, "l")
                for oc in range(8):
                    t_w = Pr.dma("sync", WS[:, 0:1024].rearrange(
                        "p (k o) -> p k o", o=P),
                        l2pj_bf.rearrange("(k p) o -> p k o", p=P)
                        [:, :, oc * P:(oc + 1) * P],
                        deps=[cast_ready[id(l2pj_bf)]] + bWS.wr())
                    bWS.wrote(t_w)
                    bank = oc % 2
                    def mmp(e, bank=bank):
                        last = None
                        for k in range(8):
                            rhs = (OSS[:, (k % 4) * 512:(k % 4) * 512 + 512]
                                   if k < 4 else
                                   OSL[:, (k - 4) * 512:(k - 4) * 512 + 512])
                            last = e.matmul(PM[bank][:, 0:512],
                                            WS[:, k * P:(k + 1) * P], rhs,
                                            start=(k == 0), stop=(k == 7),
                                            skip_group_check=True)
                        return last
                    t_mm = Pr.run("tensor", mmp,
                                  deps=[t_w, bOS.w] + pm_busy[bank].wr())
                    pm_busy[bank].wrote(t_mm)
                    bWS.read(t_mm)
                    bOS.read(t_mm)
                    t_se = Pr.run("scalar", lambda e, oc=oc, t0=t0, bank=bank:
                                  e.activation(
                                      out=S[:, oc * T + t0:oc * T + t0 + 512],
                                      in_=PM[bank][:, 0:512], func=AF.Copy),
                                  deps=[t_mm] + bS.wr())
                    pm_busy[bank].read(t_se)
                    bS.piece(t_se)

            t_ro = Pr.dma("gpsimd", ROUT.rearrange("p (k o) -> p k o", o=8),
                          l2_rout_T.rearrange("(k p) o -> p k o", p=P),
                          deps=bPP.wr())
            bPP.wrote(t_ro)
            t_be = Pr.dma("gpsimd", BEXB, l2_bexp_T, deps=bQT.wr())
            bQT.wrote(t_be)
            t_z1 = Pr.run("vector", lambda e: e.memset(
                pv(C_RS, 512), 0.0), deps=bRS.wr())
            bRS.wrote(t_z1)
            t_z2 = Pr.run("vector", lambda e: e.memset(
                pv(C_MS + 64, 128), 0.0), deps=bMS.wr())
            bMS.wrote(t_z2)
            for tc in range(4):
                t0 = tc * 512
                abb = load_ab(t0, 512, t_ab)
                for f in range(NF):
                    def fn(e, f=f, t0=t0):
                        e.tensor_tensor(out=MS, in0=X[:, f * T + t0:f * T + t0 + 512],
                                        in1=AB[:, 0:512], op=OP.mult)
                        return e.tensor_tensor(
                            out=XN5[:, f * 512:(f + 1) * 512],
                            in0=MS, in1=AB[:, 512:1024], op=OP.add)
                    tk = Pr.run("vector", fn,
                                deps=abb + bX.rd() + bMS.wr() +
                                (bXQ.wr() if f == 0 else []))
                    bMS.wrote(tk)
                    bX.read(tk)
                    bAB.read(tk)
                    if f == 0:
                        bXQ.wrote(tk)
                    else:
                        bXQ.piece(tk)
                def mmr(e):
                    last = None
                    for k in range(8):
                        last = e.matmul(PF0[0:8, 0:512], ROUT[:, k * 8:(k + 1) * 8],
                                        XN5[:, k * 512:(k + 1) * 512],
                                        start=(k == 0), stop=(k == 7),
                                        skip_group_check=True)
                    return last
                t_lg = Pr.run("tensor", mmr,
                              deps=[t_ro, bXQ.w] + pf_busy[0].wr())
                pf_busy[0].wrote(t_lg)
                bXQ.read(t_lg)
                bPP.read(t_lg)
                t_le = Pr.run("vector", lambda e: e.tensor_tensor(
                    out=LG, in0=PF0[0:8, 0:512],
                    in1=RBIAS[:].to_broadcast([8, 512]), op=OP.add),
                    deps=[t_lg, t_rb, t_z1] + bRS.wr())
                bRS.wrote(t_le)
                pf_busy[0].read(t_le)
                for tch in range(4):
                    t_tr = Pr.run("tensor", lambda e, tch=tch: e.transpose(
                        PF1[0:P, 0:P],
                        pv(C_RS, 512)[:, tch * P:(tch + 1) * P],
                        IDF[:]), deps=[t_le, t_idf] + pf_busy[1].wr())
                    pf_busy[1].wrote(t_tr)
                    bRS.read(t_tr)
                    t_ge = Pr.run("vector", lambda e: e.tensor_copy(
                        GT, PF1[0:P, 0:8]), deps=[t_tr, t_z2] + bMS.wr())
                    pf_busy[1].read(t_ge)
                    bMS.wrote(t_ge)
                    t_t1 = Pr.run("vector", lambda e: e.tensor_reduce(
                        out=pv(C_MS, 1), in_=GT, axis=AX.X, op=OP.max),
                        deps=[t_ge])
                    bMS.wrote(t_t1)
                    t_t2 = Pr.run("vector", lambda e: e.tensor_scalar(
                        out=pv(C_MS + 8, 8), in0=GT, scalar1=pv(C_MS, 1),
                        scalar2=None, op0=OP.is_ge), deps=[t_t1])
                    bMS.wrote(t_t2)
                    t_t3 = Pr.run("vector", lambda e: e.tensor_tensor_scan(
                        out=pv(C_MS + 16, 8), data0=ONES8[:, 0:8],
                        data1=pv(C_MS + 8, 8), initial=0.0,
                        op0=OP.mult, op1=OP.add), deps=[t_t2, t_o8])
                    bMS.wrote(t_t3)
                    t_t4a = Pr.run("vector", lambda e: e.tensor_scalar(
                        out=pv(C_MS + 24, 8), in0=pv(C_MS + 16, 8),
                        scalar1=-1.0, scalar2=2.0, op0=OP.mult, op1=OP.add),
                        deps=[t_t3])
                    bMS.wrote(t_t4a)
                    t_t4b = Pr.run("vector", lambda e: e.tensor_scalar(
                        out=pv(C_MS + 24, 8), in0=pv(C_MS + 24, 8),
                        scalar1=0.0, scalar2=None, op0=OP.max), deps=[t_t4a])
                    bMS.wrote(t_t4b)
                    t_t4 = Pr.run("vector", lambda e: e.tensor_tensor(
                        out=pv(C_MS + 48, 8), in0=pv(C_MS + 24, 8),
                        in1=pv(C_MS + 8, 8), op=OP.mult), deps=[t_t4b])
                    bMS.wrote(t_t4)
                    t_t5 = Pr.run("vector", lambda e: e.scalar_tensor_tensor(
                        out=pv(C_MS + 24, 8), in0=pv(C_MS + 48, 8),
                        scalar=-1e30, in1=GT, op0=OP.mult, op1=OP.add),
                        deps=[t_t4])
                    bMS.wrote(t_t5)
                    t_t6 = Pr.run("vector", lambda e: e.tensor_reduce(
                        out=pv(C_MS, 1), in_=pv(C_MS + 24, 8), axis=AX.X,
                        op=OP.max), deps=[t_t5])
                    bMS.wrote(t_t6)
                    t_t7 = Pr.run("vector", lambda e: e.tensor_scalar(
                        out=pv(C_MS + 8, 8), in0=pv(C_MS + 24, 8),
                        scalar1=pv(C_MS, 1), scalar2=None, op0=OP.is_ge),
                        deps=[t_t6])
                    bMS.wrote(t_t7)
                    t_t8 = Pr.run("vector", lambda e: e.tensor_tensor_scan(
                        out=pv(C_MS + 16, 8), data0=ONES8[:, 0:8],
                        data1=pv(C_MS + 8, 8), initial=0.0,
                        op0=OP.mult, op1=OP.add), deps=[t_t7, t_o8])
                    bMS.wrote(t_t8)
                    t_t9a = Pr.run("vector", lambda e: e.tensor_scalar(
                        out=pv(C_MS + 32, 8), in0=pv(C_MS + 16, 8),
                        scalar1=-1.0, scalar2=2.0, op0=OP.mult, op1=OP.add),
                        deps=[t_t8])
                    bMS.wrote(t_t9a)
                    t_t9b = Pr.run("vector", lambda e: e.tensor_scalar(
                        out=pv(C_MS + 32, 8), in0=pv(C_MS + 32, 8),
                        scalar1=0.0, scalar2=None, op0=OP.max), deps=[t_t9a])
                    bMS.wrote(t_t9b)
                    t_t9 = Pr.run("vector", lambda e: e.tensor_tensor(
                        out=pv(C_MS + 40, 8), in0=pv(C_MS + 32, 8),
                        in1=pv(C_MS + 8, 8), op=OP.mult), deps=[t_t9b])
                    bMS.wrote(t_t9)
                    t_sl = Pr.run("vector", lambda e: e.tensor_tensor(
                        out=GT, in0=pv(C_MS + 48, 8), in1=pv(C_MS + 40, 8),
                        op=OP.add), deps=[t_t9])
                    bMS.wrote(t_sl)
                    t_tb = Pr.run("tensor", lambda e, tch=tch: e.transpose(
                        PF1[0:P, 0:P], pv(C_MS + 64, 128), IDF[:]),
                        deps=[t_sl, t_idf] + pf_busy[1].wr())
                    pf_busy[1].wrote(t_tb)
                    bMS.read(t_tb)
                    t_sf = Pr.run("vector", lambda e, tch=tch: e.tensor_copy(
                        SELFB[:, tch * P:(tch + 1) * P], PF1[0:8, 0:P]),
                        deps=[t_tb] + (bVT.wr() if tch == 0 else []))
                    pf_busy[1].read(t_sf)
                    if tch == 0:
                        bVT.wrote(t_sf)
                    else:
                        bVT.piece(t_sf)
                t_sw = Pr.dma("gpsimd", selm_dr[0:8, t0:t0 + 512], SELFB,
                              deps=[bVT.w])
                bVT.read(t_sw)
                for half in range(2):
                    mm_last = None
                    for e_ in range(8):
                        t_mk = Pr.dma("sync", MASKE,
                                      selm_dr[e_:e_ + 1, t0:t0 + 512]
                                      .to_broadcast([P, 512]),
                                      deps=[t_sw] + bKF.wr())
                        bKF.wrote(t_mk)
                        me_toks = []
                        for k in range(8):
                            tk = Pr.run("vector", lambda e, k=k: e.tensor_tensor(
                                out=ME[:, k * 512:(k + 1) * 512],
                                in0=XN5[:, k * 512:(k + 1) * 512],
                                in1=MASKE, op=OP.mult),
                                deps=[t_mk, bXQ.w] +
                                (bKF.wr() if k == 0 else []))
                            bKF.piece(tk)
                            bXQ.read(tk)
                            me_toks.append(tk)
                        for oi in range(4):
                            oc = half * 4 + oi
                            t_w = Pr.dma("sync", WS[:, 0:1024].rearrange(
                                "p (k o) -> p k o", o=P),
                                l2we_bf.rearrange("(k p) o -> p k o", p=P)
                                [:, e_ * 8:(e_ + 1) * 8, oc * P:(oc + 1) * P],
                                deps=[cast_ready[id(l2we_bf)]] + bWS.wr())
                            bWS.wrote(t_w)
                            def mme(e, oi=oi, e_=e_):
                                last = None
                                for k in range(8):
                                    last = e.matmul(
                                        PM[oi][:, 0:512],
                                        WS[:, k * P:(k + 1) * P],
                                        ME[:, k * 512:(k + 1) * 512],
                                        start=(e_ == 0 and k == 0), stop=False,
                                        skip_group_check=True)
                                return last
                            mm_last = Pr.run("tensor", mme,
                                             deps=[t_w, me_toks[-1]] +
                                             (pm_busy[oi].wr()
                                              if e_ == 0 else []))
                            pm_busy[oi].wrote(mm_last)
                            bWS.read(mm_last)
                            bKF.read(mm_last)
                    for oi in range(4):
                        oc = half * 4 + oi
                        t_bm = Pr.run("tensor", lambda e, oi=oi, oc=oc: e.matmul(
                            PM[oi][:, 0:512], BEXB[:, oc * P:(oc + 1) * P],
                            SELFB, start=False, stop=True,
                            skip_group_check=True),
                            deps=[t_be, bVT.w, mm_last] + pm_busy[oi].wr())
                        pm_busy[oi].wrote(t_bm)
                        bVT.read(t_bm)
                        t_gs = Pr.run("scalar", lambda e, oi=oi: e.activation(
                            out=MS, in_=PM[oi][:, 0:512], func=AF.Silu),
                            deps=[t_bm] + bMS.wr())
                        bMS.wrote(t_gs)
                        pm_busy[oi].read(t_gs)
                        def upd(e, oc=oc, t0=t0):
                            e.tensor_tensor(out=MS, in0=MS,
                                            in1=S[:, oc * T + t0:oc * T + t0 + 512],
                                            op=OP.mult)
                            return e.tensor_tensor(
                                out=X[:, oc * T + t0:oc * T + t0 + 512],
                                in0=X[:, oc * T + t0:oc * T + t0 + 512],
                                in1=MS, op=OP.add)
                        t_up = Pr.run("vector", upd, deps=[t_gs, bS.w] + bX.rd())
                        bMS.wrote(t_up)
                        bS.read(t_up)
                        bX.piece(t_up)

            return joinb(*bufs.values())


        def layer3(join_in):
            A_XNQ, A_KB, A_VT = 0, 4096, 8192
            A_CS, A_RP, A_QB = 12288, 14336, 16384
            A_SSC, A_PPC, A_PTC, A_OSC = 16896, 17536, 17856, 18176
            A_WSC, A_RWC, A_MSC = 20224, 21248, 21312
            A_MSK = 21824

            XNQ = pv(A_XNQ, NF * 1024, BF16)
            KB = pv(A_KB, NF * 1024, BF16)
            VT = pv(A_VT, NF * 1024, BF16)
            CCS = pv(A_CS, 1024)
            SSS = pv(A_CS + 1024, 1024)
            QF = pv(A_RP, 1024)
            QRF = pv(A_RP + 1024, 1024)
            VWS = pv(A_RP, 4096, BF16)
            QB = pv(A_QB, NF * 128, BF16)
            SS = pv(A_SSC, 640)
            PPB = pv(A_PPC, 640, BF16)
            PT = pv(A_PTC, 640, BF16)
            OS = pv(A_OSC, NF * 512, BF16)
            WS = pv(A_WSC, 2048, BF16)
            NM = pv(A_RWC, 1)
            SM = pv(A_RWC + 4, 1)
            MS = pv(A_MSC, 512)
            MSK3 = pv(A_MSK, 640)

            bufs = {nm: BufT() for nm in
                    ("XNQ", "KB", "VT", "CS", "RP", "QB", "SS", "PP", "PT",
                     "OS", "WS", "RW", "MS")}
            for b in bufs.values():
                for tok in join_in:
                    b._note(tok)
            bXQ = bufs["XNQ"]; bKB = bufs["KB"]; bVT = bufs["VT"]
            bCS = bufs["CS"]; bRP = bufs["RP"]; bQB = bufs["QB"]
            bSS = bufs["SS"]; bPP = bufs["PP"]; bPT = bufs["PT"]
            bOS = bufs["OS"]; bWS = bufs["WS"]; bRW = bufs["RW"]
            bMS = bufs["MS"]

            t_ab = layernorm(A_XNQ, extra_deps=join_in)
            t_msk3 = Pr.dma("sync", MSK3[:], l3_mask, deps=list(join_in))

            def build_xnq(v0, span, extra):
                for si in range((span + 511) // 512):
                    c0 = v0 + si * 512
                    cw = min(512, v0 + span - c0)
                    abb = load_ab(c0, cw, t_ab)
                    for f in range(NF):
                        def fn(e, f=f, c0=c0, cw=cw, si=si, extra=extra):
                            e.tensor_tensor(out=MS[:, 0:cw],
                                            in0=X[:, f * T + c0:f * T + c0 + cw],
                                            in1=AB[:, 0:cw], op=OP.mult)
                            return e.tensor_tensor(
                                out=XNQ[:, f * 1024 + extra + si * 512:
                                        f * 1024 + extra + si * 512 + cw],
                                in0=MS[:, 0:cw], in1=AB[:, 512:512 + cw],
                                op=OP.add)
                        tk = Pr.run("vector", fn,
                                    deps=abb + bX.rd() + bMS.wr() +
                                    (bXQ.wr() if si == 0 and f == 0 else []))
                        bMS.wrote(tk)
                        bX.read(tk)
                        bAB.read(tk)
                        if si == 0 and f == 0:
                            bXQ.wrote(tk)
                        else:
                            bXQ.piece(tk)

            def rope3(wreg, wreg2, out_ap, cs0, cw, deps):
                def fn(e, cs0=cs0, cw=cw):
                    e.tensor_tensor(out=QF[:, 0:cw], in0=QF[:, 0:cw],
                                    in1=CCS[:, cs0:cs0 + cw], op=OP.mult)
                    e.tensor_tensor(out=QRF[:, 0:cw], in0=QRF[:, 0:cw],
                                    in1=SSS[:, cs0:cs0 + cw], op=OP.mult)
                    return e.tensor_tensor(out=out_ap, in0=QF[:, 0:cw],
                                           in1=QRF[:, 0:cw], op=OP.add)
                return Pr.run("vector", fn, deps=deps)

            WIN3 = 640
            for qr in range(4):
                t0 = qr * 512
                halo = 512
                span = 1024
                v0 = t0 - halo
                pad = -v0 if v0 < 0 else 0
                v0 = max(v0, 0)
                if pad > 0:
                    def zf(e, pad=pad):
                        last = None
                        for f in range(NF):
                            last = e.memset(XNQ[:, f * 1024:f * 1024 + pad], 0.0)
                        return last
                    tz = Pr.run("vector", zf, deps=bXQ.wr() + [t_ab])
                    bXQ.wrote(tz)
                build_xnq(v0, span - pad, pad)
                t_cs = Pr.dma("sync", CCS[:, pad:1024], l3_cc[:, v0:t0 + 512],
                              deps=bCS.wr())
                t_cs2 = Pr.dma("sync", SSS[:, pad:1024], l3_ssn[:, v0:t0 + 512],
                               deps=[t_cs])
                bCS.wrote(t_cs2)
                if pad > 0:
                    t_csz = Pr.run("vector", lambda e, pad=pad: (
                        e.memset(CCS[:, 0:pad], 0.0),
                        e.memset(SSS[:, 0:pad], 0.0))[-1],
                        deps=[t_cs2])
                    bCS.wrote(t_csz)

                for oc in range(NF):
                    t_w = Pr.dma("sync", WS[:, 0:1024].rearrange(
                        "p (k o) -> p k o", o=P),
                        l3qkv_bf.rearrange("(k p) o -> p k o", p=P)
                        [:, :, 2048 + oc * P:2048 + (oc + 1) * P],
                        deps=[cast_ready[id(l3qkv_bf)]] + bWS.wr())
                    t_w2 = Pr.dma("sync", WS[:, 1024:2048].rearrange(
                        "p (k o) -> p k o", o=P),
                        l3qkv_bf.rearrange("(k p) o -> p k o", p=P)
                        [:, :, 3072 + oc * P:3072 + (oc + 1) * P],
                        deps=[t_w])
                    bWS.wrote(t_w2)
                    ev_toks = []
                    for nt in range(2):
                        def mmk(e, nt=nt, kr=0):
                            last = None
                            for k in range(8):
                                last = e.matmul(
                                    PM[nt + 2 * kr][:, 0:512],
                                    WS[:, kr * 1024 + k * P:
                                       kr * 1024 + (k + 1) * P],
                                    XNQ.rearrange("p (f t) -> p f t", f=NF)
                                    [:, k, nt * 512:(nt + 1) * 512],
                                    start=(k == 0), stop=(k == 7),
                                    skip_group_check=True)
                            return last
                        for kr in range(2):
                            bank = nt + 2 * kr
                            t_mm = Pr.run("tensor",
                                          lambda e, nt=nt, kr=kr: mmk(e, nt, kr),
                                          deps=[t_w2, bXQ.w] +
                                          pm_busy[bank].wr())
                            pm_busy[bank].wrote(t_mm)
                            bWS.read(t_mm)
                            bXQ.read(t_mm)
                            dst = QF if kr == 0 else QRF
                            t_ev = Pr.run("vector",
                                          lambda e, nt=nt, bank=bank, dst=dst:
                                          e.tensor_copy(
                                              dst[:, nt * 512:(nt + 1) * 512],
                                              PM[bank][:, 0:512]),
                                          deps=[t_mm] +
                                          (bRP.wr() if nt == 0 and kr == 0
                                           else []))
                            pm_busy[bank].read(t_ev)
                            if nt == 0 and kr == 0:
                                bRP.wrote(t_ev)
                            else:
                                bRP.piece(t_ev)
                            ev_toks.append(t_ev)
                    t_rp = rope3(None, None, KB[:, oc * 1024:(oc + 1) * 1024],
                                 0, 1024,
                                 ev_toks + [bCS.w] +
                                 (bKB.wr() if oc == 0 else []))
                    bRP.wrote(t_rp)
                    bCS.read(t_rp)
                    if oc == 0:
                        bKB.wrote(t_rp)
                    else:
                        bKB.piece(t_rp)

                for half in range(2):
                    t_wv = Pr.dma("sync",
                                  VWS.rearrange("p (k o) -> p k o", o=512),
                                  l3qkv_bf.rearrange("(k p) o -> p k o", p=P)
                                  [:, :, 4096 + half * 512:4096 + (half + 1) * 512],
                                  deps=[cast_ready[id(l3qkv_bf)]] + bRP.wr())
                    bRP.wrote(t_wv)
                    for tch in range(8):
                        bank = tch % 2
                        def mmv(e, tch=tch, bank=bank):
                            last = None
                            for k in range(8):
                                last = e.matmul(
                                    PM[bank][:, 0:512],
                                    XNQ.rearrange("p (f t) -> p f t", f=NF)
                                    [:, k, tch * P:(tch + 1) * P],
                                    VWS[:, k * 512:(k + 1) * 512],
                                    start=(k == 0), stop=(k == 7),
                                    skip_group_check=True)
                            return last
                        t_mm = Pr.run("tensor", mmv,
                                      deps=[t_wv, bXQ.w] + pm_busy[bank].wr())
                        pm_busy[bank].wrote(t_mm)
                        bRP.read(t_mm)
                        bXQ.read(t_mm)
                        t_ev = Pr.run("vector",
                                      lambda e, tch=tch, half=half, bank=bank:
                                      e.tensor_copy(
                                          VT[:, tch * 1024 + half * 512:
                                             tch * 1024 + half * 512 + 512],
                                          PM[bank][:, 0:512]),
                                      deps=[t_mm] +
                                      (bVT.wr() if tch == 0 and half == 0
                                       else []))
                        pm_busy[bank].read(t_ev)
                        if tch == 0 and half == 0:
                            bVT.wrote(t_ev)
                        else:
                            bVT.piece(t_ev)

                for ti in range(4):
                    qoff = halo + ti * P
                    for oc in range(NF):
                        t_w = Pr.dma("sync", WS[:, 0:1024].rearrange(
                            "p (k o) -> p k o", o=P),
                            l3qkv_bf.rearrange("(k p) o -> p k o", p=P)
                            [:, :, oc * P:(oc + 1) * P],
                            deps=[cast_ready[id(l3qkv_bf)]] + bWS.wr())
                        t_w2 = Pr.dma("sync", WS[:, 1024:2048].rearrange(
                            "p (k o) -> p k o", o=P),
                            l3qkv_bf.rearrange("(k p) o -> p k o", p=P)
                            [:, :, 1024 + oc * P:1024 + (oc + 1) * P],
                            deps=[t_w])
                        bWS.wrote(t_w2)
                        ev_toks = []
                        for kr in range(2):
                            bank = kr
                            def mmq(e, oc=oc, qoff=qoff, kr=kr, bank=bank):
                                last = None
                                for k in range(8):
                                    last = e.matmul(
                                        PM[bank][:, 0:P],
                                        WS[:, kr * 1024 + k * P:
                                           kr * 1024 + (k + 1) * P],
                                        XNQ.rearrange("p (f t) -> p f t", f=NF)
                                        [:, k, qoff:qoff + P],
                                        start=(k == 0), stop=(k == 7),
                                        skip_group_check=True)
                                return last
                            t_mm = Pr.run("tensor", mmq,
                                          deps=[t_w2, bXQ.w] +
                                          pm_busy[bank].wr())
                            pm_busy[bank].wrote(t_mm)
                            bWS.read(t_mm)
                            bXQ.read(t_mm)
                            dst = QF if kr == 0 else QRF
                            t_ev = Pr.run("vector",
                                          lambda e, bank=bank, dst=dst:
                                          e.tensor_copy(dst[:, 0:P],
                                                        PM[bank][:, 0:P]),
                                          deps=[t_mm] + bRP.wr())
                            pm_busy[bank].read(t_ev)
                            bRP.wrote(t_ev)
                            ev_toks.append(t_ev)
                        t_rp = rope3(None, None, QB[:, oc * P:(oc + 1) * P],
                                     qoff, P,
                                     ev_toks + [bCS.w] +
                                     (bQB.wr() if oc == 0 else []))
                        bRP.wrote(t_rp)
                        bCS.read(t_rp)
                        if oc == 0:
                            bQB.wrote(t_rp)
                        else:
                            bQB.piece(t_rp)
                    for h in range(16):
                        ch = h // 2
                        base = (h % 2) * 64
                        s0 = qoff - WIN3 + P
                        for sc, (c0, cw) in enumerate(((0, 512), (512, 128))):
                            bank = sc % 2
                            def mms(e, ch=ch, base=base, s0=s0, c0=c0, cw=cw,
                                    bank=bank):
                                return e.matmul(
                                    PM[bank][:, 0:cw],
                                    QB[base:base + 64, ch * P:(ch + 1) * P],
                                    KB[base:base + 64,
                                       ch * 1024 + s0 + c0:
                                       ch * 1024 + s0 + c0 + cw],
                                    start=True, stop=True,
                                    skip_group_check=True)
                            t_mm = Pr.run("tensor", mms,
                                          deps=[bQB.w, bKB.w] +
                                          pm_busy[bank].wr())
                            pm_busy[bank].wrote(t_mm)
                            bQB.read(t_mm)
                            bKB.read(t_mm)
                            t_ms = Pr.run("vector",
                                          lambda e, c0=c0, cw=cw, bank=bank:
                                          e.scalar_tensor_tensor(
                                              out=SS[:, c0:c0 + cw],
                                              in0=PM[bank][:, 0:cw],
                                              scalar=0.125,
                                              in1=MSK3[:, c0:c0 + cw],
                                              op0=OP.mult, op1=OP.add),
                                          deps=[t_mm, t_msk3] +
                                          (bSS.wr() if sc == 0 else []))
                            pm_busy[bank].read(t_ms)
                            if sc == 0:
                                bSS.wrote(t_ms)
                            else:
                                bSS.piece(t_ms)
                        clean0 = max(0, pad - s0)
                        if clean0 > 0:
                            t_pz = Pr.run("vector",
                                          lambda e, clean0=clean0: e.memset(
                                              SS[:, 0:clean0], -1e30),
                                          deps=[bSS.w])
                            bSS.wrote(t_pz)
                        t_nm = Pr.run("vector", lambda e: e.tensor_reduce(
                            out=NM, in_=SS[:, 0:WIN3], axis=AX.X, op=OP.max,
                            negate=True), deps=[bSS.w] + bRW.wr())
                        bRW.wrote(t_nm)
                        bSS.read(t_nm)
                        t_ex = Pr.run("scalar", lambda e: e.activation(
                            out=SS[:, 0:WIN3], in_=SS[:, 0:WIN3], func=AF.Exp,
                            bias=NM, accum_out=SM), deps=[t_nm])
                        bSS.wrote(t_ex)
                        bRW.wrote(t_ex)
                        t_rs = Pr.run("vector", lambda e: e.reciprocal(
                            out=SM, in_=SM), deps=[t_ex])
                        bRW.wrote(t_rs)
                        t_pn = Pr.run("vector", lambda e: e.tensor_scalar(
                            out=PPB[:, 0:WIN3], in0=SS[:, 0:WIN3], scalar1=SM,
                            scalar2=None, op0=OP.mult),
                            deps=[t_rs] + bPP.wr())
                        bPP.wrote(t_pn)
                        bSS.read(t_pn)
                        bRW.read(t_pn)
                        def trp(e):
                            last = None
                            for j in range(5):
                                last = e.transpose(PTB[:, j * P:(j + 1) * P],
                                                   PPB[:, j * P:(j + 1) * P],
                                                   IDB[:])
                            return last
                        t_tr = Pr.run("tensor", trp,
                                      deps=[t_pn, t_idb] + ptb_busy[0].wr())
                        ptb_busy[0].wrote(t_tr)
                        bPP.read(t_tr)
                        t_pe = Pr.run("vector", lambda e: e.tensor_copy(
                            PT[:, 0:WIN3], PTB[:, 0:WIN3]),
                            deps=[t_tr] + bPT.wr())
                        ptb_busy[0].read(t_pe)
                        bPT.wrote(t_pe)
                        obank = 2 + ((h // 2) % 2)
                        obase = (h % 2) * 64
                        def mmo(e, h=h, s0=s0, obank=obank, obase=obase):
                            last = None
                            for j in range(5):
                                kt = (s0 + j * P) // P
                                last = e.matmul(
                                    PM[obank][obase:obase + 64, 0:P],
                                    VT[:, kt * 1024 + h * 64:
                                       kt * 1024 + h * 64 + 64],
                                    PT[:, j * P:(j + 1) * P],
                                    start=(j == 0), stop=(j == 4),
                                    skip_group_check=True)
                            return last
                        t_om = Pr.run("tensor", mmo,
                                      deps=[t_pe, bVT.w] + pm_busy[obank].wr())
                        pm_busy[obank].wrote(t_om)
                        bPT.read(t_om)
                        bVT.read(t_om)
                        t_oe = Pr.run("vector",
                                      lambda e, h=h, ti=ti, obank=obank,
                                      obase=obase: e.tensor_copy(
                                          OS[obase:obase + 64,
                                             (h // 2) * 512 + ti * P:
                                             (h // 2) * 512 + ti * P + P],
                                          PM[obank][obase:obase + 64, 0:P]),
                                      deps=[t_om] +
                                      (bOS.wr() if h == 0 and ti == 0 else []))
                        pm_busy[obank].read(t_oe)
                        if h == 0 and ti == 0:
                            bOS.wrote(t_oe)
                        else:
                            bOS.piece(t_oe)
                for oc in range(NF):
                    t_w = Pr.dma("sync", WS[:, 0:1024].rearrange(
                        "p (k o) -> p k o", o=P),
                        l3pj_bf.rearrange("(k p) o -> p k o", p=P)
                        [:, :, oc * P:(oc + 1) * P],
                        deps=[cast_ready[id(l3pj_bf)]] + bWS.wr())
                    bWS.wrote(t_w)
                    bank = oc % 2
                    def mmp(e, bank=bank):
                        last = None
                        for k in range(8):
                            last = e.matmul(PM[bank][:, 0:512],
                                            WS[:, k * P:(k + 1) * P],
                                            OS[:, k * 512:(k + 1) * 512],
                                            start=(k == 0), stop=(k == 7),
                                            skip_group_check=True)
                        return last
                    t_mm = Pr.run("tensor", mmp,
                                  deps=[t_w, bOS.w] + pm_busy[bank].wr())
                    pm_busy[bank].wrote(t_mm)
                    bWS.read(t_mm)
                    bOS.read(t_mm)
                    t_se = Pr.run("vector", lambda e, oc=oc, t0=t0, bank=bank:
                                  e.tensor_scalar(
                                      out=S[:, oc * T + t0:oc * T + t0 + 512],
                                      in0=PM[bank][:, 0:512],
                                      scalar1=BP3[:, oc:oc + 1], scalar2=None,
                                      op0=OP.add),
                                  deps=[t_mm, t_bp3] + bS.wr())
                    pm_busy[bank].read(t_se)
                    bS.piece(t_se)

            attn_join = joinb(*bufs.values())

            F_XN, F_UV, F_LG, F_FPK = 0, 2048, 4096, 6144
            F_RT, F_E1, F_RE, F_BF = 7168, 8192, 9216, 10240
            F_OHW, F_OH0, F_U13, F_XGC = 11264, 13312, 15360, 15872
            F_PKG, F_STR, F_HS, F_HRB = 15936, 16960, 19008, 19520
            F_GG, F_GFM, F_HB, F_RW = 19808, 20832, 20896, 22944
            assert F_RW + 120 <= 23100

            XN3 = pv(F_XN, NF * 512, BF16)
            LG3 = pv(F_LG, 2048)
            FPK = pv(F_FPK, 2048, BF16, 0, 32)
            RT = F_RT
            E0 = pv(F_RT, 2048, BF16, 0, 8)
            E1 = pv(F_E1, 2048, BF16, 0, 8)
            RE = pv(F_RE, 2048, BF16, 0, 64)
            BLKF = pv(F_BF, 2048, BF16, 0, 64)
            U1 = pv(F_U13, 512, BF16, 0, 64)
            U3 = pv(F_U13 + 256, 512, BF16, 0, 64)
            XGCB = pv(F_XGC, 64, BF16)
            PKG = pv(F_PKG, 1024)
            HS = pv(F_HS, 512)
            UT8 = pv(F_HS, 512, BF16, 0, 8)
            HRB = pv(F_HRB, 512, BF16)
            OTT = pv(F_HRB + 256, 64, BF16)
            GG = pv(F_GG, 1024)
            GFM = pv(F_GFM, 64)
            HB = pv(F_HB, NF * 512, BF16)
            CNT = pv(F_RW, 1, F32, 0, 8)
            SROW = pv(F_RW + 8, 8, F32, 0, 1)
            SHROW = pv(F_RW + 16, 8, F32, 0, 1)
            SREG = pv(F_RW + 24, 1, F32, 0, 8)
            SHREG = pv(F_RW + 32, 1, F32, 0, 8)
            CROW = pv(F_RW + 40, 8, F32, 0, 1)
            BIAS8 = pv(F_RW + 48, 8)
            GT3W = pv(F_RW + 56, 64, BF16)
            GS = pv(F_OHW, 512)
            MS3 = pv(F_OH0, 512)

            fb = {nm: BufT() for nm in
                  ("XN", "LG", "FPK", "RT", "E", "RE", "BF", "OH", "U", "PKG",
                   "STR", "HS", "HRB", "GG", "GFM", "HB", "RW", "GS", "MS")}
            for b in fb.values():
                for tok in attn_join:
                    b._note(tok)
            fXN = fb["XN"]; fLG = fb["LG"]; fFPK = fb["FPK"]; fRT = fb["RT"]
            fE = fb["E"]; fRE = fb["RE"]; fBF = fb["BF"]; fOH = fb["OH"]
            fU = fb["U"]; fPKG = fb["PKG"]; fSTR = fb["STR"]; fHS = fb["HS"]
            fHRB = fb["HRB"]; fGG = fb["GG"]; fGFM = fb["GFM"]; fHB = fb["HB"]
            fRW = fb["RW"]; fGS = fb["GS"]; fMS = fb["MS"]

            def rv(off, n=8):
                return pv(RT + off, n)

            t_zl = Pr.run("vector", lambda e: (
                e.memset(LG3, 0.0), e.memset(pv(RT, 256), 0.0),
                e.memset(PKG, 0.0), e.memset(GG, 0.0))[-1],
                deps=fLG.wr() + fRT.wr() + fPKG.wr() + fGG.wr())
            fLG.wrote(t_zl)
            fRT.wrote(t_zl)
            fPKG.wrote(t_zl)
            fGG.wrote(t_zl)

            t_b8 = Pr.dma("gpsimd", BIAS8[:],
                          l3_ebias[0:1, :].to_broadcast([P, 8]),
                          deps=fRW.wr())
            t_g3w = Pr.dma("gpsimd", GT3W.rearrange("p (k o) -> p k o", o=8),
                           l3_gate_T.rearrange("(k p) o -> p k o", p=P),
                           deps=[t_b8])
            fRW.wrote(t_g3w)
            fRW._note(t_b8)
            for tc in range(4):
                t0 = tc * 512
                abb = load_ab(t0, 512, t_ab)
                for f in range(NF):
                    def fn(e, f=f, t0=t0):
                        e.tensor_tensor(out=MS3,
                                        in0=X[:, f * T + t0:f * T + t0 + 512],
                                        in1=AB[:, 0:512], op=OP.mult)
                        return e.tensor_tensor(
                            out=XN3[:, f * 512:(f + 1) * 512],
                            in0=MS3, in1=AB[:, 512:1024], op=OP.add)
                    tk = Pr.run("vector", fn,
                                deps=abb + bX.rd() + fMS.wr() +
                                (fXN.wr() if f == 0 else []))
                    fMS.wrote(tk)
                    bX.read(tk)
                    bAB.read(tk)
                    if f == 0:
                        fXN.wrote(tk)
                    else:
                        fXN.piece(tk)
                def mmr(e):
                    last = None
                    for k in range(8):
                        last = e.matmul(PF0[0:8, 0:512],
                                        GT3W[:, k * 8:(k + 1) * 8],
                                        XN3[:, k * 512:(k + 1) * 512],
                                        start=(k == 0), stop=(k == 7),
                                        skip_group_check=True)
                    return last
                t_lg = Pr.run("tensor", mmr,
                              deps=[t_g3w, fXN.w] + pf_busy[0].wr())
                pf_busy[0].wrote(t_lg)
                fXN.read(t_lg)
                t_le = Pr.run("vector", lambda e, t0=t0: e.tensor_copy(
                    pv(F_LG, 2048, F32, 0, 8)[:, t0:t0 + 512],
                    PF0[0:8, 0:512]),
                    deps=[t_lg] + (fLG.wr() if tc == 0 else []))
                pf_busy[0].read(t_le)
                if tc == 0:
                    fLG.wrote(t_le)
                else:
                    fLG.piece(t_le)
                for tch in range(4):
                    col = t0 + tch * P
                    t_tr = Pr.run("tensor", lambda e, col=col: e.transpose(
                        PF1[0:P, 0:P], LG3[:, col:col + P], IDF[:]),
                        deps=[fLG.w, t_idf] + pf_busy[1].wr())
                    pf_busy[1].wrote(t_tr)
                    fLG.read(t_tr)

                    GT3 = rv(0); p_ = rv(8); m_ = rv(16, 1); z_ = rv(18, 1)
                    s8 = rv(24); gs4 = rv(32, 4); gs8 = rv(40); q_ = rv(48, 1)
                    eq1 = rv(56); gsx = rv(64); eq2 = rv(72); keep = rv(80)
                    madd = rv(88); msk = rv(96); cum = rv(104); tmp = rv(112)
                    rm1 = rv(120); rm2 = rv(128); pw1 = rv(136); pw2 = rv(144)
                    cacc = rv(152); tsum = rv(160)
                    PKT = pv(RT + 168, 32, BF16)

                    def st(fn_, deps):
                        tk = Pr.run("vector", fn_, deps=deps)
                        fRT.wrote(tk)
                        return tk

                    tk = st(lambda e: e.tensor_copy(GT3, PF1[0:P, 0:8]),
                            [t_tr] + fRT.wr())
                    pf_busy[1].read(tk)
                    tk = st(lambda e: e.tensor_reduce(
                        out=m_, in_=GT3, axis=AX.X, op=OP.max, negate=True),
                        [tk])
                    tk2 = Pr.run("scalar", lambda e: e.activation(
                        out=p_, in_=GT3, func=AF.Exp, bias=m_, accum_out=z_),
                        deps=[tk])
                    fRT.wrote(tk2)
                    tk = st(lambda e: e.reciprocal(out=z_, in_=z_), [tk2])
                    tk = st(lambda e: e.tensor_scalar(
                        out=p_, in0=p_, scalar1=z_, scalar2=None, op0=OP.mult),
                        [tk])
                    tk = st(lambda e: e.tensor_tensor(
                        out=s8, in0=p_, in1=BIAS8[:], op=OP.add),
                        [tk, t_b8])
                    tk = st(lambda e: e.tensor_reduce(
                        out=gs4, in_=s8.rearrange("p (a b) -> p a b", b=2),
                        axis=AX.X, op=OP.add), [tk])
                    tk = st(lambda e: e.tensor_tensor(
                        out=gs8.rearrange("p (a b) -> p a b", b=2),
                        in0=gs4.unsqueeze(-1).to_broadcast([P, 4, 2]),
                        in1=gs4.unsqueeze(-1).to_broadcast([P, 4, 2]),
                        op=OP.max), [tk])
                    tk = st(lambda e: e.tensor_reduce(
                        out=q_, in_=gs8, axis=AX.X, op=OP.max), [tk])
                    tk = st(lambda e: e.tensor_scalar(
                        out=eq1, in0=gs8, scalar1=q_, scalar2=None,
                        op0=OP.is_ge), [tk])
                    tk = st(lambda e: e.scalar_tensor_tensor(
                        out=gsx, in0=eq1, scalar=-1e30, in1=gs8,
                        op0=OP.mult, op1=OP.add), [tk])
                    tk = st(lambda e: e.tensor_reduce(
                        out=q_, in_=gsx, axis=AX.X, op=OP.max), [tk])
                    tk = st(lambda e: e.tensor_scalar(
                        out=eq2, in0=gsx, scalar1=q_, scalar2=None,
                        op0=OP.is_ge), [tk])
                    tk = st(lambda e: e.tensor_tensor(
                        out=keep, in0=eq1, in1=eq2, op=OP.add), [tk])
                    tk = st(lambda e: e.tensor_scalar(
                        out=madd, in0=keep, scalar1=1e30, scalar2=-1e30,
                        op0=OP.mult, op1=OP.add), [tk])
                    tk = st(lambda e: e.tensor_tensor(
                        out=msk, in0=s8, in1=madd, op=OP.add), [tk])
                    tk = st(lambda e: e.tensor_reduce(
                        out=q_, in_=msk, axis=AX.X, op=OP.max), [tk])
                    tk = st(lambda e: e.tensor_scalar(
                        out=eq1, in0=msk, scalar1=q_, scalar2=None,
                        op0=OP.is_ge), [tk])
                    tk = st(lambda e: e.tensor_tensor_scan(
                        out=cum, data0=ONES8[:, 0:8], data1=eq1, initial=0.0,
                        op0=OP.mult, op1=OP.add), [tk, t_o8])
                    tk = st(lambda e: e.tensor_scalar(
                        out=tmp, in0=cum, scalar1=-1.0, scalar2=2.0,
                        op0=OP.mult, op1=OP.add), [tk])
                    tk = st(lambda e: e.tensor_scalar(
                        out=tmp, in0=tmp, scalar1=0.0, scalar2=None,
                        op0=OP.max), [tk])
                    tk = st(lambda e: e.tensor_tensor(
                        out=rm1, in0=tmp, in1=eq1, op=OP.mult), [tk])
                    tk = st(lambda e: e.scalar_tensor_tensor(
                        out=gsx, in0=rm1, scalar=-1e30, in1=msk,
                        op0=OP.mult, op1=OP.add), [tk])
                    tk = st(lambda e: e.tensor_reduce(
                        out=q_, in_=gsx, axis=AX.X, op=OP.max), [tk])
                    tk = st(lambda e: e.tensor_scalar(
                        out=eq2, in0=gsx, scalar1=q_, scalar2=None,
                        op0=OP.is_ge), [tk])
                    tk = st(lambda e: e.tensor_tensor_scan(
                        out=cum, data0=ONES8[:, 0:8], data1=eq2, initial=0.0,
                        op0=OP.mult, op1=OP.add), [tk, t_o8])
                    tk = st(lambda e: e.tensor_scalar(
                        out=tmp, in0=cum, scalar1=-1.0, scalar2=2.0,
                        op0=OP.mult, op1=OP.add), [tk])
                    tk = st(lambda e: e.tensor_scalar(
                        out=tmp, in0=tmp, scalar1=0.0, scalar2=None,
                        op0=OP.max), [tk])
                    tk = st(lambda e: e.tensor_tensor(
                        out=rm2, in0=tmp, in1=eq2, op=OP.mult), [tk])
                    tk = st(lambda e: e.tensor_tensor(
                        out=pw1, in0=rm1, in1=p_, op=OP.mult), [tk])
                    tk = st(lambda e: e.tensor_tensor(
                        out=pw2, in0=rm2, in1=p_, op=OP.mult), [tk])
                    tk = st(lambda e: e.tensor_tensor(
                        out=tsum, in0=rm1, in1=rm2, op=OP.add), [tk])
                    tk = st(lambda e: e.tensor_tensor(
                        out=cacc, in0=cacc, in1=tsum, op=OP.add), [tk])
                    def pack(e):
                        e.tensor_copy(PKT[:, 0:8], rm1)
                        e.tensor_copy(PKT[:, 8:16], rm2)
                        e.tensor_copy(PKT[:, 16:24], pw1)
                        return e.tensor_copy(PKT[:, 24:32], pw2)
                    tk = st(pack, [tk])
                    t_tp = Pr.run("tensor", lambda e: e.transpose(
                        PTB[0:32, 0:P], PKT[:], IDB[:]),
                        deps=[tk, t_idb] + ptb_busy[0].wr())
                    ptb_busy[0].wrote(t_tp)
                    fRT.read(t_tp)
                    t_fe = Pr.run("vector", lambda e, col=col: e.tensor_copy(
                        FPK[:, col:col + P], PTB[0:32, 0:P]),
                        deps=[t_tp] + (fFPK.wr() if tc == 0 and tch == 0
                                       else []))
                    ptb_busy[0].read(t_fe)
                    if tc == 0 and tch == 0:
                        fFPK.wrote(t_fe)
                    else:
                        fFPK.piece(t_fe)

            t_cm = Pr.run("tensor", lambda e: e.matmul(
                PF0[0:8, 0:1], rv(152), ONE[:], start=True, stop=True,
                skip_group_check=True),
                deps=[fRT.w, t_one] + pf_busy[0].wr())
            pf_busy[0].wrote(t_cm)
            fRT.read(t_cm)
            t_ce = Pr.run("vector", lambda e: e.tensor_copy(
                CNT, PF0[0:8, 0:1]), deps=[t_cm] + fRW.wr())
            pf_busy[0].read(t_ce)
            fRW.wrote(t_ce)
            abb = load_ab(0, 8, t_ab)
            XG8F = pv(F_STR, 64)
            t_x8 = None
            for f in range(NF):
                t_m8 = Pr.run("vector", lambda e, f=f: e.tensor_tensor(
                    out=HS[:, 0:8], in0=X[:, f * T:f * T + 8],
                    in1=AB[:, 0:8], op=OP.mult),
                    deps=abb + bX.rd() + fHS.wr() +
                    (fSTR.wr() if f == 0 else []))
                fHS.wrote(t_m8)
                bX.read(t_m8)
                bAB.read(t_m8)
                t_x8 = Pr.run("vector", lambda e, f=f: e.tensor_tensor(
                    out=XG8F[:, f * 8:(f + 1) * 8], in0=HS[:, 0:8],
                    in1=AB[:, 512:520], op=OP.add), deps=[t_m8])
                fHS.read(t_x8)
                bAB.read(t_x8)
                if f == 0:
                    fSTR.wrote(t_x8)
                else:
                    fSTR.piece(t_x8)
            if DEBUG_L3:
                dx8 = Pr.dma("gpsimd", dbg_xg8, XG8F, deps=[fSTR.w])
                fSTR.read(dx8)
                def abcp(e):
                    e.tensor_copy(HS[:, 32:40], AB[:, 0:8])
                    return e.tensor_copy(HS[:, 40:48], AB[:, 512:520])
                tac = Pr.run("vector", abcp, deps=[fSTR.w] + fHS.wr())
                fHS.wrote(tac)
                bAB.read(tac)
                dab = Pr.dma("gpsimd", dbg_ab8, HS[:, 32:48], deps=[tac])
                fHS.read(dab)
            for k in range(8):
                t_tr = Pr.run("tensor", lambda e, k=k: e.transpose(
                    PF1[0:8, 0:P], XG8F[:, k * 8:(k + 1) * 8], IDF[:]),
                    deps=[fSTR.w, t_idf] + pf_busy[1].wr())
                pf_busy[1].wrote(t_tr)
                fSTR.read(t_tr)
                t_pe = Pr.run("vector", lambda e, k=k: e.tensor_scalar(
                    out=pv(F_PKG, 1024, F32, 0, 8)[:, k * P:(k + 1) * P],
                    in0=PF1[0:8, 0:P], scalar1=C0M[0:8, 0:1], scalar2=None,
                    op0=OP.mult),
                    deps=[t_tr, t_c0m] + (fPKG.wr() if k == 0 else []))
                pf_busy[1].read(t_pe)
                if k == 0:
                    fPKG.wrote(t_pe)
                else:
                    fPKG.piece(t_pe)
            t_d1 = Pr.dma("sync", pk3_dr[0:8, :], pv(F_PKG, 1024, F32, 0, 8),
                          deps=[fPKG.w])
            fPKG.read(t_d1)
            t_d2 = Pr.dma("sync", pk3_dr[8:9, 0:8].rearrange("a b -> b a"),
                          CNT, deps=[fRW.w])
            fRW.read(t_d2)
            t_ar1 = Pr.run("gpsimd", lambda e: e.collective_compute(
                "AllReduce", OP.add, replica_groups=[list(range(8))],
                ins=[pk3_dr.opt()], outs=[pko3_dr.opt()]),
                deps=[t_d1, t_d2])
            t_d3 = Pr.dma("sync", pv(F_PKG, 1024, F32, 0, 8), pko3_dr[0:8, :],
                          deps=[t_ar1] + fPKG.wr())
            fPKG.wrote(t_d3)
            t_d4 = Pr.dma("sync", CROW, pko3_dr[8:9, 0:8],
                          deps=[t_ar1] + fRW.wr())
            fRW.wrote(t_d4)

            t_sc = Pr.run("vector", lambda e: e.tensor_tensor_scan(
                out=SROW, data0=ONES8[0:1, 0:8], data1=CROW, initial=0.0,
                op0=OP.mult, op1=OP.add), deps=[t_d4, t_o8])
            fRW.wrote(t_sc)
            def shf(e):
                e.memset(SHROW[:, 0:1], 0.0)
                return e.tensor_copy(SHROW[:, 1:8], SROW[:, 0:7])
            t_sh = Pr.run("vector", shf, deps=[t_sc])
            fRW.wrote(t_sh)
            t_r1 = Pr.dma("sync", row_dr[2:3, 0:8], SROW, deps=[t_sh])
            t_r2 = Pr.dma("sync", row_dr[3:4, 0:8], SHROW, deps=[t_sh])
            fRW.read(t_r1)
            fRW.read(t_r2)
            t_r3 = Pr.dma("sync", SREG, row_dr[2:3, 0:8].rearrange("a b -> b a"),
                          deps=[t_r1] + fRW.wr())
            t_r4 = Pr.dma("sync", SHREG,
                          row_dr[3:4, 0:8].rearrange("a b -> b a"),
                          deps=[t_r2, t_r3])
            fRW.wrote(t_r4)
            IO = pv(F_LG, 2048, F32, 0, 8)
            UB = pv(F_UV, 2048, BF16, 0, 8)
            VB = pv(F_UV + 1024, 2048, BF16, 0, 8)
            t_io = Pr.dma("sync", IO, l3_iota[0:1, :].to_broadcast([8, T]),
                          deps=fLG.wr())
            fLG.wrote(t_io)
            e_toks = []
            for cls in range(2):
                if cls == 1:
                    t_io2 = Pr.run("vector", lambda e: e.tensor_scalar(
                        out=IO, in0=IO, scalar1=1.0, scalar2=None, op0=OP.add),
                        deps=fLG.wr())
                    fLG.wrote(t_io2)
                EDST = E0 if cls == 0 else E1
                edb = fRT if cls == 0 else fE
                t_u = Pr.run("vector", lambda e: e.tensor_scalar(
                    out=UB, in0=IO, scalar1=SHREG, scalar2=None, op0=OP.is_ge),
                    deps=[fLG.w, t_r4] + fXN.wr())
                fXN.wrote(t_u)
                fLG.read(t_u)
                fRW.read(t_u)
                t_v = Pr.run("vector", lambda e: e.tensor_scalar(
                    out=VB, in0=IO, scalar1=SREG, scalar2=None, op0=OP.is_ge),
                    deps=[t_u])
                fXN.wrote(t_v)
                fLG.read(t_v)
                t_e = Pr.run("vector", lambda e, EDST=EDST: e.tensor_tensor(
                    out=EDST, in0=UB, in1=VB, op=OP.subtract),
                    deps=[t_v] + edb.wr())
                edb.wrote(t_e)
                fXN.read(t_e)
                e_toks.append(t_e)
            t_be0 = Pr.dma("sync", fe3_dr[0:8, :], E0, deps=[e_toks[0]])
            t_be1 = Pr.dma("sync", fe3_dr[8:16, :], E1, deps=[e_toks[1]])
            fRT.read(t_be0)
            fE.read(t_be1)
            t_bf = Pr.dma("sync", ff3_dr, FPK, deps=[fFPK.w])
            fFPK.read(t_bf)
            for k in range(8):
                t_tr = Pr.run("tensor", lambda e, k=k: e.transpose(
                    PF1[0:P, 0:P], PKG[:, k * P:(k + 1) * P], IDF[:]),
                    deps=[fPKG.w, t_idf] + pf_busy[1].wr())
                pf_busy[1].wrote(t_tr)
                fPKG.read(t_tr)
                t_xc = Pr.run("vector", lambda e, k=k: e.tensor_copy(
                    XGCB[:, k * 8:(k + 1) * 8], PF1[0:P, 0:8]),
                    deps=[t_tr] + (fU.wr() if k == 0 else []))
                pf_busy[1].read(t_xc)
                if k == 0:
                    fU.wrote(t_xc)
                else:
                    fU.piece(t_xc)
            for src_bf, dst_rows in ((l3w1_bf, 0), (l3w3_bf, 64)):
                for e_ in range(8):
                    t_w = Pr.dma("sync", pv(F_STR, 4096, BF16).rearrange(
                        "p (k o) -> p k o", o=512),
                        src_bf.rearrange("(k p) o -> p k o", p=P)
                        [:, e_ * 8:(e_ + 1) * 8, :],
                        deps=[cast_ready[id(src_bf)]] + fSTR.wr())
                    fSTR.wrote(t_w)
                    def mmu(e, e_=e_):
                        last = None
                        for k in range(8):
                            last = e.matmul(
                                PF0[0:8, 0:512], XGCB[:, k * 8:(k + 1) * 8],
                                pv(F_STR, 4096, BF16)[:, k * 512:(k + 1) * 512],
                                start=(k == 0), stop=(k == 7),
                                skip_group_check=True)
                        return last
                    t_mm = Pr.run("tensor", mmu,
                                  deps=[t_w, fU.w] + pf_busy[0].wr())
                    pf_busy[0].wrote(t_mm)
                    fSTR.read(t_mm)
                    fU.read(t_mm)
                    t_ue = Pr.run("vector", lambda e: e.tensor_copy(
                        UT8, PF0[0:8, 0:512]), deps=[t_mm] + fHS.wr())
                    pf_busy[0].read(t_ue)
                    fHS.wrote(t_ue)
                    t_ud = Pr.dma("sync",
                                  u13_dr[dst_rows + e_ * 8:
                                         dst_rows + (e_ + 1) * 8, :],
                                  UT8, deps=[t_ue])
                    fHS.read(t_ud)
            t_u1 = Pr.dma("sync", U1, u13_dr[0:64, :],
                          deps=fHS.wr() + fU.wr())
            t_u3 = Pr.dma("sync", U3, u13_dr[64:128, :], deps=[t_u1])
            fU.wrote(t_u3)
            scat_first = True
            scat_last = None
            for cls in range(2):
                fsrc = fe3_dr[cls * 8:(cls + 1) * 8, :]
                def reload(dst, src_rows, bcast, deps):
                    toks = []
                    for g in range(8):
                        if bcast:
                            sl = fsrc[g:g + 1, :].to_broadcast([8, T])
                        else:
                            sl = src_rows
                        toks.append(Pr.dma(
                            "sync", pv(dst, 2048, BF16, 0, 64)
                            [g * 8:(g + 1) * 8, :], sl,
                            deps=(deps if g == 0 else [toks[-1]])))
                    return toks
                tre = reload(F_RE, None, True, [t_be0, t_be1] + fRE.wr())
                fRE.wrote(tre[-1])
                for wsel, woff in ((2, 0), (0, 1024)):
                    tbf = []
                    for g in range(8):
                        tbf.append(Pr.dma(
                            "sync", BLKF[g * 8:(g + 1) * 8, :],
                            ff3_dr[wsel * 8 + cls * 8:
                                   wsel * 8 + cls * 8 + 8, :],
                            deps=([t_bf] + fBF.wr()) if g == 0 else [tbf[-1]]))
                    fBF.wrote(tbf[-1])
                    dsto = (F_OHW if wsel == 2 else F_OH0) + cls * 1024
                    t_oh = Pr.run("vector", lambda e, dsto=dsto: e.tensor_tensor(
                        out=pv(dsto, 2048, BF16, 0, 64), in0=RE, in1=BLKF,
                        op=OP.mult),
                        deps=[fRE.w, tbf[-1]] + (fOH.wr()
                                                 if cls == 0 and wsel == 2
                                                 else []))
                    if cls == 0 and wsel == 2:
                        fOH.wrote(t_oh)
                    else:
                        fOH.piece(t_oh)
                    fRE.read(t_oh)
                    fBF.read(t_oh)
            for cls in range(2):
                for j in range(16):
                    OHW = pv(F_OHW + cls * 1024, 2048, BF16, 0, 64)
                    OH0 = pv(F_OH0 + cls * 1024, 2048, BF16, 0, 64)
                    t_g1 = Pr.run("tensor", lambda e, OHW=OHW, j=j: e.matmul(
                        PF0[:, 0:512], OHW[:, j * P:(j + 1) * P], U1,
                        start=True, stop=True, skip_group_check=True),
                        deps=[fOH.w, fU.w] + pf_busy[0].wr())
                    pf_busy[0].wrote(t_g1)
                    fOH.read(t_g1)
                    fU.read(t_g1)
                    t_g3 = Pr.run("tensor", lambda e, OHW=OHW, j=j: e.matmul(
                        PF1[:, 0:512], OHW[:, j * P:(j + 1) * P], U3,
                        start=True, stop=True, skip_group_check=True),
                        deps=[fOH.w, fU.w] + pf_busy[1].wr())
                    pf_busy[1].wrote(t_g3)
                    t_si = Pr.run("scalar", lambda e: e.activation(
                        out=HS, in_=PF0[:, 0:512], func=AF.Silu),
                        deps=[t_g1] + fHS.wr())
                    fHS.wrote(t_si)
                    pf_busy[0].read(t_si)
                    t_hr = Pr.run("vector", lambda e: e.tensor_tensor(
                        out=HRB, in0=HS, in1=PF1[:, 0:512], op=OP.mult),
                        deps=[t_si, t_g3] + fHRB.wr())
                    fHRB.wrote(t_hr)
                    fHS.read(t_hr)
                    pf_busy[1].read(t_hr)
                    t_tt = Pr.run("tensor", lambda e, OH0=OH0, j=j: e.transpose(
                        PTB[0:P, 0:64], OH0[:, j * P:(j + 1) * P],
                        IDB[0:64, 0:64]),
                        deps=[fOH.w, t_idb] + ptb_busy[0].wr())
                    ptb_busy[0].wrote(t_tt)
                    fOH.read(t_tt)
                    t_ot = Pr.run("vector", lambda e: e.tensor_copy(
                        OTT, PTB[0:P, 0:64]), deps=[t_tt] + fHRB.wr())
                    ptb_busy[0].read(t_ot)
                    fHRB.wrote(t_ot)
                    t_sc2 = Pr.run("tensor", lambda e, cls=cls, j=j: e.matmul(
                        PM[3][0:64, 0:512], OTT, HRB,
                        start=(cls == 0 and j == 0),
                        stop=(cls == 1 and j == 15),
                        skip_group_check=True),
                        deps=[t_ot, t_hr] +
                        (pm_busy[3].wr() if cls == 0 and j == 0 else []))
                    pm_busy[3].wrote(t_sc2)
                    fHRB.read(t_sc2)
                    scat_last = t_sc2
            HB64 = pv(F_RE, 512, BF16, 0, 64)
            t_he = Pr.run("vector", lambda e: e.tensor_copy(
                HB64, PM[3][0:64, 0:512]), deps=[scat_last] + fRE.wr())
            pm_busy[3].read(t_he)
            fRE.wrote(t_he)
            HTB = pv(F_BF, 512, BF16)
            for k in range(4):
                t_tr = Pr.run("tensor", lambda e, k=k: e.transpose(
                    PTB[0:P, 0:64], HB64[:, k * P:(k + 1) * P],
                    IDB[0:64, 0:64]),
                    deps=[fRE.w, t_idb] + ptb_busy[0].wr())
                ptb_busy[0].wrote(t_tr)
                fRE.read(t_tr)
                t_hv = Pr.run("vector", lambda e, k=k: e.tensor_copy(
                    HTB[:, k * 64:(k + 1) * 64], PTB[0:P, 0:64]),
                    deps=[t_tr] + (fBF.wr() if k == 0 else []))
                ptb_busy[0].read(t_hv)
                if k == 0:
                    fBF.wrote(t_hv)
                else:
                    fBF.piece(t_hv)
            for half in range(2):
                mm_last = None
                for e_ in range(8):
                    t_w = Pr.dma("sync", pv(F_STR, 2048, BF16).rearrange(
                        "p (k o) -> p k o", o=512),
                        l3w2_bf.rearrange("(k p) o -> p k o", p=P)
                        [:, e_ * 4:(e_ + 1) * 4,
                         half * 512:(half + 1) * 512],
                        deps=[cast_ready[id(l3w2_bf)]] + fSTR.wr())
                    fSTR.wrote(t_w)
                    def mmg(e, e_=e_):
                        last = None
                        for k in range(4):
                            last = e.matmul(
                                PF0[0:8, 0:512],
                                HTB[:, k * 64 + e_ * 8:k * 64 + e_ * 8 + 8],
                                pv(F_STR, 2048, BF16)[:, k * 512:(k + 1) * 512],
                                start=(e_ == 0 and k == 0),
                                stop=(e_ == 7 and k == 3),
                                skip_group_check=True)
                        return last
                    mm_last = Pr.run("tensor", mmg,
                                     deps=[t_w, fBF.w] +
                                     (pf_busy[0].wr() if e_ == 0 else []))
                    pf_busy[0].wrote(mm_last)
                    fSTR.read(mm_last)
                    fBF.read(mm_last)
                t_ge = Pr.run("vector", lambda e, half=half: e.tensor_copy(
                    pv(F_GG, 1024, F32, 0, 8)[:, half * 512:(half + 1) * 512],
                    PF0[0:8, 0:512]),
                    deps=[mm_last] + (fGG.wr() if half == 0 else []))
                pf_busy[0].read(t_ge)
                if half == 0:
                    fGG.wrote(t_ge)
                else:
                    fGG.piece(t_ge)
            t_gd = Pr.dma("sync", g3_dr, pv(F_GG, 1024, F32, 0, 8),
                          deps=[fGG.w])
            fGG.read(t_gd)
            t_ar2 = Pr.run("gpsimd", lambda e: e.collective_compute(
                "AllReduce", OP.add, replica_groups=[list(range(8))],
                ins=[g3_dr.opt()], outs=[go3_dr.opt()]), deps=[t_gd])
            t_gb = Pr.dma("sync", pv(F_GG, 1024, F32, 0, 8), go3_dr,
                          deps=[t_ar2] + fGG.wr())
            fGG.wrote(t_gb)
            for k in range(8):
                t_tr = Pr.run("tensor", lambda e, k=k: e.transpose(
                    PF1[0:P, 0:P], GG[:, k * P:(k + 1) * P], IDF[:]),
                    deps=[fGG.w, t_idf] + pf_busy[1].wr())
                pf_busy[1].wrote(t_tr)
                fGG.read(t_tr)
                t_gf = Pr.run("vector", lambda e, k=k: e.tensor_scalar(
                    out=GFM[:, k * 8:(k + 1) * 8], in0=PF1[0:P, 0:8],
                    scalar1=C0M[:, 0:1], scalar2=None, op0=OP.mult),
                    deps=[t_tr, t_c0m] + (fGFM.wr() if k == 0 else []))
                pf_busy[1].read(t_gf)
                if k == 0:
                    fGFM.wrote(t_gf)
                else:
                    fGFM.piece(t_gf)

            ph2 = joinb(fOH, fRE, fBF, fU, fHRB, fHS, fSTR, fGG)
            for b in (fGS, fMS, fHB):
                for tok in ph2:
                    b._note(tok)
            for tc in range(4):
                t0 = tc * 512
                abb = load_ab(t0, 512, t_ab)
                for f in range(NF):
                    def fn(e, f=f, t0=t0):
                        e.tensor_tensor(out=MS3,
                                        in0=X[:, f * T + t0:f * T + t0 + 512],
                                        in1=AB[:, 0:512], op=OP.mult)
                        return e.tensor_tensor(
                            out=XN3[:, f * 512:(f + 1) * 512],
                            in0=MS3, in1=AB[:, 512:1024], op=OP.add)
                    tk = Pr.run("vector", fn,
                                deps=abb + bX.rd() + fMS.wr() +
                                (fXN.wr() if f == 0 else []))
                    fMS.wrote(tk)
                    bX.read(tk)
                    bAB.read(tk)
                    if f == 0:
                        fXN.wrote(tk)
                    else:
                        fXN.piece(tk)
                for oc in range(NF):
                    t_w1s = Pr.dma("sync", pv(F_STR, 1024, BF16).rearrange(
                        "p (k o) -> p k o", o=P),
                        l3sw1_bf.rearrange("(k p) o -> p k o", p=P)
                        [:, :, oc * P:(oc + 1) * P],
                        deps=[cast_ready[id(l3sw1_bf)]] + fSTR.wr())
                    t_w3s = Pr.dma("sync",
                                   pv(F_STR + 512, 1024, BF16).rearrange(
                                       "p (k o) -> p k o", o=P),
                                   l3sw3_bf.rearrange("(k p) o -> p k o", p=P)
                                   [:, :, oc * P:(oc + 1) * P],
                                   deps=[cast_ready[id(l3sw3_bf)], t_w1s])
                    fSTR.wrote(t_w3s)
                    def mm1(e, off=0, bank=0):
                        last = None
                        for k in range(8):
                            last = e.matmul(
                                PM[bank][:, 0:512],
                                pv(F_STR + off, 1024, BF16)
                                [:, k * P:(k + 1) * P],
                                XN3[:, k * 512:(k + 1) * 512],
                                start=(k == 0), stop=(k == 7),
                                skip_group_check=True)
                        return last
                    t_m1 = Pr.run("tensor", lambda e: mm1(e, 0, 0),
                                  deps=[t_w3s, fXN.w] + pm_busy[0].wr())
                    pm_busy[0].wrote(t_m1)
                    fSTR.read(t_m1)
                    fXN.read(t_m1)
                    t_m3 = Pr.run("tensor", lambda e: mm1(e, 512, 1),
                                  deps=[t_w3s, fXN.w] + pm_busy[1].wr())
                    pm_busy[1].wrote(t_m3)
                    fSTR.read(t_m3)
                    t_sl = Pr.run("scalar", lambda e: e.activation(
                        out=HS, in_=PM[0][:, 0:512], func=AF.Silu),
                        deps=[t_m1] + fHS.wr())
                    fHS.wrote(t_sl)
                    pm_busy[0].read(t_sl)
                    t_hb = Pr.run("vector", lambda e, oc=oc: e.tensor_tensor(
                        out=HB[:, oc * 512:(oc + 1) * 512], in0=HS,
                        in1=PM[1][:, 0:512], op=OP.mult),
                        deps=[t_sl, t_m3] + (fHB.wr() if oc == 0 else []))
                    fHS.read(t_hb)
                    pm_busy[1].read(t_hb)
                    if oc == 0:
                        fHB.wrote(t_hb)
                    else:
                        fHB.piece(t_hb)
                for oc in range(NF):
                    t_w = Pr.dma("sync", pv(F_STR, 1024, BF16).rearrange(
                        "p (k o) -> p k o", o=P),
                        l3sw2_bf.rearrange("(k p) o -> p k o", p=P)
                        [:, :, oc * P:(oc + 1) * P],
                        deps=[cast_ready[id(l3sw2_bf)]] + fSTR.wr())
                    fSTR.wrote(t_w)
                    bank = 2 + oc % 2
                    def mmd(e, bank=bank):
                        last = None
                        for k in range(8):
                            last = e.matmul(
                                PM[bank][:, 0:512],
                                pv(F_STR, 1024, BF16)[:, k * P:(k + 1) * P],
                                HB[:, k * 512:(k + 1) * 512],
                                start=(k == 0), stop=(k == 7),
                                skip_group_check=True)
                        return last
                    t_mm = Pr.run("tensor", mmd,
                                  deps=[t_w, fHB.w] + pm_busy[bank].wr())
                    pm_busy[bank].wrote(t_mm)
                    fSTR.read(t_mm)
                    fHB.read(t_mm)
                    if tc == 0:
                        t_cp = Pr.run("vector", lambda e, bank=bank:
                                      e.tensor_copy(GS, PM[bank][:, 0:512]),
                                      deps=[t_mm] + fGS.wr())
                        fGS.wrote(t_cp)
                        pm_busy[bank].read(t_cp)
                        t_ad = Pr.run("vector", lambda e, oc=oc:
                                      e.tensor_tensor(
                                          out=GS[:, 0:8], in0=GS[:, 0:8],
                                          in1=GFM[:, oc * 8:(oc + 1) * 8],
                                          op=OP.add),
                                      deps=[t_cp, fGFM.w])
                        fGS.wrote(t_ad)
                        fGFM.read(t_ad)
                        t_gs = Pr.run("scalar", lambda e: e.activation(
                            out=MS3, in_=GS, func=AF.Silu),
                            deps=[t_ad] + fMS.wr())
                        fGS.read(t_gs)
                    else:
                        t_gs = Pr.run("scalar", lambda e, bank=bank:
                                      e.activation(
                                          out=MS3, in_=PM[bank][:, 0:512],
                                          func=AF.Silu),
                                      deps=[t_mm] + fMS.wr())
                        pm_busy[bank].read(t_gs)
                    fMS.wrote(t_gs)
                    def upd(e, oc=oc, t0=t0):
                        e.tensor_tensor(out=MS3, in0=MS3,
                                        in1=S[:, oc * T + t0:oc * T + t0 + 512],
                                        op=OP.mult)
                        return e.tensor_tensor(
                            out=X[:, oc * T + t0:oc * T + t0 + 512],
                            in0=X[:, oc * T + t0:oc * T + t0 + 512],
                            in1=MS3, op=OP.add)
                    t_up = Pr.run("vector", upd, deps=[t_gs, bS.w] + bX.rd())
                    fMS.wrote(t_up)
                    bS.read(t_up)
                    bX.piece(t_up)
            if DEBUG_L3:
                d1 = Pr.dma("gpsimd", dbg_g3, pv(F_GG, 1024, F32, 0, 8),
                            deps=[fGG.w])
                d2 = Pr.dma("gpsimd", dbg_cnt, CROW, deps=[fRW.w, d1])
                d3 = Pr.dma("gpsimd", dbg_fpk, FPK, deps=[fFPK.w, d2])
                d4 = Pr.dma("gpsimd", dbg_e3[0:8, :], E0, deps=[fRT.w, d3])
                d5 = Pr.dma("gpsimd", dbg_e3[8:16, :], E1, deps=[fE.w, d4])
                d6 = Pr.dma("gpsimd", dbg_u13[0:64, :], U1, deps=[fU.w, d5])
                d7 = Pr.dma("gpsimd", dbg_u13[64:128, :], U3, deps=[d6])
                d8 = Pr.dma("gpsimd", dbg_pkg, pv(F_PKG, 1024, F32, 0, 8),
                            deps=[fPKG.w, d7])
                d9 = Pr.dma("gpsimd", dbg_xgc, XGCB, deps=[fU.w, d8])
                fPKG.read(d8)
                fU.read(d9)
                fGG.read(d1)
                fRW.read(d2)
                fFPK.read(d3)
                fRT.read(d4)
                fE.read(d5)
                fU.read(d7)
            return joinb(*fb.values())

        if n_layers >= 1:
            l0j = layer0()
        if n_layers >= 2:
            l1j = layer1(l0j)
        if n_layers >= 3:
            l2j = layer2(l1j)
        if n_layers >= 4:
            layer3(l2j)

        last = None
        for f in range(NF):
            last = Pr.dma("sync", y_out[f * P:(f + 1) * P, :],
                          X[:, f * T:(f + 1) * T], deps=bX.wr())
        if DEBUG_S:
            for f in range(NF):
                last = Pr.dma("sync", dbg_s[f * P:(f + 1) * P, :],
                              S[:, f * T:(f + 1) * T], deps=bS.rd())
        if DEBUG_DUMP:
            for r in range(NF):
                tq1 = Pr.dma("sync", pv(A_WIB, FI, BF16), wqg_dr[r * P:(r + 1) * P, :],
                             deps=[last])
                last = Pr.dma("sync", dbg_wqg[r * P:(r + 1) * P, :], pv(A_WIB, FI, BF16),
                              deps=[tq1])
        Pr.final_wait("sync", last)

        Pr.replay(nc, esems, dsems)
    return nc


_NC_CACHE = {}


def kernel(**inputs):
    n_layers = N_LAYERS
    if n_layers not in _NC_CACHE:
        _NC_CACHE[n_layers] = build_program(n_layers)
    nc = _NC_CACHE[n_layers]

    f32 = np.float32
    x = np.asarray(inputs["L0_attn_x"], f32)

    common = {
        "c_idf32": np.eye(P, dtype=f32),
        "c_idbf": np.eye(P, dtype=ml_dtypes.bfloat16),
        "c_ones": np.ones((P, 1), f32),
        "l0_wi_T": np.ascontiguousarray(np.asarray(inputs["L0_attn_W_in"], f32).T),
        "l0_wo_T": np.ascontiguousarray(np.asarray(inputs["L0_attn_W_out"], f32).T),
        "l0_wx_T": np.ascontiguousarray(np.asarray(inputs["L0_attn_W_x"], f32).T
                                        .reshape(NDI, P, 96).transpose(1, 0, 2)
                                        .reshape(P, NDI * 96)),
        "l0_wdt_T": np.ascontiguousarray(np.asarray(inputs["L0_attn_W_dt"], f32).T),
        "l0_convw": np.ascontiguousarray(
            np.asarray(inputs["L0_attn_conv_w"], f32).reshape(NDI, P, 4)
            .transpose(1, 0, 2).reshape(P, NDI * 4)),
        "l0_convb": np.ascontiguousarray(
            np.asarray(inputs["L0_attn_conv_b"], f32).reshape(NDI, P).T),
        "l0_bdt": np.ascontiguousarray(
            np.asarray(inputs["L0_attn_b_dt"], f32).reshape(NDI, P).T),
        "l0_alog": np.ascontiguousarray(
            np.asarray(inputs["L0_attn_A_log"], f32).reshape(NDI, P, DST)
            .transpose(1, 0, 2).reshape(P, NDI * DST)),
        "l0_dvec": np.ascontiguousarray(
            np.asarray(inputs["L0_attn_D"], f32).reshape(NDI, P).T),
        "l0_wg": np.ascontiguousarray(np.asarray(inputs["L0_ffn_w_gate"], f32)),
        "l0_wu": np.ascontiguousarray(np.asarray(inputs["L0_ffn_w_up"], f32)),
        "l0_wd": np.ascontiguousarray(np.asarray(inputs["L0_ffn_w_down"], f32)),
    }

    common.update({
        "l1_qkv_T": np.ascontiguousarray(np.asarray(inputs["L1_attn_qkv_w"], f32).T),
        "l1_oproj_T": np.ascontiguousarray(np.asarray(inputs["L1_attn_out_proj_w"], f32).T),
        "l1_ogate_T": np.ascontiguousarray(np.asarray(inputs["L1_attn_out_gate_w"], f32).T),
        "l1_ogate_b": np.ascontiguousarray(
            np.asarray(inputs["L1_attn_out_gate_b"], f32).reshape(NF, P).T),
        "l1_decay": np.ascontiguousarray(
            np.asarray(inputs["L1_attn_decay_param"], f32).reshape(1, 8)),
        "l1_convw": np.ascontiguousarray(
            np.asarray(inputs["L1_attn_conv_w"], f32).reshape(NF, P, 4)
            .transpose(1, 0, 2).reshape(P, NF * 4)),
        "l1_convb": np.ascontiguousarray(
            np.asarray(inputs["L1_attn_conv_b"], f32).reshape(NF, P).T),
        "l1_pw_T": np.ascontiguousarray(np.asarray(inputs["L1_attn_pw_w"], f32).T),
        "l1_gate_T": np.ascontiguousarray(np.asarray(inputs["L1_attn_gate_w"], f32).T),
        "l1_w1_T": np.ascontiguousarray(np.asarray(inputs["L1_attn_w1"], f32).T),
        "l1_w2_T": np.ascontiguousarray(np.asarray(inputs["L1_attn_w2"], f32).T),
        "l1_w3_T": np.ascontiguousarray(np.asarray(inputs["L1_attn_w3"], f32).T),
        "l1_n1": np.ascontiguousarray(
            np.asarray(inputs["L1_attn_norm1_w"], f32).reshape(NF, P).T),
        "l1_wr_T": np.ascontiguousarray(np.asarray(inputs["L1_ffn_W_r"], f32).T),
        "l1_wk_T": np.ascontiguousarray(np.asarray(inputs["L1_ffn_W_k"], f32).T),
        "l1_wv_T": np.ascontiguousarray(np.asarray(inputs["L1_ffn_W_v"], f32).T),
        "l1_mur": np.ascontiguousarray(
            np.asarray(inputs["L1_ffn_mu_r"], f32).reshape(NF, P).T),
        "l1_muk": np.ascontiguousarray(
            np.asarray(inputs["L1_ffn_mu_k"], f32).reshape(NF, P).T),
        "l1_n2": np.ascontiguousarray(
            np.asarray(inputs["L1_attn_norm2_w"], f32).reshape(NF, P).T),
    })

    qks = np.asarray(inputs["L2_attn_Wqk_short"], f32)
    qpad = np.zeros((DM, 1024), f32)
    for h in range(8):
        qpad[:, h * 64:h * 64 + 32] = qks[:, h * 32:(h + 1) * 32]
        qpad[:, 512 + h * 64:512 + h * 64 + 32] = \
            qks[:, 256 + h * 32:256 + (h + 1) * 32]
    ii = np.arange(P)[:, None]
    dd = np.arange(384)[None, :] - ii
    msk_s = np.where((dd >= 1) & (dd <= 256), 0.0, -1e30).astype(f32)
    dd = np.arange(1152)[None, :] - ii
    msk_l = np.where((dd >= 1) & (dd <= 1024), 0.0, -1e30).astype(f32)
    common.update({
        "l2_qks": np.ascontiguousarray(qpad),
        "l2_vs": np.ascontiguousarray(np.asarray(inputs["L2_attn_Wv_short"], f32)),
        "l2_qkl": np.ascontiguousarray(np.asarray(inputs["L2_attn_Wqk_long"], f32)),
        "l2_vl": np.ascontiguousarray(np.asarray(inputs["L2_attn_Wv_long"], f32)),
        "l2_proj": np.ascontiguousarray(np.asarray(inputs["L2_attn_Wproj"], f32)),
        "l2_rout_T": np.ascontiguousarray(
            np.asarray(inputs["L2_ffn_w_router"], f32).T),
        "l2_rout_b": np.ascontiguousarray(
            np.asarray(inputs["L2_ffn_b_router"], f32).reshape(8, 1)),
        "l2_wexp_T": np.ascontiguousarray(
            np.asarray(inputs["L2_ffn_w_experts"], f32)
            .transpose(0, 2, 1).reshape(8 * DM, DM)),
        "l2_bexp_T": np.ascontiguousarray(
            np.asarray(inputs["L2_ffn_b_experts"], f32)),
        "l2_mask_s": msk_s,
        "l2_mask_l": msk_l,
    })

    Wqkv = np.asarray(inputs["L3_attn_Wqkv"], f32)
    perm = np.empty(1024, np.int64)
    for h in range(16):
        perm[h * 64:h * 64 + 32] = h * 64 + 2 * np.arange(32)
        perm[h * 64 + 32:h * 64 + 64] = h * 64 + 2 * np.arange(32) + 1
    Wq = Wqkv[0:1024][perm]
    Wk = Wqkv[1024:2048][perm]
    Wv = Wqkv[2048:3072]
    Wqr = np.empty_like(Wq)
    Wkr = np.empty_like(Wk)
    for h in range(16):
        eix = slice(h * 64, h * 64 + 32)
        oix = slice(h * 64 + 32, h * 64 + 64)
        Wqr[eix] = -Wq[oix]
        Wqr[oix] = Wq[eix]
        Wkr[eix] = -Wk[oix]
        Wkr[oix] = Wk[eix]
    qkvT = np.concatenate([Wq, Wqr, Wk, Wkr, Wv], axis=0).T
    inv_freq = 1.0 / 10000.0 ** (np.arange(32, dtype=np.float64) / 32.0)
    freqs = np.arange(T)[:, None] * inv_freq[None, :]
    ccp = np.tile(np.cos(freqs).T, (4, 1)).astype(f32)
    ssp = np.tile(np.sin(freqs).T, (4, 1)).astype(f32)
    dd = np.arange(640)[None, :] - np.arange(P)[:, None]
    msk3 = np.where((dd >= 0) & (dd <= 512), 0.0, -1e30).astype(f32)
    common.update({
        "l3_qkv_T": np.ascontiguousarray(qkvT),
        "l3_proj_T": np.ascontiguousarray(
            np.asarray(inputs["L3_attn_Wproj"], f32).T),
        "l3_bproj": np.ascontiguousarray(
            np.asarray(inputs["L3_attn_bproj"], f32).reshape(NF, P).T),
        "l3_cc": ccp,
        "l3_ssn": ssp,
        "l3_mask": msk3,
        "l3_gate_T": np.ascontiguousarray(
            np.asarray(inputs["L3_ffn_w_gate"], f32).T),
        "l3_ebias": np.ascontiguousarray(
            np.asarray(inputs["L3_ffn_expert_bias"], f32).reshape(1, 8)),
        "l3_sw1_T": np.ascontiguousarray(np.asarray(inputs["L3_ffn_sw1"], f32).T),
        "l3_sw3_T": np.ascontiguousarray(np.asarray(inputs["L3_ffn_sw3"], f32).T),
        "l3_sw2_T": np.ascontiguousarray(np.asarray(inputs["L3_ffn_sw2"], f32).T),
        "l3_w1": np.ascontiguousarray(
            np.asarray(inputs["L3_ffn_w1"], f32).reshape(8 * DM, 512)),
        "l3_w3": np.ascontiguousarray(
            np.asarray(inputs["L3_ffn_w3"], f32).reshape(8 * DM, 512)),
        "l3_w2": np.ascontiguousarray(
            np.asarray(inputs["L3_ffn_w2"], f32).reshape(8 * 512, DM)),
    })

    in_maps = []
    for b in range(8):
        m = dict(common)
        m["x"] = np.ascontiguousarray(x[b].T)
        m["l3_iota"] = (2.0 * (np.arange(T, dtype=np.float64) + b * T)) \
            .astype(f32).reshape(1, T)
        m["l3_c0"] = np.full((P, 1), 1.0 if b == 0 else 0.0, f32)
        in_maps.append(m)

    res = run_bass_kernel_spmd(nc, in_maps, core_ids=list(range(8)))
    global _LAST_RES
    _LAST_RES = res
    out = np.stack([np.asarray(r["y"]).T for r in res.results], axis=0)
    return out.astype(np.float32)
